# version 02
# baseline (speedup 1.0000x reference)
import numpy as np
import ml_dtypes
import concourse.bass as bass
import concourse.mybir as mybir
from concourse.bass_utils import run_bass_kernel_spmd
from contextlib import ExitStack

F32 = mybir.dt.float32
BF16 = mybir.dt.bfloat16
U32 = mybir.dt.uint32
AF = mybir.ActivationFunctionType
ALU = mybir.AluOpType
AX = mybir.AxisListType
NPBF = ml_dtypes.bfloat16

B, S, D = 4, 4096, 2048
T = 2048
NT = T // 128
NH, HD = 16, 128
EPS = 1e-5
NCORE = 8
RSQRT8 = float(1.0 / np.sqrt(8.0))

def core_token_ranges(c):
    return [(0, 1024), (3072, 4096)] if c % 2 == 0 else [(1024, 3072)]

def shard_rows(x_bsd):
    out = []
    for c in range(NCORE):
        b = c // 2
        segs = [x_bsd[b, a:e] for (a, e) in core_token_ranges(c)]
        out.append(np.ascontiguousarray(np.concatenate(segs, axis=0)))
    return out

def unshard_rows(per_core, dtype=None):
    tail = per_core[0].shape[1:]
    out = np.empty((B, S) + tuple(tail), dtype=dtype or per_core[0].dtype)
    for c in range(NCORE):
        b = c // 2
        ofs = 0
        for (a, e) in core_token_ranges(c):
            out[b, a:e] = per_core[c][ofs:ofs + (e - a)]
            ofs += e - a
    return out


class Ctr:
    def __init__(self):
        self.v = 0
    def inc(self, n=1):
        self.v += n
        return self.v

def global_positions(c):
    return np.concatenate([np.arange(a, e) for (a, e) in core_token_ranges(c)])

IDN_BF = np.eye(128, dtype=np.float32).astype(NPBF)
ONESROW_BF = np.ones((1, 128), dtype=np.float32).astype(NPBF)

def build_l1c(TT=T):
    NT = TT // 128
    nc = bass.Bass("TRN2", target_bir_lowering=False)
    x = nc.dram_tensor("x", [TT, D], F32, kind="ExternalInput").ap()
    fac = nc.dram_tensor("fac", [8, D, D], F32, kind="ExternalInput").ap()
    bias = nc.dram_tensor("bias", [1, D], F32, kind="ExternalInput").ap()
    idn = nc.dram_tensor("idn", [128, 128], BF16, kind="ExternalInput").ap()
    onesr = nc.dram_tensor("onesr", [1, 128], BF16, kind="ExternalInput").ap()
    y = nc.dram_tensor("y", [TT, D], F32, kind="ExternalOutput").ap()

    OS = 512
    NOS = D // OS

    es = ExitStack()
    sb = lambda n, s, d: es.enter_context(nc.sbuf_tensor(n, s, d))
    ps = lambda n, s, d: es.enter_context(nc.psum_tensor(n, s, d))
    sem = lambda n: es.enter_context(nc.semaphore(n))
    with es:
        xb = sb("xb", [128, NT * D], BF16)
        idb = sb("idb", [128, 128], BF16)
        onesb = sb("onesb", [1, 128], BF16)
        biasb = sb("biasb", [1, D], BF16)
        facc = sb("facc", [128, 16 * OS], F32)
        fbw = sb("fbw", [128, 16 * D], BF16)
        xtr = [sb(f"xtr{i}", [128, 16 * 128], BF16) for i in range(2)]
        ssl = [sb(f"ssl{i}", [128, D], BF16) for i in range(2)]
        sqs = sb("sqs", [128, D], BF16)
        acc8 = sb("acc8", [128, 8], F32)
        stat = sb("stat", [128, 4], F32)
        epsc = sb("epsc", [128, 1], F32)
        yo = [sb(f"yo{i}", [128, D], F32) for i in range(2)]
        p_tr = [ps(f"p_tr{i}", [128, 128], BF16) for i in range(2)]
        p_mm = [ps(f"p_mm{i}", [128, OS], F32) for i in range(2)]

        s_cl = sem("s_cl")
        s_cl2 = sem("s_cl2")
        s_bb = sem("s_bb")
        s_x = [sem(f"s_x{i}") for i in range(2)]
        s_f = [sem(f"s_f{i}") for i in range(4)]
        s_fb = sem("s_fb")
        s_petr = sem("s_petr")
        s_trev = sem("s_trev")
        s_mm = sem("s_mm")
        s_ev = sem("s_ev")
        s_vq = sem("s_vq")
        s_lnx = sem("s_lnx")
        s_vs = sem("s_vs")
        s_rs = sem("s_rs")
        s_no = sem("s_no")
        s_st = [sem(f"s_st{i}") for i in range(2)]
        block = es.enter_context(nc.Block())

        @block.sync
        def _(sync):
            sync.dma_start(out=idb[:], in_=idn).then_inc(s_cl, 16)
            sync.dma_start(out=onesb[:], in_=onesr).then_inc(s_cl2, 16)

            for t in range(NT):
                sync.wait_ge(s_no, t + 1)
                sync.dma_start(out=y[t * 128:(t + 1) * 128, :],
                               in_=yo[t % 2][:]).then_inc(s_st[t % 2], 16)
            sync.wait_ge(s_st[0], 16 * ((NT + 1) // 2))
            sync.wait_ge(s_st[1], 16 * (NT // 2))

        @block.gpsimd
        def _(g):
            g.dma_start(out=biasb[:], in_=bias).then_inc(s_bb, 16)
            for t in range(NT):
                if t >= 2:
                    g.wait_ge(s_x[t % 2], 16 * (t // 2))
                g.dma_start(out=xb[:, t * D:(t + 1) * D],
                            in_=x[t * 128:(t + 1) * 128, :]).then_inc(s_x[t % 2], 16)
            for o in range(NOS):
                if o >= 1:
                    g.wait_ge(s_fb, o)
                src = fac[:, :, o * OS:(o + 1) * OS].rearrange("r (dc p) o -> r p dc o", p=128)
                for r in range(8):
                    if r > 0:
                        g.wait_ge(s_f[o], 16 * r)
                    g.dma_start(out=facc[:].rearrange("p (dc o) -> p dc o", o=OS), in_=src[r],
                                accum_op=(ALU.bypass if r == 0 else ALU.add)).then_inc(s_f[o], 16)

        @block.vector
        def _(v):
            v.wait_ge(s_cl2, 16)
            v.memset(epsc[:], EPS)
            for o in range(NOS):
                v.wait_ge(s_f[o], 16 * 8)
                v.tensor_scalar(fbw[:, o * 16 * OS:(o + 1) * 16 * OS], facc[:],
                                RSQRT8, None, ALU.mult).then_inc(s_fb, 1)
            for t in range(NT):
                if t >= 1:
                    v.wait_ge(s_rs, t)
                    if t - 1 >= 2:
                        v.wait_ge(s_st[(t - 1) % 2], 16 * ((t - 1) // 2))
                    v.tensor_scalar(yo[(t - 1) % 2][:], ssl[(t - 1) % 2][:],
                                    stat[:, 0:1], stat[:, 3:4],
                                    ALU.subtract, ALU.mult).then_inc(s_no, 1)
                if t >= 2:
                    v.wait_ge(s_no, t - 1)
                if t >= 1:
                    v.wait_ge(s_vs, 5 * t)
                for o in range(NOS):
                    gidx = t * NOS + o
                    v.wait_ge(s_mm, gidx + 1)
                    v.scalar_tensor_tensor(
                        ssl[t % 2][:, o * OS:(o + 1) * OS], p_mm[gidx % 2][:], 1.0,
                        xb[:, t * D + o * OS: t * D + (o + 1) * OS],
                        ALU.mult, ALU.add, accum_out=acc8[:, o:o + 1]).then_inc(s_ev, 1)
                v.wait_ge(s_ev, 4 * (t + 1))
                if t >= 1:
                    v.wait_ge(s_vq, 4 * t)
                for o in range(NOS):
                    v.scalar_tensor_tensor(
                        sqs[:, o * OS:(o + 1) * OS], ssl[t % 2][:, o * OS:(o + 1) * OS], 1.0,
                        ssl[t % 2][:, o * OS:(o + 1) * OS],
                        ALU.mult, ALU.mult, accum_out=acc8[:, 4 + o:5 + o]).then_inc(s_vq, 1)
                v.tensor_reduce(stat[:, 0:1], acc8[:, 0:4], AX.X, ALU.add).then_inc(s_vs, 1)
                v.wait_ge(s_vq, 4 * (t + 1))
                v.tensor_reduce(stat[:, 2:3], acc8[:, 4:8], AX.X, ALU.add).then_inc(s_vs, 1)
                v.wait_ge(s_vs, 5 * t + 1)
                v.tensor_scalar(stat[:, 0:1], stat[:, 0:1], 1.0 / D, None, ALU.mult).then_inc(s_vs, 1)
                v.wait_ge(s_vs, 5 * t + 3)
                v.tensor_tensor(stat[:, 1:2], stat[:, 0:1], stat[:, 0:1], ALU.mult).then_inc(s_vs, 1)
                v.wait_ge(s_vs, 5 * t + 4)
                v.scalar_tensor_tensor(stat[:, 2:3], stat[:, 2:3], 1.0 / D, stat[:, 1:2],
                                       ALU.mult, ALU.subtract).then_inc(s_vs, 1)
            t = NT - 1
            v.wait_ge(s_rs, NT)
            v.wait_ge(s_st[t % 2], 16 * (t // 2))
            v.tensor_scalar(yo[t % 2][:], ssl[t % 2][:], stat[:, 0:1], stat[:, 3:4],
                            ALU.subtract, ALU.mult).then_inc(s_no, 1)

        @block.scalar
        def _(sc):
            k = 0
            for t in range(NT):
                if t >= 2:
                    sc.wait_ge(s_mm, (t - 1) * NOS)
                for dc in range(16):
                    sc.wait_ge(s_petr, k + 1)
                    sc.copy(xtr[t % 2][:, dc * 128:(dc + 1) * 128], p_tr[k % 2][:]).then_inc(s_trev, 1)
                    k += 1
                if t >= 1:
                    sc.wait_ge(s_vs, 5 * t)
                    sc.activation(stat[:, 2:3], stat[:, 2:3], AF.Ln, bias=epsc[:]).then_inc(s_lnx, 1)
                    sc.wait_ge(s_lnx, t)
                    sc.activation(stat[:, 3:4], stat[:, 2:3], AF.Exp, scale=-0.5).then_inc(s_rs, 1)
            sc.wait_ge(s_vs, 5 * NT)
            sc.activation(stat[:, 2:3], stat[:, 2:3], AF.Ln, bias=epsc[:]).then_inc(s_lnx, 1)
            sc.wait_ge(s_lnx, NT)
            sc.activation(stat[:, 3:4], stat[:, 2:3], AF.Exp, scale=-0.5).then_inc(s_rs, 1)

        @block.tensor
        def _(te):
            te.wait_ge(s_cl, 16)
            k = 0
            for t in range(NT):
                te.wait_ge(s_x[t % 2], 16 * (t // 2 + 1))
                for dc in range(16):
                    if k >= 2:
                        te.wait_ge(s_trev, k - 1)
                    te.transpose(p_tr[k % 2][:], xb[:, t * D + dc * 128: t * D + (dc + 1) * 128],
                                 idb[:]).then_inc(s_petr, 1)
                    k += 1
                te.wait_ge(s_trev, (t + 1) * 16)
                for o in range(NOS):
                    gidx = t * NOS + o
                    te.wait_ge(s_fb, o + 1)
                    if gidx == 0:
                        te.wait_ge(s_bb, 16)
                    if gidx >= 2:
                        te.wait_ge(s_ev, gidx - 1)
                    for dc in range(16):
                        te.matmul(p_mm[gidx % 2][:],
                                  xtr[t % 2][:, dc * 128:(dc + 1) * 128],
                                  fbw[:, o * 16 * OS + dc * OS: o * 16 * OS + (dc + 1) * OS],
                                  start=(dc == 0), stop=False)
                    te.matmul(p_mm[gidx % 2][:], onesb[:], biasb[:, o * OS:(o + 1) * OS],
                              start=False, stop=True).then_inc(s_mm, 1)
    return nc

def _run(nc, in_maps, **kw):
    return run_bass_kernel_spmd(nc, in_maps, core_ids=list(range(NCORE)), **kw)

def run_l1c(x3_full, fac, bias_, trace=False):
    nc = build_l1c()
    shards = shard_rows(x3_full)
    fac = np.ascontiguousarray(np.asarray(fac, np.float32))
    bias_ = np.ascontiguousarray(np.asarray(bias_, np.float32))
    in_maps = [{"x": s, "fac": fac, "bias": bias_.reshape(1, D), "idn": IDN_BF, "onesr": ONESROW_BF}
               for s in shards]
    res = _run(nc, in_maps, trace=trace)
    return unshard_rows([r["y"] for r in res.results]), res


def build_l0a(TT=T):
    NTt = TT // 128
    nc = bass.Bass("TRN2", target_bir_lowering=False)
    x = nc.dram_tensor("x", [TT, D], F32, kind="ExternalInput").ap()
    wq = nc.dram_tensor("wq", [16, 16, 128, 128], F32, kind="ExternalInput").ap()
    wkv = nc.dram_tensor("wkv", [16, 16, 128, 128], F32, kind="ExternalInput").ap()
    idn = nc.dram_tensor("idn", [128, 128], BF16, kind="ExternalInput").ap()
    qch = nc.dram_tensor("qch", [D, TT], BF16, kind="ExternalOutput").ap()
    kvch = nc.dram_tensor("kvch", [D, TT], BF16, kind="ExternalOutput").ap()

    es = ExitStack()
    sb = lambda n, s, d: es.enter_context(nc.sbuf_tensor(n, s, d))
    ps = lambda n, s, d: es.enter_context(nc.psum_tensor(n, s, d))
    sem = lambda n: es.enter_context(nc.semaphore(n))
    with es:
        xb = [sb(f"xb{i}", [128, D], BF16) for i in range(2)]
        xtf = sb("xtf", [128, 16 * TT], BF16)
        wblk = [sb(f"wblk{i}", [128, 16 * 128], BF16) for i in range(4)]
        ob = [sb(f"ob{i}", [128, TT], BF16) for i in range(2)]
        idb = sb("idb", [128, 128], BF16)
        p_tr = [ps(f"p_tr{i}", [128, 128], BF16) for i in range(2)]
        p_mm = [ps(f"p_mm{i}", [128, 512], F32) for i in range(2)]
        s_cl = sem("s_cl")
        s_x = [sem(f"s_x{i}") for i in range(2)]
        s_w = [sem(f"s_w{i}") for i in range(4)]
        s_petr = sem("s_petr")
        s_trev = sem("s_trev")
        s_mm = sem("s_mm")
        s_ev = sem("s_ev")
        s_st = [sem(f"s_st{i}") for i in range(2)]
        block = es.enter_context(nc.Block())

        NB = 32
        TG = TT // 512

        @block.sync
        def _(sync):
            sync.dma_start(out=idb[:], in_=idn).then_inc(s_cl, 16)
            for bkid in range(NB):
                sync.wait_ge(s_ev, bkid * TG + TG)
                dst = (qch if bkid < 16 else kvch)[(bkid % 16) * 128:(bkid % 16) * 128 + 128, :]
                sync.dma_start(out=dst, in_=ob[bkid % 2][:]).then_inc(s_st[bkid % 2], 16)
            sync.wait_ge(s_st[0], 16 * (NB // 2))
            sync.wait_ge(s_st[1], 16 * (NB // 2))

        @block.gpsimd
        def _(g):
            for t in range(NTt):
                if t >= 2:
                    g.wait_ge(s_x[t % 2], 16 * (t // 2))
                    g.wait_ge(s_petr, (t - 1) * 16)
                g.dma_start(out=xb[t % 2][:], in_=x[t * 128:(t + 1) * 128, :]).then_inc(s_x[t % 2], 16)
            for bkid in range(NB):
                w_ap = (wq if bkid < 16 else wkv)[bkid % 16]
                if bkid >= 4:
                    g.wait_ge(s_mm, (bkid - 3) * TG)
                g.dma_start(out=wblk[bkid % 4][:].rearrange("p (dc c) -> p dc c", c=128),
                            in_=w_ap.rearrange("dc p c -> p dc c")).then_inc(s_w[bkid % 4], 16)

        @block.scalar
        def _(sc):
            k = 0
            for t in range(NTt):
                for dc in range(16):
                    sc.wait_ge(s_petr, k + 1)
                    sc.copy(xtf[:, dc * TT + t * 128: dc * TT + (t + 1) * 128], p_tr[k % 2][:]).then_inc(s_trev, 1)
                    k += 1

        @block.vector
        def _(v):
            for bkid in range(NB):
                for tg in range(TG):
                    gidx = bkid * TG + tg
                    v.wait_ge(s_mm, gidx + 1)
                    if bkid >= 2 and tg == 0:
                        v.wait_ge(s_st[bkid % 2], 16 * (bkid // 2))
                    v.tensor_copy(ob[bkid % 2][:, tg * 512:(tg + 1) * 512], p_mm[gidx % 2][:]).then_inc(s_ev, 1)

        @block.tensor
        def _(te):
            te.wait_ge(s_cl, 16)
            k = 0
            for t in range(NTt):
                te.wait_ge(s_x[t % 2], 16 * (t // 2 + 1))
                for dc in range(16):
                    if k >= 2:
                        te.wait_ge(s_trev, k - 1)
                    te.transpose(p_tr[k % 2][:], xb[t % 2][:, dc * 128:(dc + 1) * 128], idb[:]).then_inc(s_petr, 1)
                    k += 1
            te.wait_ge(s_trev, NTt * 16)
            for bkid in range(NB):
                te.wait_ge(s_w[bkid % 4], 16 * (bkid // 4 + 1))
                for tg in range(TG):
                    gidx = bkid * TG + tg
                    if gidx >= 2:
                        te.wait_ge(s_ev, gidx - 1)
                    for dc in range(16):
                        mi = te.matmul(p_mm[gidx % 2][:],
                                       wblk[bkid % 4][:, dc * 128:(dc + 1) * 128],
                                       xtf[:, dc * TT + tg * 512: dc * TT + tg * 512 + 512],
                                       start=(dc == 0), stop=(dc == 15))
                    mi.then_inc(s_mm, 1)
    return nc

def run_l0a(x_full, W_q, W_kv, trace=False):
    nc = build_l0a()
    shards = shard_rows(x_full)
    wq_t = np.ascontiguousarray(
        np.asarray(W_q, np.float32).T.reshape(16, 128, 16, 128).transpose(2, 0, 1, 3))
    wkv_t = np.ascontiguousarray(
        np.asarray(W_kv, np.float32).T.reshape(16, 128, 16, 128).transpose(2, 0, 1, 3))
    in_maps = [{"x": s, "wq": wq_t, "wkv": wkv_t, "idn": IDN_BF} for s in shards]
    res = _run(nc, in_maps, trace=trace)
    return ([r["qch"] for r in res.results], [r["kvch"] for r in res.results], res)


def build_l0c(TT=T):
    NTt = TT // 128
    nc = bass.Bass("TRN2", target_bir_lowering=False)
    x = nc.dram_tensor("x", [TT, D], F32, kind="ExternalInput").ap()
    vht = nc.dram_tensor("vht", [16, NTt, 128, 128], BF16, kind="ExternalInput").ap()
    wo = nc.dram_tensor("wo", [16, 128, D], F32, kind="ExternalInput").ap()
    y = nc.dram_tensor("y", [TT, D], F32, kind="ExternalOutput").ap()

    OS = 512
    NOS = D // OS
    es = ExitStack()
    sb = lambda n, s, d: es.enter_context(nc.sbuf_tensor(n, s, d))
    ps = lambda n, s, d: es.enter_context(nc.psum_tensor(n, s, d))
    sem = lambda n: es.enter_context(nc.semaphore(n))
    with es:
        xb = sb("xb", [128, NTt * D], BF16)
        wob = sb("wob", [128, 16 * D], BF16)
        vhb = [sb(f"vhb{i}", [128, 16 * 128], BF16) for i in range(2)]
        ssl = [sb(f"ssl{i}", [128, D], BF16) for i in range(2)]
        sqs = sb("sqs", [128, D], BF16)
        acc8 = sb("acc8", [128, 8], F32)
        stat = sb("stat", [128, 4], F32)
        epsc = sb("epsc", [128, 1], F32)
        yo = [sb(f"yo{i}", [128, D], F32) for i in range(2)]
        p_mm = [ps(f"p_mm{i}", [128, OS], F32) for i in range(2)]
        s_x = [sem(f"s_x{i}") for i in range(2)]
        s_w = sem("s_w")
        s_vh = [sem(f"s_vh{i}") for i in range(2)]
        s_mm = sem("s_mm")
        s_ev = sem("s_ev")
        s_vq = sem("s_vq")
        s_vs = sem("s_vs")
        s_lnx = sem("s_lnx")
        s_rs = sem("s_rs")
        s_no = sem("s_no")
        s_st = [sem(f"s_st{i}") for i in range(2)]
        block = es.enter_context(nc.Block())

        @block.sync
        def _(sync):
            for t in range(NTt):
                sync.wait_ge(s_no, t + 1)
                sync.dma_start(out=y[t * 128:(t + 1) * 128, :],
                               in_=yo[t % 2][:]).then_inc(s_st[t % 2], 16)
            sync.wait_ge(s_st[0], 16 * ((NTt + 1) // 2))
            sync.wait_ge(s_st[1], 16 * (NTt // 2))

        @block.gpsimd
        def _(g):
            for dc in range(16):
                g.wait_ge(s_w, 16 * dc)
                g.dma_start(out=wob[:, dc * D:(dc + 1) * D], in_=wo[dc]).then_inc(s_w, 16)
            for t in range(NTt):
                if t >= 2:
                    g.wait_ge(s_x[t % 2], 16 * (t // 2))
                g.dma_start(out=xb[:, t * D:(t + 1) * D],
                            in_=x[t * 128:(t + 1) * 128, :]).then_inc(s_x[t % 2], 16)
            for t in range(NTt):
                if t >= 2:
                    g.wait_ge(s_mm, (t - 1) * NOS)
                    g.wait_ge(s_vh[t % 2], 16 * (t // 2))
                g.dma_start(out=vhb[t % 2][:].rearrange("p (dc c) -> p dc c", c=128),
                            in_=vht[:, t].rearrange("dc p c -> p dc c")).then_inc(s_vh[t % 2], 16)

        @block.vector
        def _(v):
            v.memset(epsc[:], EPS)
            for t in range(NTt):
                if t >= 1:
                    v.wait_ge(s_rs, t)
                    if t - 1 >= 2:
                        v.wait_ge(s_st[(t - 1) % 2], 16 * ((t - 1) // 2))
                    v.tensor_scalar(yo[(t - 1) % 2][:], ssl[(t - 1) % 2][:],
                                    stat[:, 0:1], stat[:, 3:4],
                                    ALU.subtract, ALU.mult).then_inc(s_no, 1)
                if t >= 2:
                    v.wait_ge(s_no, t - 1)
                if t >= 1:
                    v.wait_ge(s_vs, 5 * t)
                for o in range(NOS):
                    gidx = t * NOS + o
                    v.wait_ge(s_mm, gidx + 1)
                    v.wait_ge(s_x[t % 2], 16 * (t // 2 + 1))
                    v.scalar_tensor_tensor(
                        ssl[t % 2][:, o * OS:(o + 1) * OS], p_mm[gidx % 2][:], 1.0,
                        xb[:, t * D + o * OS: t * D + (o + 1) * OS],
                        ALU.mult, ALU.add, accum_out=acc8[:, o:o + 1]).then_inc(s_ev, 1)
                v.wait_ge(s_ev, 4 * (t + 1))
                if t >= 1:
                    v.wait_ge(s_vq, 4 * t)
                for o in range(NOS):
                    v.scalar_tensor_tensor(
                        sqs[:, o * OS:(o + 1) * OS], ssl[t % 2][:, o * OS:(o + 1) * OS], 1.0,
                        ssl[t % 2][:, o * OS:(o + 1) * OS],
                        ALU.mult, ALU.mult, accum_out=acc8[:, 4 + o:5 + o]).then_inc(s_vq, 1)
                v.tensor_reduce(stat[:, 0:1], acc8[:, 0:4], AX.X, ALU.add).then_inc(s_vs, 1)
                v.wait_ge(s_vq, 4 * (t + 1))
                v.tensor_reduce(stat[:, 2:3], acc8[:, 4:8], AX.X, ALU.add).then_inc(s_vs, 1)
                v.wait_ge(s_vs, 5 * t + 1)
                v.tensor_scalar(stat[:, 0:1], stat[:, 0:1], 1.0 / D, None, ALU.mult).then_inc(s_vs, 1)
                v.wait_ge(s_vs, 5 * t + 3)
                v.tensor_tensor(stat[:, 1:2], stat[:, 0:1], stat[:, 0:1], ALU.mult).then_inc(s_vs, 1)
                v.wait_ge(s_vs, 5 * t + 4)
                v.scalar_tensor_tensor(stat[:, 2:3], stat[:, 2:3], 1.0 / D, stat[:, 1:2],
                                       ALU.mult, ALU.subtract).then_inc(s_vs, 1)
            t = NTt - 1
            v.wait_ge(s_rs, NTt)
            v.wait_ge(s_st[t % 2], 16 * (t // 2))
            v.tensor_scalar(yo[t % 2][:], ssl[t % 2][:], stat[:, 0:1], stat[:, 3:4],
                            ALU.subtract, ALU.mult).then_inc(s_no, 1)

        @block.scalar
        def _(sc):
            for t in range(NTt):
                sc.wait_ge(s_vs, 5 * (t + 1))
                sc.activation(stat[:, 2:3], stat[:, 2:3], AF.Ln, bias=epsc[:]).then_inc(s_lnx, 1)
                sc.wait_ge(s_lnx, t + 1)
                sc.activation(stat[:, 3:4], stat[:, 2:3], AF.Exp, scale=-0.5).then_inc(s_rs, 1)

        @block.tensor
        def _(te):
            te.wait_ge(s_w, 16 * 16)
            for t in range(NTt):
                te.wait_ge(s_vh[t % 2], 16 * (t // 2 + 1))
                for o in range(NOS):
                    gidx = t * NOS + o
                    if gidx >= 2:
                        te.wait_ge(s_ev, gidx - 1)
                    for dc in range(16):
                        mi = te.matmul(p_mm[gidx % 2][:],
                                       vhb[t % 2][:, dc * 128:(dc + 1) * 128],
                                       wob[:, dc * D + o * OS: dc * D + o * OS + OS],
                                       start=(dc == 0), stop=(dc == 15))
                    mi.then_inc(s_mm, 1)
    return nc


def build_l1a(TT=T):
    NTt = TT // 128
    nc = bass.Bass("TRN2", target_bir_lowering=False)
    x = nc.dram_tensor("x", [TT, D], F32, kind="ExternalInput").ap()
    wqkv = nc.dram_tensor("wqkv", [12, 16, 128, 512], F32, kind="ExternalInput").ap()
    cosr = nc.dram_tensor("cosr", [TT, 512], F32, kind="ExternalInput").ap()
    sinr = nc.dram_tensor("sinr", [TT, 512], F32, kind="ExternalInput").ap()
    idn = nc.dram_tensor("idn", [128, 128], BF16, kind="ExternalInput").ap()
    qkvt = nc.dram_tensor("qkvt", [12, NTt, 128, 512], BF16, kind="ExternalOutput").ap()

    es = ExitStack()
    sb = lambda n, s, d: es.enter_context(nc.sbuf_tensor(n, s, d))
    ps = lambda n, s, d: es.enter_context(nc.psum_tensor(n, s, d))
    sem = lambda n: es.enter_context(nc.semaphore(n))
    with es:
        xb = [sb(f"xb{i}", [128, D], BF16) for i in range(2)]
        xtf = sb("xtf", [128, 16 * TT], BF16)
        idb = sb("idb", [128, 128], BF16)
        wfs = [sb(f"wfs{i}", [128, 16 * 512], BF16) for i in range(2)]
        cosb = sb("cosb", [128, NTt * 512], F32)
        sinb = sb("sinb", [128, NTt * 512], F32)
        slab = [sb(f"slab{i}", [128, 512], BF16) for i in range(2)]
        rb1 = sb("rb1", [128, 128], F32)
        rb2 = sb("rb2", [128, 128], F32)
        p_tr = [ps(f"p_tr{i}", [128, 128], BF16) for i in range(2)]
        p_mm = [ps(f"p_mm{i}", [128, 512], F32) for i in range(2)]
        s_cl = sem("s_cl")
        s_x = [sem(f"s_x{i}") for i in range(2)]
        s_w = [sem(f"s_w{i}") for i in range(2)]
        s_petr = sem("s_petr")
        s_trev = sem("s_trev")
        s_mm = sem("s_mm")
        s_ev = sem("s_ev")
        s_vr = sem("s_vr")
        s_ro = sem("s_ro")
        s_st = [sem(f"s_st{i}") for i in range(2)]
        block = es.enter_context(nc.Block())

        @block.sync
        def _(sync):
            sync.dma_start(out=idb[:], in_=idn).then_inc(s_cl, 16)
            sync.wait_ge(s_cl, 16)
            sync.dma_start(out=cosb[:].rearrange("p (tc c) -> p tc c", c=512),
                           in_=cosr.rearrange("(tc p) c -> p tc c", p=128)).then_inc(s_cl, 16)
            sync.wait_ge(s_cl, 32)
            sync.dma_start(out=sinb[:].rearrange("p (tc c) -> p tc c", c=512),
                           in_=sinr.rearrange("(tc p) c -> p tc c", p=128)).then_inc(s_cl, 16)
            ro_c = 0
            for g in range(12 * NTt):
                ro_c += 2 if g < 8 * NTt else 1
                sync.wait_ge(s_ro, ro_c)
                sync.dma_start(out=qkvt[g // NTt, g % NTt],
                               in_=slab[g % 2][:]).then_inc(s_st[g % 2], 16)
            sync.wait_ge(s_st[0], 16 * (12 * NTt // 2))
            sync.wait_ge(s_st[1], 16 * (12 * NTt // 2))

        @block.gpsimd
        def _(g):
            for t in range(NTt):
                if t >= 2:
                    g.wait_ge(s_x[t % 2], 16 * (t // 2))
                    g.wait_ge(s_petr, (t - 1) * 16)
                g.dma_start(out=xb[t % 2][:], in_=x[t * 128:(t + 1) * 128, :]).then_inc(s_x[t % 2], 16)
            for fs in range(12):
                if fs >= 2:
                    g.wait_ge(s_mm, (fs - 1) * NTt)
                g.dma_start(out=wfs[fs % 2][:].rearrange("p (dc c) -> p dc c", c=512),
                            in_=wqkv[fs].rearrange("dc p c -> p dc c")).then_inc(s_w[fs % 2], 16)

        @block.scalar
        def _(sc):
            k = 0
            for t in range(NTt):
                for dc in range(16):
                    sc.wait_ge(s_petr, k + 1)
                    sc.copy(xtf[:, dc * TT + t * 128: dc * TT + (t + 1) * 128], p_tr[k % 2][:]).then_inc(s_trev, 1)
                    k += 1

        @block.vector
        def _(v):
            v.wait_ge(s_cl, 48)
            vr = 0
            ro = [0]
            for fs in range(12):
                for t in range(NTt):
                    g = fs * NTt + t
                    v.wait_ge(s_mm, g + 1)
                    if g >= 2:
                        v.wait_ge(s_st[g % 2], 16 * (g // 2))
                    v.tensor_copy(slab[g % 2][:], p_mm[g % 2][:]).then_inc(s_ev, 1)
                    if fs < 8:
                        sl3 = slab[g % 2][:].rearrange("p (h c) -> p h c", c=128)
                        q1 = sl3[:, :, 0:16]
                        q2 = sl3[:, :, 16:32]
                        tb3c = cosb[:, t * 512 + (fs % 4) * 128: t * 512 + (fs % 4) * 128 + 128].rearrange("p (h c) -> p h c", c=32)
                        tb3s = sinb[:, t * 512 + (fs % 4) * 128: t * 512 + (fs % 4) * 128 + 128].rearrange("p (h c) -> p h c", c=32)
                        rA = rb1[:, 0:64].rearrange("p (h c) -> p h c", c=16)
                        rB = rb1[:, 64:128].rearrange("p (h c) -> p h c", c=16)
                        rC = rb2[:, 0:64].rearrange("p (h c) -> p h c", c=16)
                        rD = rb2[:, 64:128].rearrange("p (h c) -> p h c", c=16)
                        v.wait_ge(s_ev, g + 1)
                        if ro[0] > 0:
                            v.wait_ge(s_ro, ro[0])
                        v.tensor_tensor(rA, q1, tb3c[:, :, 0:16], ALU.mult).then_inc(s_vr, 1)
                        v.tensor_tensor(rB, q2, tb3s[:, :, 0:16], ALU.mult).then_inc(s_vr, 1)
                        v.tensor_tensor(rC, q2, tb3c[:, :, 16:32], ALU.mult).then_inc(s_vr, 1)
                        v.tensor_tensor(rD, q1, tb3s[:, :, 16:32], ALU.mult).then_inc(s_vr, 1)
                        vr += 4
                        v.wait_ge(s_vr, vr)
                        v.tensor_tensor(q1, rA, rB, ALU.subtract).then_inc(s_ro, 1)
                        v.tensor_tensor(q2, rC, rD, ALU.add).then_inc(s_ro, 1)
                        ro[0] += 2
                    else:
                        v.wait_ge(s_ev, g + 1)
                        v.nop().then_inc(s_ro, 1)
                        ro[0] += 1

        @block.tensor
        def _(te):
            te.wait_ge(s_cl, 16)
            k = 0
            for t in range(NTt):
                te.wait_ge(s_x[t % 2], 16 * (t // 2 + 1))
                for dc in range(16):
                    if k >= 2:
                        te.wait_ge(s_trev, k - 1)
                    te.transpose(p_tr[k % 2][:], xb[t % 2][:, dc * 128:(dc + 1) * 128], idb[:]).then_inc(s_petr, 1)
                    k += 1
            te.wait_ge(s_trev, NTt * 16)
            for fs in range(12):
                te.wait_ge(s_w[fs % 2], 16 * (fs // 2 + 1))
                for t in range(NTt):
                    g = fs * NTt + t
                    if g >= 2:
                        te.wait_ge(s_ev, g - 1)
                    for dc in range(16):
                        mi = te.matmul(p_mm[g % 2][:],
                                       xtf[:, dc * TT + t * 128: dc * TT + (t + 1) * 128],
                                       wfs[fs % 2][:, dc * 512:(dc + 1) * 512],
                                       start=(dc == 0), stop=(dc == 15))
                    mi.then_inc(s_mm, 1)
    return nc



def build_l0b(TT=T, n_active_cores=NCORE, variant=''):
    NTt = TT // 128
    FB = NTt * 256
    HSEG = NTt // 2
    NTG = NTt // 4
    nc = bass.Bass("TRN2", target_bir_lowering=False)
    qch = nc.dram_tensor("qch", [D, TT], BF16, kind="ExternalInput").ap()
    kvch = nc.dram_tensor("kvch", [D, TT], BF16, kind="ExternalInput").ap()
    csb = nc.dram_tensor("csb", [128, 256], BF16, kind="ExternalInput").ap()
    ub = nc.dram_tensor("ub", [128, 128], BF16, kind="ExternalInput").ap()
    c2b = nc.dram_tensor("c2b", [128, 128], BF16, kind="ExternalInput").ap()
    s2nb = nc.dram_tensor("s2nb", [128, 128], BF16, kind="ExternalInput").ap()
    onescol = nc.dram_tensor("onescol", [128, 1], BF16, kind="ExternalInput").ap()
    onesr = nc.dram_tensor("onesr", [1, 128], BF16, kind="ExternalInput").ap()
    idn = nc.dram_tensor("idn", [128, 128], BF16, kind="ExternalInput").ap()
    flags = nc.dram_tensor("flags", [1, 2], F32, kind="ExternalInput").ap()
    pcolf = nc.dram_tensor("pcolf", [n_active_cores, 1], F32, kind="ExternalInput").ap()
    vht = nc.dram_tensor("vht", [16, NTt, 128, 128], BF16, kind="ExternalOutput").ap()
    kvfn_sp = nc.dram_tensor("kvfn_sp", [16, 128, FB], BF16).ap()
    cc_in = nc.dram_tensor("cc_in", [1, 16 * 512], F32).ap()
    cc_out = nc.dram_tensor("cc_out", [n_active_cores, 16 * 512], F32).ap()

    es = ExitStack()
    sb = lambda n, s, d: es.enter_context(nc.sbuf_tensor(n, s, d))
    ps = lambda n, s, d: es.enter_context(nc.psum_tensor(n, s, d))
    sem = lambda n: es.enter_context(nc.semaphore(n))
    with es:
        csb_s = sb("csb_s", [128, 256], BF16)
        ub_s = sb("ub_s", [128, 128], BF16)
        c2_s = sb("c2_s", [128, 128], BF16)
        s2_s = sb("s2_s", [128, 128], BF16)
        oc_s = sb("oc_s", [128, 1], BF16)
        or_s = sb("or_s", [1, 128], BF16)
        id_s = sb("id_s", [128, 128], BF16)
        fl_s = sb("fl_s", [1, 2], F32)
        pc_s = sb("pc_s", [n_active_cores, 1], F32)
        hin = [sb(f"hin{i}", [128, TT], BF16) for i in range(2)]
        kvn2 = [sb(f"kvn2{i}", [128, FB], BF16) for i in range(2)]
        ff32 = [sb(f"ff32{i}", [128, FB], F32) for i in range(2)]
        fn16 = [sb(f"fn16{i}", [128, FB], BF16) for i in range(2)]
        sqscr = sb("sqscr", [128, 256], BF16)
        ssqs = sb("ssqs", [128, 16], F32)
        invs = sb("invs", [128, 16], F32)
        carryf = sb("carryf", [1, 256], F32)
        carryb = sb("carryb", [1, 256], BF16)
        contrib = sb("contrib", [1, 16 * 512], F32)
        ccs = sb("ccs", [n_active_cores, 16 * 512], F32)
        prow = sb("prow", [1, 16 * 512], F32)
        c1b = sb("c1b", [1, 256], BF16)
        c2rb = sb("c2rb", [1, 256], BF16)
        tmpr = sb("tmpr", [1, 256], F32)
        kvfc = sb("kvfc", [128, FB], BF16)
        qvb = sb("qvb", [128, FB], BF16)
        qvt = sb("qvt", [128, FB], BF16)
        rbm1 = sb("rbm1", [128, 128], F32)
        rbm2 = sb("rbm2", [128, 128], F32)
        vhs = [sb(f"vhs{i}", [128, 512], BF16) for i in range(2)]
        p_fft = [ps(f"p_fft{i}", [128, 256], F32) for i in range(2)]
        p_cum = ps("p_cum", [128, 256], F32)
        p_tot = ps("p_tot", [1, 512], F32)
        p_tr = [ps(f"p_tr{i}", [128, 128], BF16) for i in range(2)]
        p_if = [ps(f"p_if{i}", [128, 512], F32) for i in range(2)]
        s_cl = sem("s_cl")
        s_hd = [sem(f"s_hd{i}") for i in range(2)]
        s_kb = [sem(f"s_kb{i}") for i in range(2)]
        s_fft = sem("s_fft")
        s_fe = sem("s_fe")
        s_sq = sem("s_sq")
        s_nr = sem("s_nr")
        s_nz = sem("s_nz")
        s_sp = [sem(f"s_sp{i}") for i in range(2)]
        s_ta = sem("s_ta")
        s_te = sem("s_te")
        s_cum = sem("s_cum")
        s_ce = sem("s_ce")
        s_cb = sem("s_cb")
        s_qv = sem("s_qv")
        s_petr = sem("s_petr")
        s_trev = sem("s_trev")
        s_if = sem("s_if")
        s_ie = sem("s_ie")
        s_vst = [sem(f"s_vst{i}") for i in range(2)]
        s_cc = sem("s_cc")
        s_cc2 = sem("s_cc2")
        s_cc3 = sem("s_cc3")
        block = es.enter_context(nc.Block())

        NST = 16 if "passa" in variant else 32
        for tok in variant.split("_"):
            if tok.startswith("h") and tok[1:].isdigit():
                NST = int(tok[1:])
        fft_v = lambda s, tc: s * NTt + tc + 1
        nz_v = lambda s: 1 + (s + 1) * NTt
        ta_passA = lambda s, seg: s * 2 + seg + 1
        TA_X0 = 32
        ta_passB = lambda h, tc: 48 + h * NTt + tc + 1
        TE_X0 = 32
        TE_XE = 48
        te_head0 = lambda h: TE_XE + (3 + NTt) * h
        te_carryrow = lambda h, k: te_head0(h) + k + 1
        te_add = lambda h, tc: te_head0(h) + 3 + tc + 1
        cb_head0 = lambda h: (NTt + 1) * h + 1
        cb_upd = lambda h, tc: (NTt + 1) * h + 2 + tc
        cum_v = lambda h, tc: h * NTt + tc + 1
        qv_v = lambda h, tc: h * (6 * NTt) + (tc + 1) * 6
        petr_v = lambda h, k: h * (2 * NTt) + k + 1
        if_v = lambda h, tg: h * NTG + tg + 1
        NCL = 6 if "notiny" in variant else 8

        @block.sync
        def _(sync):
            cl = Ctr()
            cl_pairs = [(csb_s, csb), (ub_s, ub), (c2_s, c2b), (s2_s, s2nb),
                        (or_s, onesr), (id_s, idn)]
            if "notiny" not in variant:
                cl_pairs += [(fl_s, flags), (pc_s, pcolf)]
            for (dst, srcap) in cl_pairs:
                sync.dma_start(out=dst[:], in_=srcap).then_inc(s_cl, 16)
                sync.wait_ge(s_cl, cl.inc(16))
            for s in range(NST):
                h = s if s < 16 else s - 16
                if 2 <= s and s - 2 < 16 and "nospill" not in variant:
                    sp = s - 2
                    sync.wait_ge(s_nz, nz_v(sp))
                    if sp >= 2:
                        sync.wait_ge(s_sp[sp % 2], 16 * (sp // 2))
                    sync.dma_start(out=kvfn_sp[sp], in_=fn16[sp % 2][:]).then_inc(s_sp[sp % 2], 16)
                if s >= 18:
                    hh = s - 18
                    for tg in range(NTG):
                        ig = hh * NTG + tg
                        sync.wait_ge(s_ie, if_v(hh, tg))
                        sync.dma_start(out=vht[hh, tg * 4:(tg + 1) * 4].rearrange("tc p c -> p tc c"),
                                       in_=vhs[ig % 2][:].rearrange("p (tc c) -> p tc c", c=128)
                                       ).then_inc(s_vst[ig % 2], 16)
                src = kvch if s < 16 else qch
                if s >= 2:
                    sync.wait_ge(s_fft, (s - 1) * NTt)
                sync.dma_start(out=hin[s % 2][:], in_=src[h * 128:(h + 1) * 128, :]).then_inc(s_hd[s % 2], 16)
            for hh in (14, 15):
                for tg in range(NTG):
                    ig = hh * NTG + tg
                    sync.wait_ge(s_ie, if_v(hh, tg))
                    sync.dma_start(out=vht[hh, tg * 4:(tg + 1) * 4].rearrange("tc p c -> p tc c"),
                                   in_=vhs[ig % 2][:].rearrange("p (tc c) -> p tc c", c=128)
                                   ).then_inc(s_vst[ig % 2], 16)
            if "passa" in variant:
                nsp = (max(0, min(NST, 16) - 2) if NST < 18 else 16) if "nospill" not in variant else 0
                sync.wait_ge(s_nz, NST * NTt)
                if nsp >= 1:
                    sync.wait_ge(s_sp[0], 16 * ((nsp + 1) // 2))
                if nsp >= 2:
                    sync.wait_ge(s_sp[1], 16 * (nsp // 2))
            else:
                sync.wait_ge(s_vst[0], 16 * ((16 * NTG + 1) // 2))
                sync.wait_ge(s_vst[1], 16 * (16 * NTG // 2))

        @block.gpsimd
        def _(g):
            if "passa" in variant:
                g.nop()
                return
            g.wait_ge(s_te, TE_X0)
            g.dma_start(out=cc_in, in_=contrib[:]).then_inc(s_cc, 16)
            g.wait_ge(s_cc, 16)
            if "nocc" in variant:
                cct = Ctr()
                for r in range(n_active_cores):
                    g.wait_ge(s_cc2, cct.v)
                    g.dma_start(out=cc_out[r:r + 1, :], in_=cc_in).then_inc(s_cc2, 16)
                    cct.inc(16)
            else:
                g.collective_compute("AllGather", ALU.bypass,
                                     replica_groups=[list(range(n_active_cores))],
                                     ins=[cc_in], outs=[cc_out]).then_inc(s_cc2)
            g.wait_ge(s_cc2, 1)
            g.dma_start(out=ccs[:], in_=cc_out).then_inc(s_cc3, 16)
            for h in range(16 if "passa" not in variant else 0):
                g.wait_ge(s_sp[h % 2], 16 * (h // 2 + 1))
                if h >= 2:
                    g.wait_ge(s_cum, cum_v(h - 2, NTt - 1))
                    g.wait_ge(s_ta, 48 + (h - 1) * NTt)
                g.dma_start(out=kvn2[h % 2][:], in_=kvfn_sp[h]).then_inc(s_kb[h % 2], 16)

        @block.tensor
        def _(te):
            te.wait_ge(s_cl, 16 * NCL)
            nfft = 0
            for s in range(NST):
                h = s if s < 16 else s - 16
                te.wait_ge(s_hd[s % 2], 16 * (s // 2 + 1))
                for tc in range(NTt):
                    if nfft >= 2:
                        te.wait_ge(s_fe, nfft - 1)
                        te.wait_ge(s_sq, nfft - 1)
                    te.matmul(p_fft[nfft % 2][:], hin[s % 2][:, tc * 128:(tc + 1) * 128], csb_s[:],
                              start=True, stop=True).then_inc(s_fft, 1)
                    nfft += 1
                if s < 16:
                    if "nototal" in variant:
                        continue
                    te.wait_ge(s_nz, nz_v(s))
                    for seg in range(2):
                        gidx = s * 2 + seg
                        if gidx >= 1:
                            te.wait_ge(s_te, gidx)
                        for j in range(HSEG):
                            tc = seg * HSEG + j
                            mi = te.matmul(p_tot[:, 0:256], oc_s[:],
                                           fn16[s % 2][:, tc * 256:(tc + 1) * 256],
                                           start=(j == 0), stop=(j == HSEG - 1))
                        mi.then_inc(s_ta, 1)
                    continue
                if s == 16:
                    te.wait_ge(s_cc3, 16)
                    te.wait_ge(s_te, TE_X0)
                    for k in range(16):
                        if k >= 1:
                            te.wait_ge(s_te, TE_X0 + k)
                        te.matmul(p_tot[:], pc_s[:], ccs[:, k * 512:(k + 1) * 512],
                                  start=True, stop=True).then_inc(s_ta, 1)
                te.wait_ge(s_nz, nz_v(s))
                te.wait_ge(s_kb[h % 2], 16 * (h // 2 + 1))
                te.wait_ge(s_te, te_carryrow(h, 2))
                for tc in range(NTt):
                    if cum_v(h, tc) >= 2:
                        te.wait_ge(s_ce, cum_v(h, tc) - 1)
                    te.wait_ge(s_cb, cb_head0(h) if tc == 0 else cb_upd(h, tc - 1))
                    te.matmul(p_cum[:], ub_s[:], kvn2[h % 2][:, tc * 256:(tc + 1) * 256],
                              start=True, stop=False)
                    te.matmul(p_cum[:], or_s[:], carryb[:], start=False, stop=False)
                    te.matmul(p_cum[:], or_s[:], (c1b if tc < HSEG else c2rb)[:],
                              start=False, stop=True).then_inc(s_cum, 1)
                    te.wait_ge(s_te, te_add(h, tc) - 1)
                    te.matmul(p_tot[:, 0:256], oc_s[:], kvn2[h % 2][:, tc * 256:(tc + 1) * 256],
                              start=True, stop=True).then_inc(s_ta, 1)
                ntr = h * 2 * NTt
                for tc in range(NTt):
                    te.wait_ge(s_qv, qv_v(h, tc))
                    for half in range(2):
                        if ntr >= 2:
                            te.wait_ge(s_trev, ntr - 1)
                        te.transpose(p_tr[ntr % 2][:],
                                     qvb[:, tc * 256 + half * 128: tc * 256 + half * 128 + 128],
                                     id_s[:]).then_inc(s_petr, 1)
                        ntr += 1
                for tg in range(NTG):
                    ig = h * NTG + tg
                    te.wait_ge(s_trev, h * 2 * NTt + (tg + 1) * 8)
                    if ig >= 2:
                        te.wait_ge(s_ie, ig - 1)
                    te.matmul(p_if[ig % 2][:], c2_s[:],
                              qvt[:, tg * 1024: tg * 1024 + 512], start=True, stop=False)
                    te.matmul(p_if[ig % 2][:], s2_s[:],
                              qvt[:, tg * 1024 + 512: tg * 1024 + 1024],
                              start=False, stop=True).then_inc(s_if, 1)

        @block.scalar
        def _(sc):
            nsq = 0
            ntrev = 0
            for s in range(NST):
                h = s if s < 16 else s - 16
                for tc in range(NTt):
                    sc.wait_ge(s_fft, nsq + 1)
                    sc.wait_ge(s_fe, nsq + 1)
                    if nsq >= 1:
                        sc.wait_ge(s_sq, nsq)
                    if "nosq" in variant:
                        sc.copy(sqscr[:], p_fft[nsq % 2][:]).then_inc(s_sq, 1)
                    else:
                        sc.activation(sqscr[:], p_fft[nsq % 2][:], AF.Square,
                                      accum_out=ssqs[:, tc:tc + 1]).then_inc(s_sq, 1)
                    nsq += 1
                sc.wait_ge(s_sq, (s + 1) * NTt)
                if s >= 1:
                    sc.wait_ge(s_nz, nz_v(s - 1))
                if "nosq" in variant:
                    sc.nop().then_inc(s_nr, 1)
                    sc.wait_ge(s_nr, 2 * s + 1)
                    sc.nop().then_inc(s_nr, 1)
                else:
                    sc.activation(ssqs[:, 0:NTt], ssqs[:, 0:NTt], AF.Ln).then_inc(s_nr, 1)
                    sc.wait_ge(s_nr, 2 * s + 1)
                    sc.activation(invs[:, 0:NTt], ssqs[:, 0:NTt], AF.Exp, scale=-0.5).then_inc(s_nr, 1)
                if s >= 16:
                    sc.wait_ge(s_if, NTG * h)
                    for k2 in range(2 * NTt):
                        sc.wait_ge(s_petr, petr_v(h, k2) - 1 + 1)
                        tc = k2 // 2
                        half = k2 % 2
                        tg = tc // 4
                        dst = qvt[:, tg * 1024 + half * 512 + (tc % 4) * 128:
                                  tg * 1024 + half * 512 + (tc % 4) * 128 + 128]
                        sc.copy(dst, p_tr[ntrev % 2][:]).then_inc(s_trev, 1)
                        ntrev += 1

        @block.vector
        def _(v):
            v.wait_ge(s_cl, 16 * NCL)
            v.memset(oc_s[:], 1.0).then_inc(s_nz, 1)
            for s in range(NST):
                h = s if s < 16 else s - 16
                if s >= 2:
                    v.wait_ge(s_nz, nz_v(s - 1))
                for tc in range(NTt):
                    v.wait_ge(s_fft, fft_v(s, tc))
                    v.tensor_copy(ff32[s % 2][:, tc * 256:(tc + 1) * 256],
                                  p_fft[(s * NTt + tc) % 2][:]).then_inc(s_fe, 1)
                v.wait_ge(s_fe, (s + 1) * NTt)
                v.wait_ge(s_nr, 2 * s + 2)
                if 2 <= s < 18:
                    v.wait_ge(s_sp[s % 2], 16 * (s // 2))
                    if s < 16:
                        pass
                if (2 <= s < 16 or s in (16, 17)) and "nototal" not in variant:
                    if s - 2 < 16:
                        v.wait_ge(s_ta, 2 * (s - 1))
                if s >= 18:
                    v.wait_ge(s_qv, qv_v(s - 18, NTt - 1))
                for tc in range(NTt):
                    v.tensor_scalar(fn16[s % 2][:, tc * 256:(tc + 1) * 256],
                                    ff32[s % 2][:, tc * 256:(tc + 1) * 256],
                                    (1.0 if "nosq" in variant else invs[:, tc:tc + 1]),
                                    None, ALU.mult).then_inc(s_nz, 1)
                if s < 16 and "nototal" not in variant:
                    for seg in range(2):
                        v.wait_ge(s_ta, ta_passA(s, seg))
                        v.tensor_copy(contrib[:, s * 512 + seg * 256: s * 512 + (seg + 1) * 256],
                                      p_tot[:, 0:256]).then_inc(s_te, 1)
                if s >= 16:
                    if h == 0:
                        for k in range(16):
                            v.wait_ge(s_ta, TA_X0 + k + 1)
                            v.tensor_copy(prow[:, k * 512:(k + 1) * 512], p_tot[:]).then_inc(s_te, 1)
                    if h >= 1:
                        v.wait_ge(s_cum, cum_v(h - 1, NTt - 1))
                    v.tensor_scalar(c1b[:], prow[:, h * 512: h * 512 + 256],
                                    fl_s[:, 0:1], None, ALU.mult).then_inc(s_te, 1)
                    v.tensor_scalar(tmpr[:], prow[:, h * 512 + 256: h * 512 + 512],
                                    fl_s[:, 1:2], None, ALU.mult).then_inc(s_te, 1)
                    v.wait_ge(s_te, te_carryrow(h, 1))
                    v.tensor_tensor(c2rb[:], prow[:, h * 512: h * 512 + 256], tmpr[:],
                                    ALU.add).then_inc(s_te, 1)
                    v.memset(carryf[:], 0.0)
                    v.memset(carryb[:], 0.0).then_inc(s_cb, 1)
                    if h >= 1:
                        v.wait_ge(s_petr, petr_v(h - 1, 2 * NTt - 1))
                        v.wait_ge(s_qv, qv_v(h - 1, NTt - 1))
                    for tc in range(NTt):
                        v.wait_ge(s_cum, cum_v(h, tc))
                        v.tensor_copy(kvfc[:, tc * 256:(tc + 1) * 256], p_cum[:]).then_inc(s_ce, 1)
                        v.wait_ge(s_ta, ta_passB(h, tc))
                        v.tensor_tensor(carryf[:], carryf[:], p_tot[:, 0:256], ALU.add).then_inc(s_te, 1)
                        v.wait_ge(s_te, te_add(h, tc))
                        v.tensor_copy(carryb[:], carryf[:]).then_inc(s_cb, 1)
                        qr = fn16[s % 2][:, tc * 256: tc * 256 + 128]
                        qi = fn16[s % 2][:, tc * 256 + 128: tc * 256 + 256]
                        kr = kvfc[:, tc * 256: tc * 256 + 128]
                        ki = kvfc[:, tc * 256 + 128: tc * 256 + 256]
                        orr = qvb[:, tc * 256: tc * 256 + 128]
                        oii = qvb[:, tc * 256 + 128: tc * 256 + 256]
                        base = qv_v(h, tc) - 6
                        v.wait_ge(s_ce, cum_v(h, tc))
                        if base >= 1:
                            v.wait_ge(s_qv, base)
                        v.tensor_tensor(rbm1[:], qr, kr, ALU.mult).then_inc(s_qv, 1)
                        v.tensor_tensor(rbm2[:], qi, ki, ALU.mult).then_inc(s_qv, 1)
                        v.wait_ge(s_qv, base + 2)
                        v.tensor_tensor(orr, rbm1[:], rbm2[:], ALU.subtract).then_inc(s_qv, 1)
                        v.wait_ge(s_qv, base + 3)
                        v.tensor_tensor(rbm1[:], qr, ki, ALU.mult).then_inc(s_qv, 1)
                        v.tensor_tensor(rbm2[:], qi, kr, ALU.mult).then_inc(s_qv, 1)
                        v.wait_ge(s_qv, base + 5)
                        v.tensor_tensor(oii, rbm1[:], rbm2[:], ALU.add).then_inc(s_qv, 1)
                    for tg in range(NTG):
                        ig = h * NTG + tg
                        v.wait_ge(s_if, if_v(h, tg))
                        if ig >= 2:
                            v.wait_ge(s_vst[ig % 2], 16 * (ig // 2))
                        v.tensor_copy(vhs[ig % 2][:], p_if[ig % 2][:]).then_inc(s_ie, 1)
    return nc


def build_moe(TT=T, n_active_cores=NCORE, CAP=2560.0):
    NTt = TT // 128
    HS = NTt // 2
    TH = TT // 2
    NE = 8
    TW1 = 512 if TH % 512 == 0 else TH
    NG1 = TH // TW1
    nc = bass.Bass("TRN2", target_bir_lowering=False)
    x = nc.dram_tensor("x", [TT, D], F32, kind="ExternalInput").ap()
    gt = nc.dram_tensor("gt", [16, 128, 8], F32, kind="ExternalInput").ap()
    w1t = nc.dram_tensor("w1t", [8, 16, 16, 128, 128], F32, kind="ExternalInput").ap()
    w2t = nc.dram_tensor("w2t", [8, 4, 16, 128, 512], F32, kind="ExternalInput").ap()
    iota8 = nc.dram_tensor("iota8", [128, 8], F32, kind="ExternalInput").ap()
    iotabig = nc.dram_tensor("iotabig", [128, 8], F32, kind="ExternalInput").ap()
    uf32 = nc.dram_tensor("uf32", [128, 128], F32, kind="ExternalInput").ap()
    onescf = nc.dram_tensor("onescf", [128, 1], F32, kind="ExternalInput").ap()
    onesrf = nc.dram_tensor("onesrf", [1, 128], F32, kind="ExternalInput").ap()
    idn = nc.dram_tensor("idn", [128, 128], BF16, kind="ExternalInput").ap()
    flags = nc.dram_tensor("flags", [1, 2], F32, kind="ExternalInput").ap()
    pcolf = nc.dram_tensor("pcolf", [n_active_cores, 1], F32, kind="ExternalInput").ap()
    y = nc.dram_tensor("y", [TT, D], F32, kind="ExternalOutput").ap()
    cc_in = nc.dram_tensor("cc_in", [1, 16], F32).ap()
    cc_out = nc.dram_tensor("cc_out", [n_active_cores, 16], F32).ap()

    es = ExitStack()
    sb = lambda n, s, d: es.enter_context(nc.sbuf_tensor(n, s, d))
    ps = lambda n, s, d: es.enter_context(nc.psum_tensor(n, s, d))
    sem = lambda n: es.enter_context(nc.semaphore(n))
    with es:
        xb = sb("xb", [128, NTt * D], BF16)
        xf = sb("xf", [128, D], F32)
        xlo = sb("xlo", [128, D], BF16)
        gtf = sb("gtf", [128, 128], F32)
        ghi = sb("ghi", [128, 128], BF16)
        glo = sb("glo", [128, 128], BF16)
        io8 = sb("io8", [128, 8], F32)
        iobig = sb("iobig", [128, 8], F32)
        uf_s = sb("uf_s", [128, 128], F32)
        ocf = sb("ocf", [128, 1], F32)
        orf = sb("orf", [1, 128], F32)
        id_s = sb("id_s", [128, 128], BF16)
        fl_s = sb("fl_s", [1, 2], F32)
        pc_s = sb("pc_s", [n_active_cores, 1], F32)
        xtr = [sb(f"xtr{i}", [128, 16 * 128], BF16) for i in range(2)]
        xlotr = sb("xlotr", [128, 16 * 128], BF16)
        xth = sb("xth", [128, 16 * TH], BF16)
        lg = sb("lg", [128, 8], F32)
        pu = sb("pu", [128, 8], F32)
        mx = sb("mx", [128, 4], F32)
        equ = sb("equ", [128, 8], F32)
        zed = sb("zed", [128, 8], F32)
        idxc = sb("idxc", [128, 1], F32)
        m1b = sb("m1b", [128, NTt * 8], F32)
        pum = sb("pum", [128, NTt * 8], F32)
        rsmb = sb("rsmb", [128, NTt], F32)
        rkl = sb("rkl", [128, NTt * 8], F32)
        carry8 = sb("carry8", [1, 8], F32)
        contrib = sb("contrib", [1, 16], F32)
        ccs = sb("ccs", [n_active_cores, 16], F32)
        prow16 = sb("prow16", [1, 16], F32)
        c1c = sb("c1c", [1, 8], F32)
        c2c = sb("c2c", [1, 8], F32)
        thr = sb("thr", [1, 16], F32)
        threp = sb("threp", [128, 16], F32)
        keptm = sb("keptm", [128, 8], F32)
        wem = sb("wem", [128, NTt * 8], F32)
        hch = sb("hch", [128, 16 * TH], BF16)
        w1s = [sb(f"w1s{i}", [128, 16 * 128], BF16) for i in range(2)]
        w2s = [sb(f"w2s{i}", [128, 16 * 512], BF16) for i in range(2)]
        acc8 = sb("acc8", [128, 2], F32)
        stat = sb("stat", [128, 4], F32)
        epsc = sb("epsc", [128, 1], F32)
        yo = [sb("yo0", [128, D], F32)] * 2
        p_gt = ps("p_gt", [128, 8], F32)
        p_rkt = ps("p_rkt", [128, 16], F32)
        p_tr = [ps(f"p_tr{i}", [128, 128], BF16) for i in range(2)]
        p1 = [ps(f"p1_{i}", [128, 512], F32) for i in range(2)]
        p2 = [ps(f"p2_{i}", [128, 512], F32) for i in range(2)]
        s_cl = sem("s_cl")
        s_x = [sem(f"s_x{i}") for i in range(2)]
        s_vc = sem("s_vc")
        s_petr = sem("s_petr")
        s_trev = sem("s_trev")
        s_gmm = sem("s_gmm")
        s_rt = sem("s_rt")
        s_ax = sem("s_ax")
        s_rk = sem("s_rk")
        s_rke = sem("s_rke")
        s_cc = sem("s_cc")
        s_cc2 = sem("s_cc2")
        s_cc3 = sem("s_cc3")
        s_pw = sem("s_pw")
        s_eptv = sem("s_eptv")
        s_w1 = [sem(f"s_w1{i}") for i in range(2)]
        s_w2 = [sem(f"s_w2{i}") for i in range(2)]
        s_m1 = sem("s_m1")
        s_ge = sem("s_ge")
        s_m2 = sem("s_m2")
        s_ac = sem("s_ac")
        s_vs = sem("s_vs")
        s_lnx = sem("s_lnx")
        s_rs = sem("s_rs")
        s_no = sem("s_no")
        s_st = [sem(f"s_st{i}") for i in range(2)]
        block = es.enter_context(nc.Block())

        vc_v = lambda t: 2 + 2 * (t + 1)
        rt_v = lambda t, k: 10 * t + k
        rke_ev = lambda t: 1 + 2 * t + 1 + (1 if t >= HS else 0)
        rke_cu = lambda t: 1 + 2 * t + 2 + (1 if t >= HS else 0)
        RKE_TOT = 2 * NTt + 3
        m1_v = lambda th, e, ic, tg: (((th * NE + e) * 16 + ic) * NG1) + tg + 1
        M1_TOT = 2 * NE * 16 * NG1
        m2_v = lambda th, e, os_, tc: (((th * NE + e) * 4 + os_) * HS) + tc + 1
        M2_TOT = 2 * NE * HS * 4
        PW_TOT = 6 + 3 * NTt
        geval = lambda th, e, ic, tg: m1_v(th, e, ic, tg)

        @block.sync
        def _(sync):
            cl = Ctr()
            for (dst, srcap) in ((gtf[:].rearrange("p (dc e) -> p dc e", e=8), gt.rearrange("dc p e -> p dc e")),
                                 (io8[:], iota8), (iobig[:], iotabig), (uf_s[:], uf32),
                                 (ocf[:], onescf), (orf[:], onesrf), (id_s[:], idn),
                                 (fl_s[:], flags), (pc_s[:], pcolf)):
                sync.dma_start(out=dst, in_=srcap).then_inc(s_cl, 16)
                sync.wait_ge(s_cl, cl.inc(16))
            for t in range(NTt):
                sync.wait_ge(s_no, t + 1)
                sync.dma_start(out=y[t * 128:(t + 1) * 128, :],
                               in_=yo[t % 2][:]).then_inc(s_st[t % 2], 16)
            sync.wait_ge(s_st[0], 16 * ((NTt + 1) // 2))
            sync.wait_ge(s_st[1], 16 * (NTt // 2))

        @block.gpsimd
        def _(g):
            for t in range(NTt):
                if t >= 1:
                    g.wait_ge(s_vc, vc_v(t - 1))
                if t >= 2:
                    g.wait_ge(s_x[t % 2], 16 * (t // 2))
                g.dma_start(out=xf[:], in_=x[t * 128:(t + 1) * 128, :]).then_inc(s_x[t % 2], 16)
            g.wait_ge(s_rke, RKE_TOT)
            g.dma_start(out=cc_in, in_=contrib[:]).then_inc(s_cc, 16)
            g.wait_ge(s_cc, 16)
            g.collective_compute("AllGather", ALU.bypass,
                                 replica_groups=[list(range(n_active_cores))],
                                 ins=[cc_in], outs=[cc_out]).then_inc(s_cc2)
            g.wait_ge(s_cc2, 1)
            g.dma_start(out=ccs[:], in_=cc_out).then_inc(s_cc3, 16)
            for th in range(2):
                for e in range(NE):
                    for ic in range(16):
                        k = (th * NE + e) * 16 + ic
                        if k >= 2:
                            g.wait_ge(s_m1, (k - 2) * NG1 + NG1)
                        g.dma_start(out=w1s[k % 2][:].rearrange("p (dc c) -> p dc c", c=128),
                                    in_=w1t[e, ic].rearrange("dc p c -> p dc c")).then_inc(s_w1[k % 2], 16)
                    for os_ in range(4):
                        k2 = (th * NE + e) * 4 + os_
                        if k2 >= 2:
                            ko = k2 - 2
                            oth, rem = divmod(ko, NE * 4)
                            oe, oos = divmod(rem, 4)
                            g.wait_ge(s_m2, m2_v(oth, oe, oos, HS - 1))
                        g.dma_start(out=w2s[k2 % 2][:].rearrange("p (ic c) -> p ic c", c=512),
                                    in_=w2t[e, os_].rearrange("ic p c -> p ic c")).then_inc(s_w2[k2 % 2], 16)

        @block.tensor
        def _(te):
            te.wait_ge(s_cl, 16 * 9)
            ktr = 0
            for t in range(NTt):
                te.wait_ge(s_vc, vc_v(t))
                for which in range(2):
                    for dc in range(16):
                        if ktr >= 2:
                            te.wait_ge(s_trev, ktr - 1)
                        srcap = (xb[:, t * D + dc * 128: t * D + (dc + 1) * 128] if which == 0
                                 else xlo[:, dc * 128:(dc + 1) * 128])
                        te.transpose(p_tr[ktr % 2][:], srcap, id_s[:]).then_inc(s_petr, 1)
                        ktr += 1
                te.wait_ge(s_trev, (t + 1) * 32)
                if t >= 1:
                    te.wait_ge(s_rt, rt_v(t - 1, 1))
                n_emitted = 0
                for grp, buf in ((0, None), (1, None), (2, None)):
                    for dc in range(16):
                        lhs = (xtr[t % 2] if grp != 1 else xlotr)[:, dc * 128:(dc + 1) * 128]
                        rhs = (ghi if grp != 1 else ghi) if grp != 2 else glo
                        rhs = rhs[:, dc * 8:(dc + 1) * 8]
                        last = (grp == 2 and dc == 15)
                        mi = te.matmul(p_gt[:], lhs, rhs, start=(n_emitted == 0), stop=last)
                        n_emitted += 1
                mi.then_inc(s_gmm, 1)
                te.wait_ge(s_rt, rt_v(t, 9))
                te.wait_ge(s_rke, rke_cu(t - 1) if t >= 1 else 1)
                te.matmul(p_rkt[:, 0:8], uf_s[:], m1b[:, t * 8:(t + 1) * 8],
                          start=True, stop=False)
                te.matmul(p_rkt[:, 0:8], orf[:], carry8[:], start=False, stop=True)
                te.matmul(p_rkt[0:1, 8:16], ocf[:], m1b[:, t * 8:(t + 1) * 8],
                          start=True, stop=True).then_inc(s_rk, 1)
            te.wait_ge(s_cc3, 16)
            te.wait_ge(s_rke, RKE_TOT)
            te.matmul(p_rkt[0:1, 0:16], pc_s[:], ccs[:], start=True, stop=True).then_inc(s_rk, 1)
            te.wait_ge(s_pw, 5)
            te.matmul(p_rkt[:, 0:16], orf[:], thr[:], start=True, stop=True).then_inc(s_rk, 1)
            for th in range(2):
                if th == 1:
                    te.wait_ge(s_m1, m1_v(0, NE - 1, 15, NG1 - 1))
                for tc in range(HS):
                    t = th * HS + tc
                    for dc in range(16):
                        if ktr >= 2:
                            te.wait_ge(s_trev, ktr - 1)
                        te.transpose(p_tr[ktr % 2][:], xb[:, t * D + dc * 128: t * D + (dc + 1) * 128],
                                     id_s[:]).then_inc(s_petr, 1)
                        ktr += 1
                te.wait_ge(s_trev, 32 * NTt + (th * HS + HS) * 16)
                for e in range(NE):
                    for ic in range(16):
                        k = (th * NE + e) * 16 + ic
                        te.wait_ge(s_w1[k % 2], 16 * (k // 2 + 1))
                        for tg in range(NG1):
                            gi = m1_v(th, e, ic, tg)
                            if gi >= 3:
                                te.wait_ge(s_ge, gi - 2)
                            for dc in range(16):
                                mi = te.matmul(p1[gi % 2][:, 0:TW1],
                                               w1s[k % 2][:, dc * 128:(dc + 1) * 128],
                                               xth[:, dc * TH + tg * TW1: dc * TH + tg * TW1 + TW1],
                                               start=(dc == 0), stop=(dc == 15))
                            mi.then_inc(s_m1, 1)
                    te.wait_ge(s_ge, m1_v(th, e, 15, NG1 - 1))
                    for os_ in range(4):
                        te.wait_ge(s_w2[((th * NE + e) * 4 + os_) % 2],
                                   16 * (((th * NE + e) * 4 + os_) // 2 + 1))
                        for tc in range(HS):
                            gi2 = m2_v(th, e, os_, tc)
                            if gi2 >= 3:
                                te.wait_ge(s_ac, gi2 - 2)
                            for ic in range(16):
                                mi = te.matmul(p2[gi2 % 2][:],
                                               hch[:, ic * TH + tc * 128: ic * TH + tc * 128 + 128],
                                               w2s[((th * NE + e) * 4 + os_) % 2][:, ic * 512:(ic + 1) * 512],
                                               start=(ic == 0), stop=(ic == 15))
                            mi.then_inc(s_m2, 1)

        @block.scalar
        def _(sc):
            sc.wait_ge(s_cl, 16 * 9)
            ktr = 0
            for t in range(NTt):
                for k2 in range(32):
                    sc.wait_ge(s_petr, ktr + 1)
                    dst = (xtr[t % 2][:, (k2 % 16) * 128:(k2 % 16 + 1) * 128] if k2 < 16
                           else xlotr[:, (k2 - 16) * 128:(k2 - 15) * 128])
                    if k2 < 16 and t >= 2:
                        sc.wait_ge(s_gmm, t - 1)
                    if k2 == 16 and t >= 1:
                        sc.wait_ge(s_gmm, t)
                    sc.copy(dst, p_tr[ktr % 2][:]).then_inc(s_trev, 1)
                    ktr += 1
                sc.wait_ge(s_rt, rt_v(t, 3))
                sc.activation(pu[:], lg[:], AF.Exp, bias=mx[:, 1:2]).then_inc(s_ax, 1)
            for th in range(2):
                for tc in range(HS):
                    for dc in range(16):
                        sc.wait_ge(s_petr, ktr + 1)
                        t = th * HS + tc
                        sc.copy(xth[:, dc * TH + tc * 128: dc * TH + (tc + 1) * 128],
                                p_tr[ktr % 2][:]).then_inc(s_trev, 1)
                        ktr += 1
                for e in range(NE):
                    if e >= 1 or th >= 1:
                        pe = (th, e - 1) if e >= 1 else (th - 1, NE - 1)
                        sc.wait_ge(s_m2, m2_v(pe[0], pe[1], 3, HS - 1))
                    for ic in range(16):
                        for tg in range(NG1):
                            gi = m1_v(th, e, ic, tg)
                            sc.wait_ge(s_m1, gi)
                            sc.activation(hch[:, ic * TH + tg * TW1: ic * TH + tg * TW1 + TW1],
                                          p1[gi % 2][:, 0:TW1], AF.Gelu).then_inc(s_ge, 1)
            for t in range(NTt):
                sc.wait_ge(s_vs, 5 * (t + 1))
                sc.activation(stat[:, 2:3], stat[:, 2:3], AF.Ln, bias=epsc[:]).then_inc(s_lnx, 1)
                sc.wait_ge(s_lnx, t + 1)
                sc.activation(stat[:, 3:4], stat[:, 2:3], AF.Exp, scale=-0.5).then_inc(s_rs, 1)

        @block.vector
        def _(v):
            v.wait_ge(s_cl, 16 * 9)
            v.memset(epsc[:], EPS)
            v.memset(carry8[:], 0.0).then_inc(s_rke, 1)
            v.tensor_copy(ghi[:], gtf[:]).then_inc(s_vc, 1)
            v.wait_ge(s_vc, 1)
            v.scalar_tensor_tensor(glo[:], gtf[:], 1.0, ghi[:], ALU.mult, ALU.subtract).then_inc(s_vc, 1)
            for t in range(NTt):
                v.wait_ge(s_x[t % 2], 16 * (t // 2 + 1))
                if t >= 1:
                    v.wait_ge(s_petr, 32 * t)
                v.tensor_copy(xb[:, t * D:(t + 1) * D], xf[:]).then_inc(s_vc, 1)
                v.wait_ge(s_vc, vc_v(t) - 1)
                v.scalar_tensor_tensor(xlo[:], xf[:], 1.0, xb[:, t * D:(t + 1) * D],
                                       ALU.mult, ALU.subtract).then_inc(s_vc, 1)
                v.wait_ge(s_gmm, t + 1)
                if t >= 1:
                    v.wait_ge(s_ax, t)
                v.tensor_copy(lg[:], p_gt[:]).then_inc(s_rt, 1)
                v.wait_ge(s_rt, rt_v(t, 1))
                v.tensor_reduce(mx[:, 0:1], lg[:], AX.X, ALU.max).then_inc(s_rt, 1)
                v.wait_ge(s_rt, rt_v(t, 2))
                v.tensor_scalar(mx[:, 1:2], mx[:, 0:1], -1.0, None, ALU.mult).then_inc(s_rt, 1)
                v.wait_ge(s_ax, t + 1)
                v.tensor_reduce(mx[:, 2:3], pu[:], AX.X, ALU.add).then_inc(s_rt, 1)
                v.wait_ge(s_rt, rt_v(t, 4))
                v.reciprocal(rsmb[:, t:t + 1], mx[:, 2:3]).then_inc(s_rt, 1)
                v.tensor_scalar(equ[:], lg[:], mx[:, 0:1], None, ALU.is_equal).then_inc(s_rt, 1)
                v.wait_ge(s_rt, rt_v(t, 6))
                v.scalar_tensor_tensor(zed[:], equ[:], -1000.0, iobig[:], ALU.mult, ALU.add).then_inc(s_rt, 1)
                v.wait_ge(s_rt, rt_v(t, 7))
                v.tensor_reduce(idxc[:], zed[:], AX.X, ALU.min).then_inc(s_rt, 1)
                v.wait_ge(s_rt, rt_v(t, 8))
                v.tensor_scalar(m1b[:, t * 8:(t + 1) * 8], io8[:], idxc[:], None,
                                ALU.is_equal).then_inc(s_rt, 1)
                v.wait_ge(s_rt, rt_v(t, 9))
                v.tensor_tensor(pum[:, t * 8:(t + 1) * 8], pu[:], m1b[:, t * 8:(t + 1) * 8],
                                ALU.mult).then_inc(s_rt, 1)
                v.wait_ge(s_rk, t + 1)
                v.tensor_copy(rkl[:, t * 8:(t + 1) * 8], p_rkt[:, 0:8]).then_inc(s_rke, 1)
                v.wait_ge(s_rke, rke_ev(t))
                v.tensor_tensor(carry8[:], carry8[:], p_rkt[0:1, 8:16], ALU.add).then_inc(s_rke, 1)
                
                if t == HS - 1:
                    v.wait_ge(s_rke, rke_cu(t))
                    v.tensor_copy(contrib[:, 0:8], carry8[:]).then_inc(s_rke, 1)
                if t == NTt - 1:
                    v.wait_ge(s_rke, rke_cu(t))
                    v.tensor_copy(contrib[:, 8:16], carry8[:]).then_inc(s_rke, 1)
            v.wait_ge(s_rk, NTt + 1)
            v.tensor_copy(prow16[:], p_rkt[0:1, 0:16]).then_inc(s_pw, 1)
            v.wait_ge(s_pw, 1)
            v.tensor_scalar(c1c[:], prow16[:, 0:8], fl_s[:, 0:1], None, ALU.mult).then_inc(s_pw, 1)
            v.scalar_tensor_tensor(c2c[:], prow16[:, 8:16], fl_s[:, 1:2], prow16[:, 0:8],
                                   ALU.mult, ALU.add).then_inc(s_pw, 1)
            v.wait_ge(s_pw, 3)
            v.tensor_scalar(thr[:, 0:8], c1c[:], -1.0, CAP, ALU.mult, ALU.add).then_inc(s_pw, 1)
            v.tensor_scalar(thr[:, 8:16], c2c[:], -1.0, CAP, ALU.mult, ALU.add).then_inc(s_pw, 1)
            v.wait_ge(s_rk, NTt + 2)
            v.tensor_copy(threp[:], p_rkt[:, 0:16]).then_inc(s_pw, 1)
            for t in range(NTt):
                seg = 0 if t < HS else 1
                v.wait_ge(s_pw, 6 + 3 * t)
                v.tensor_tensor(keptm[:], rkl[:, t * 8:(t + 1) * 8],
                                threp[:, seg * 8:(seg + 1) * 8], ALU.is_le).then_inc(s_pw, 1)
                v.wait_ge(s_pw, 6 + 3 * t + 1)
                v.tensor_tensor(wem[:, t * 8:(t + 1) * 8], pum[:, t * 8:(t + 1) * 8],
                                keptm[:], ALU.mult).then_inc(s_pw, 1)
                v.wait_ge(s_pw, 6 + 3 * t + 2)
                v.tensor_scalar(wem[:, t * 8:(t + 1) * 8], wem[:, t * 8:(t + 1) * 8],
                                rsmb[:, t:t + 1], None, ALU.mult).then_inc(s_pw, 1)
            v.wait_ge(s_pw, PW_TOT)
            for th in range(2):
                for e in range(NE):
                    for os_ in range(4):
                        for tc in range(HS):
                            t = th * HS + tc
                            gi2 = m2_v(th, e, os_, tc)
                            v.wait_ge(s_m2, gi2)
                            v.scalar_tensor_tensor(
                                xb[:, t * D + os_ * 512: t * D + os_ * 512 + 512],
                                p2[gi2 % 2][:], wem[:, t * 8 + e: t * 8 + e + 1],
                                xb[:, t * D + os_ * 512: t * D + os_ * 512 + 512],
                                ALU.mult, ALU.add).then_inc(s_ac, 1)
            for t in range(NTt):
                if t >= 1:
                    v.wait_ge(s_rs, t)
                    if t - 1 >= 1:
                        v.wait_ge(s_st[(t - 2) % 2], 16 * ((t - 2) // 2 + 1))
                    v.tensor_scalar(yo[(t - 1) % 2][:], xb[:, (t - 1) * D: t * D],
                                    stat[:, 0:1], stat[:, 3:4],
                                    ALU.subtract, ALU.mult).then_inc(s_no, 1)
                th = t // HS
                v.wait_ge(s_ac, m2_v(th, NE - 1, 3, t % HS))
                if t >= 1:
                    v.wait_ge(s_vs, 5 * t)
                v.tensor_scalar(xlo[:], xb[:, t * D:(t + 1) * D], 1.0, 0.0, ALU.mult,
                                ALU.add, accum_out=acc8[:, 0:1]).then_inc(s_vs, 1)
                v.wait_ge(s_vs, 5 * t + 1)
                v.scalar_tensor_tensor(xlo[:], xb[:, t * D:(t + 1) * D], 1.0,
                                       xb[:, t * D:(t + 1) * D], ALU.mult, ALU.mult,
                                       accum_out=acc8[:, 1:2]).then_inc(s_vs, 1)
                v.wait_ge(s_vs, 5 * t + 2)
                v.tensor_scalar(stat[:, 0:1], acc8[:, 0:1], 1.0 / D, None, ALU.mult).then_inc(s_vs, 1)
                v.wait_ge(s_vs, 5 * t + 3)
                v.tensor_tensor(stat[:, 1:2], stat[:, 0:1], stat[:, 0:1], ALU.mult).then_inc(s_vs, 1)
                v.wait_ge(s_vs, 5 * t + 4)
                v.scalar_tensor_tensor(stat[:, 2:3], acc8[:, 1:2], 1.0 / D, stat[:, 1:2],
                                       ALU.mult, ALU.subtract).then_inc(s_vs, 1)
            t = NTt - 1
            v.wait_ge(s_rs, NTt)
            v.wait_ge(s_st[(t - 1) % 2], 16 * ((t - 1) // 2 + 1))
            v.tensor_scalar(yo[t % 2][:], xb[:, t * D:(t + 1) * D], stat[:, 0:1], stat[:, 3:4],
                            ALU.subtract, ALU.mult).then_inc(s_no, 1)
    return nc


def l1b_sched(TT):
    NTt = TT // 128
    H = NTt // 2
    NG = NTt // 4
    groups = []
    for g in range(NG):
        js = range(4 * g, 4 * g + 4)
        gqA = [j if j < H else j + NTt for j in js]
        gqB = [j + H for j in js]
        ext = max(gqA + gqB) + 1
        crossing = sorted(set(gqA) | set(gqB))
        groups.append((ext, crossing))
    return groups

def build_l1b(TT=T, variant=''):
    NTt = TT // 128
    SF = 2 * TT
    NKC = SF // 128
    NG = NTt // 4
    groups = l1b_sched(TT)
    NCROSS = len(groups[0][1])
    OS = 512
    NOS = D // OS
    EXPB = -20.0

    nc = bass.Bass("TRN2", target_bir_lowering=False)
    q = nc.dram_tensor("q", [TT, D], BF16, kind="ExternalInput").ap()
    kfull = nc.dram_tensor("kfull", [SF, D], BF16, kind="ExternalInput").ap()
    vfull = nc.dram_tensor("vfull", [SF, D], BF16, kind="ExternalInput").ap()
    x = nc.dram_tensor("x", [TT, D], F32, kind="ExternalInput").ap()
    wout = nc.dram_tensor("wout", [16, 128, D], F32, kind="ExternalInput").ap()
    masks = nc.dram_tensor("masks", [NG, NCROSS, 128, 512], BF16, kind="ExternalInput").ap()
    biastab = nc.dram_tensor("biastab", [128, NKC * NG], F32, kind="ExternalInput").ap()
    idn = nc.dram_tensor("idn", [128, 128], BF16, kind="ExternalInput").ap()
    y = nc.dram_tensor("y", [TT, D], F32, kind="ExternalOutput").ap()
    aot = nc.dram_tensor("aot", [16, NTt, 128, 128], BF16).ap()

    KT0 = 0
    QT0 = 2 * SF
    V0 = QT0 + 2 * TT
    VSL = 132
    MK0 = V0 + 2 * (NKC * VSL)
    BIGN = max(MK0 + NG * NCROSS * 512, 16 * D)
    WOUT0 = 0

    es = ExitStack()
    sb = lambda n, s, d: es.enter_context(nc.sbuf_tensor(n, s, d))
    ps = lambda n, s, d: es.enter_context(nc.psum_tensor(n, s, d))
    sem = lambda n: es.enter_context(nc.semaphore(n))
    with es:
        big = sb("big", [128, BIGN], BF16)
        xb = sb("xb", [128, NTt * D], BF16)
        bias_s = sb("bias_s", [128, NKC * NG], F32)
        id_s = sb("id_s", [128, 128], BF16)
        psl = [sb(f"psl{i}", [128, 512], BF16) for i in range(2)]
        linv = sb("linv", [128, 1], F32)
        ao = [sb(f"ao{i}", [128, 128], BF16) for i in range(2)]
        aotb = [sb(f"aotb{i}", [128, 512], BF16) for i in range(2)]
        vhb = [sb(f"vhb{i}", [128, 16 * 128], BF16) for i in range(2)]
        ssl = [sb(f"ssl{i}", [128, D], BF16) for i in range(2)]
        sqs = sb("sqs", [128, D], BF16)
        acc8 = sb("acc8", [128, 8], F32)
        stat = sb("stat", [128, 4], F32)
        epsc = sb("epsc", [128, 1], F32)
        yo = [sb(f"yo{i}", [128, D], F32) for i in range(2)]
        p_s = [ps(f"p_s{i}", [128, 512], F32) for i in range(2)]
        p_av = [ps(f"p_av{i}", [128, 512], F32) for i in range(4)]
        p_tr = [ps(f"p_tr{i}", [128, 128], BF16) for i in range(2)]
        s_cl = sem("s_cl")
        s_mk = sem("s_mk")
        s_ini = sem("s_ini")
        s_kq = [sem(f"s_kq{i}") for i in range(2)]
        s_vl = [sem(f"s_vl{i}") for i in range(2)]
        s_x = [sem(f"s_x{i}") for i in range(2)]
        s_sim = sem("s_sim")
        s_px = sem("s_px")
        s_msk = sem("s_msk")
        s_av = sem("s_av")
        s_ao = sem("s_ao")
        s_petr = sem("s_petr")
        s_trev = sem("s_trev")
        s_aost = [sem(f"s_aost{i}") for i in range(2)]
        s_wl = sem("s_wl")
        s_vhl = [sem(f"s_vhl{i}") for i in range(2)]
        s_mm = sem("s_mm")
        s_ev = sem("s_ev")
        s_vq = sem("s_vq")
        s_vs = sem("s_vs")
        s_lnx = sem("s_lnx")
        s_rs = sem("s_rs")
        s_no = sem("s_no")
        s_st = [sem(f"s_st{i}") for i in range(2)]
        block = es.enter_context(nc.Block())

        def head_sched():
            out = []
            for g, (ext, crossing) in enumerate(groups):
                for kt in range(ext):
                    out.append((g, kt, kt in crossing, crossing.index(kt) if kt in crossing else -1))
            return out
        SCH = head_sched()
        NSIM = len(SCH)
        NMSK_H = sum(1 for (_g, _kt, c, _i) in SCH if c)

        @block.sync
        def _(sync):
            sync.dma_start(out=id_s[:], in_=idn).then_inc(s_cl, 16)
            sync.wait_ge(s_cl, 16)
            sync.dma_start(out=bias_s[:], in_=biastab).then_inc(s_cl, 16)
            sync.wait_ge(s_cl, 32)
            for h in range(16):
                if h >= 2 and "loadsonly" not in variant:
                    sync.wait_ge(s_av, (h - 1) * 4 * NSIM)
                sync.wait_ge(s_kq[h % 2], 32 * (h // 2))
                if "notrdma" in variant:
                    sync.dma_start(out=big[:, KT0 + (h % 2) * SF: KT0 + (h % 2) * SF + 2048],
                                   in_=kfull[0:128, :]).then_inc(s_kq[h % 2], 16)
                else:
                    sync.dma_start(out=big[:, KT0 + (h % 2) * SF: KT0 + (h % 2) * SF + SF],
                                   in_=kfull[:, h * 128:(h + 1) * 128], transpose=True).then_inc(s_kq[h % 2], 16)
                sync.wait_ge(s_kq[h % 2], 32 * (h // 2) + 16)
                if "notrdma" in variant:
                    sync.dma_start(out=big[:, QT0 + (h % 2) * TT: QT0 + (h % 2) * TT + TT],
                                   in_=q[0:128, 0:TT]).then_inc(s_kq[h % 2], 16)
                else:
                    sync.dma_start(out=big[:, QT0 + (h % 2) * TT: QT0 + (h % 2) * TT + TT],
                                   in_=q[:, h * 128:(h + 1) * 128], transpose=True).then_inc(s_kq[h % 2], 16)
                if h >= 1 and "loadsonly" not in variant:
                    hh = h - 1
                    for g in range(NG):
                        ig = hh * NG + g
                        sync.wait_ge(s_trev, hh * 4 * NG + (g + 1) * 4)
                        sync.wait_ge(s_aost[ig % 2], 16 * (ig // 2))
                        sync.dma_start(out=aot[hh, g * 4:(g + 1) * 4].rearrange("tc p c -> p tc c"),
                                       in_=aotb[ig % 2][:].rearrange("p (tc c) -> p tc c", c=128)
                                       ).then_inc(s_aost[ig % 2], 16)
            hh = 15
            for g in range(NG if "loadsonly" not in variant else 0):
                ig = hh * NG + g
                sync.wait_ge(s_trev, hh * 4 * NG + (g + 1) * 4)
                sync.wait_ge(s_aost[ig % 2], 16 * (ig // 2))
                sync.dma_start(out=aot[hh, g * 4:(g + 1) * 4].rearrange("tc p c -> p tc c"),
                               in_=aotb[ig % 2][:].rearrange("p (tc c) -> p tc c", c=128)
                               ).then_inc(s_aost[ig % 2], 16)
            if "loadsonly" in variant:
                sync.wait_ge(s_kq[0], 32 * 8)
                sync.wait_ge(s_kq[1], 32 * 8)
                if "novload" not in variant:
                    sync.wait_ge(s_vl[0], 16 * 32)
                    sync.wait_ge(s_vl[1], 16 * 32)
            elif "nophasec" in variant:
                sync.wait_ge(s_aost[0], 16 * ((16 * NG + 1) // 2))
                sync.wait_ge(s_aost[1], 16 * (16 * NG // 2))
            else:
                for t in range(NTt):
                    sync.wait_ge(s_no, t + 1)
                    sync.dma_start(out=y[t * 128:(t + 1) * 128, :],
                                   in_=yo[t % 2][:]).then_inc(s_st[t % 2], 16)
                sync.wait_ge(s_st[0], 16 * ((NTt + 1) // 2))
                sync.wait_ge(s_st[1], 16 * (NTt // 2))

        @block.gpsimd
        def _(g_):
            for gg in range(NG):
                g_.wait_ge(s_mk, 16 * gg)
                g_.dma_start(out=big[:, MK0 + gg * NCROSS * 512: MK0 + (gg + 1) * NCROSS * 512]
                             .rearrange("p (a c) -> p a c", c=512),
                             in_=masks[gg].rearrange("s p c -> p s c")).then_inc(s_mk, 16)
            for t in range(NTt):
                if t >= 2:
                    g_.wait_ge(s_x[t % 2], 16 * (t // 2))
                g_.dma_start(out=xb[:, t * D:(t + 1) * D],
                             in_=x[t * 128:(t + 1) * 128, :]).then_inc(s_x[t % 2], 16)
            g_.wait_ge(s_ini, 1)
            for h in range(16 if "novload" not in variant else 0):
                if h >= 2 and "loadsonly" not in variant:
                    g_.wait_ge(s_av, (h - 1) * 4 * NSIM)
                base = V0 + (h % 2) * (NKC * VSL)
                for part in range(4):
                    kc0 = part * (NKC // 4)
                    g_.wait_ge(s_vl[h % 2], 16 * (4 * (h // 2) + part))
                    g_.dma_start(
                        out=big[:, base + kc0 * VSL: base + (kc0 + NKC // 4) * VSL]
                            .rearrange("p (kc c) -> p kc c", c=VSL)[:, :, 0:128],
                        in_=vfull[kc0 * 128:(kc0 + NKC // 4) * 128, h * 128:(h + 1) * 128]
                            .rearrange("(kc p) c -> p kc c", p=128)).then_inc(s_vl[h % 2], 16)
            if "nophasec" in variant:
                return
            g_.wait_ge(s_trev, 16 * 4 * NG)
            for dc in range(16):
                g_.wait_ge(s_wl, 16 * dc)
                g_.dma_start(out=big[:, WOUT0 + dc * D: WOUT0 + (dc + 1) * D],
                             in_=wout[dc]).then_inc(s_wl, 16)
            g_.wait_ge(s_aost[0], 16 * ((16 * NG + 1) // 2))
            g_.wait_ge(s_aost[1], 16 * (16 * NG // 2))
            for t in range(NTt):
                if t >= 2:
                    g_.wait_ge(s_mm, (t - 1) * NOS)
                    g_.wait_ge(s_vhl[t % 2], 16 * (t // 2))
                g_.dma_start(out=vhb[t % 2][:].rearrange("p (dc c) -> p dc c", c=128),
                             in_=aot[:, t].rearrange("dc p c -> p dc c")).then_inc(s_vhl[t % 2], 16)

        @block.tensor
        def _(te):
            te.wait_ge(s_cl, 32)
            if "loadsonly" in variant:
                te.matmul(p_s[0][:, 0:128], id_s[:], id_s[:], start=True, stop=True)
                return
            nsim = 0; nav = 0; nmsk = 0; npx = 0; nao = 0; ntr = 0
            for h in range(16):
                te.wait_ge(s_kq[h % 2], 32 * (h // 2) + 32)
                te.wait_ge(s_vl[h % 2], 16 * (4 * (h // 2) + 4))
                kto = KT0 + (h % 2) * SF
                qto = QT0 + (h % 2) * TT
                vbase = V0 + (h % 2) * (NKC * VSL)
                for g, (ext, crossing) in enumerate(groups):
                    for kt in range(ext):
                        iscross = kt in crossing
                        if nsim >= 2:
                            te.wait_ge(s_px, nsim - 1)
                        te.matmul(p_s[nsim % 2][:], big[:, kto + kt * 128: kto + (kt + 1) * 128],
                                  big[:, qto + g * 512: qto + (g + 1) * 512],
                                  start=True, stop=True).then_inc(s_sim, 1)
                        nsim += 1
                        if iscross:
                            nmsk += 1
                            te.wait_ge(s_msk, nmsk)
                        else:
                            te.wait_ge(s_px, nsim)
                        if kt == 0:
                            te.wait_ge(s_ao, (h * NG + g) * 8)
                        for j in range(4):
                            mi = te.matmul(p_av[j][:, 0:132],
                                           psl[(nsim - 1) % 2][:, j * 128:(j + 1) * 128],
                                           big[:, vbase + kt * VSL: vbase + kt * VSL + 132]
                                           ,
                                           start=(kt == 0), stop=(kt == ext - 1))
                            mi.then_inc(s_av, 1)
                            nav += 1
                    for j in range(4):
                        te.wait_ge(s_ao, (h * NG + g) * 8 + 2 * (j + 1))
                        if ntr >= 2:
                            te.wait_ge(s_trev, ntr - 1)
                        te.transpose(p_tr[ntr % 2][:], ao[j % 2][:], id_s[:]).then_inc(s_petr, 1)
                        ntr += 1
            if "nophasec" in variant:
                return
            te.wait_ge(s_wl, 16 * 16)
            for t in range(NTt):
                te.wait_ge(s_vhl[t % 2], 16 * (t // 2 + 1))
                for o in range(NOS):
                    gidx = t * NOS + o
                    if gidx >= 2:
                        te.wait_ge(s_ev, gidx - 1)
                    if gidx == 0:
                        te.wait_ge(s_ao, 16 * NG * 8)
                    for dc in range(16):
                        mi = te.matmul(p_av[gidx % 2][:],
                                       vhb[t % 2][:, dc * 128:(dc + 1) * 128],
                                       big[:, WOUT0 + dc * D + o * OS: WOUT0 + dc * D + o * OS + OS],
                                       start=(dc == 0), stop=(dc == 15))
                    mi.then_inc(s_mm, 1)

        @block.scalar
        def _(sc):
            if "loadsonly" in variant:
                sc.wait_ge(s_cl, 32)
                sc.copy(bias_s[:, 0:1], bias_s[:, 0:1])
                return
            nsim = 0; ntrev = 0
            for h in range(16):
                for g, (ext, crossing) in enumerate(groups):
                    for kt in range(ext):
                        sc.wait_ge(s_sim, nsim + 1)
                        if nsim >= 2:
                            sc.wait_ge(s_av, (nsim - 1) * 4)
                        sc.activation(psl[nsim % 2][:], p_s[nsim % 2][:], AF.Exp,
                                      bias=bias_s[:, kt * NG + g: kt * NG + g + 1]).then_inc(s_px, 1)
                        nsim += 1
                for g in range(NG):
                    ig = h * NG + g
                    if ig >= 2:
                        sc.wait_ge(s_aost[ig % 2], 16 * (ig // 2))
                    for j in range(4):
                        sc.wait_ge(s_petr, ntrev + 1)
                        sc.copy(aotb[ig % 2][:, j * 128:(j + 1) * 128],
                                p_tr[ntrev % 2][:]).then_inc(s_trev, 1)
                        ntrev += 1
            for t in range(NTt if "nophasec" not in variant else 0):
                sc.wait_ge(s_vs, 5 * (t + 1))
                sc.activation(stat[:, 2:3], stat[:, 2:3], AF.Ln, bias=epsc[:]).then_inc(s_lnx, 1)
                sc.wait_ge(s_lnx, t + 1)
                sc.activation(stat[:, 3:4], stat[:, 2:3], AF.Exp, scale=-0.5).then_inc(s_rs, 1)

        @block.vector
        def _(v):
            v.wait_ge(s_cl, 32)
            v.wait_ge(s_mk, 16 * NG)
            v.memset(epsc[:], EPS)
            v.memset(big[:, V0:V0 + 2 * NKC * VSL], 1.0).then_inc(s_ini, 1)
            if "loadsonly" in variant:
                return
            nsim = 0; nmsk = 0
            for h in range(16):
                for g, (ext, crossing) in enumerate(groups):
                    for kt in range(ext):
                        if kt in crossing:
                            v.wait_ge(s_px, nsim + 1)
                            si = crossing.index(kt)
                            v.tensor_tensor(psl[nsim % 2][:], psl[nsim % 2][:],
                                            big[:, MK0 + (g * NCROSS + si) * 512:
                                                MK0 + (g * NCROSS + si) * 512 + 512],
                                            ALU.mult).then_inc(s_msk, 1)
                            nmsk += 1
                        nsim += 1
                    for j in range(4):
                        v.wait_ge(s_av, (sum(groups[gg][0] for gg in range(g + 1)) + h * NSIM) * 4)
                        if (h * NG + g) * 8 + 2 * j >= 2:
                            v.wait_ge(s_ao, (h * NG + g) * 8 + 2 * j)
                        v.reciprocal(linv[:], p_av[j][:, 128:129]).then_inc(s_ao, 1)
                        v.wait_ge(s_ao, (h * NG + g) * 8 + 2 * j + 1)
                        v.wait_ge(s_petr, max(0, (h * NG + g) * 4 + j - 1))
                        v.tensor_scalar(ao[j % 2][:], p_av[j][:, 0:128], linv[:], None,
                                        ALU.mult).then_inc(s_ao, 1)
            if "nophasec" in variant:
                return
            for t in range(NTt):
                if t >= 1:
                    v.wait_ge(s_rs, t)
                    if t - 1 >= 2:
                        v.wait_ge(s_st[(t - 1) % 2], 16 * ((t - 1) // 2))
                    v.tensor_scalar(yo[(t - 1) % 2][:], ssl[(t - 1) % 2][:],
                                    stat[:, 0:1], stat[:, 3:4],
                                    ALU.subtract, ALU.mult).then_inc(s_no, 1)
                if t >= 2:
                    v.wait_ge(s_no, t - 1)
                if t >= 1:
                    v.wait_ge(s_vs, 5 * t)
                for o in range(NOS):
                    gidx = t * NOS + o
                    v.wait_ge(s_mm, gidx + 1)
                    v.wait_ge(s_x[t % 2], 16 * (t // 2 + 1))
                    v.scalar_tensor_tensor(
                        ssl[t % 2][:, o * OS:(o + 1) * OS], p_av[gidx % 2][:], 1.0,
                        xb[:, t * D + o * OS: t * D + (o + 1) * OS],
                        ALU.mult, ALU.add, accum_out=acc8[:, o:o + 1]).then_inc(s_ev, 1)
                v.wait_ge(s_ev, 4 * (t + 1))
                if t >= 1:
                    v.wait_ge(s_vq, 4 * t)
                for o in range(NOS):
                    v.scalar_tensor_tensor(
                        sqs[:, o * OS:(o + 1) * OS], ssl[t % 2][:, o * OS:(o + 1) * OS], 1.0,
                        ssl[t % 2][:, o * OS:(o + 1) * OS],
                        ALU.mult, ALU.mult, accum_out=acc8[:, 4 + o:5 + o]).then_inc(s_vq, 1)
                v.tensor_reduce(stat[:, 0:1], acc8[:, 0:4], AX.X, ALU.add).then_inc(s_vs, 1)
                v.wait_ge(s_vq, 4 * (t + 1))
                v.tensor_reduce(stat[:, 2:3], acc8[:, 4:8], AX.X, ALU.add).then_inc(s_vs, 1)
                v.wait_ge(s_vs, 5 * t + 1)
                v.tensor_scalar(stat[:, 0:1], stat[:, 0:1], 1.0 / D, None, ALU.mult).then_inc(s_vs, 1)
                v.wait_ge(s_vs, 5 * t + 3)
                v.tensor_tensor(stat[:, 1:2], stat[:, 0:1], stat[:, 0:1], ALU.mult).then_inc(s_vs, 1)
                v.wait_ge(s_vs, 5 * t + 4)
                v.scalar_tensor_tensor(stat[:, 2:3], stat[:, 2:3], 1.0 / D, stat[:, 1:2],
                                       ALU.mult, ALU.subtract).then_inc(s_vs, 1)
            t = NTt - 1
            v.wait_ge(s_rs, NTt)
            v.wait_ge(s_st[t % 2], 16 * (t // 2))
            v.tensor_scalar(yo[t % 2][:], ssl[t % 2][:], stat[:, 0:1], stat[:, 3:4],
                            ALU.subtract, ALU.mult).then_inc(s_no, 1)
    return nc

def build_l1batt(TT=T, variant=''):
    NTt = TT // 128
    SF = 2 * TT
    NKC = SF // 128
    NG = NTt // 4
    groups = l1b_sched(TT)
    NCROSS = len(groups[0][1])
    OS = 512
    NOS = D // OS
    EXPB = -20.0

    nc = bass.Bass("TRN2", target_bir_lowering=False)
    q = nc.dram_tensor("q", [TT, D], BF16, kind="ExternalInput").ap()
    kfull = nc.dram_tensor("kfull", [SF, D], BF16, kind="ExternalInput").ap()
    vfull = nc.dram_tensor("vfull", [SF, D], BF16, kind="ExternalInput").ap()
    masks = nc.dram_tensor("masks", [NG, NCROSS, 128, 512], BF16, kind="ExternalInput").ap()
    biastab = nc.dram_tensor("biastab", [128, NKC * NG], F32, kind="ExternalInput").ap()
    idn = nc.dram_tensor("idn", [128, 128], BF16, kind="ExternalInput").ap()
    aot = nc.dram_tensor("aot", [16, NTt, 128, 128], BF16, kind="ExternalOutput").ap()

    VSL = 132

    es = ExitStack()
    sb = lambda n, s, d: es.enter_context(nc.sbuf_tensor(n, s, d))
    ps = lambda n, s, d: es.enter_context(nc.psum_tensor(n, s, d))
    sem = lambda n: es.enter_context(nc.semaphore(n))
    with es:
        ktb = [sb(f"ktb{i}", [128, SF], BF16) for i in range(2)]
        qtb = [sb(f"qtb{i}", [128, TT], BF16) for i in range(2)]
        vsb = [sb(f"vsb{i}", [128, NKC * VSL], BF16) for i in range(2)]
        mkb = sb("mkb", [128, NG * NCROSS * 512], BF16)
        bias_s = sb("bias_s", [128, NKC * NG], F32)
        id_s = sb("id_s", [128, 128], BF16)
        pslg = sb("pslg", [128, 32 * 512], BF16)
        linv = sb("linv", [128, 1], F32)
        ao = [sb(f"ao{i}", [128, 128], BF16) for i in range(2)]
        aotb = [sb(f"aotb{i}", [128, 512], BF16) for i in range(2)]
        p_s = [ps(f"p_s{i}", [128, 512], F32) for i in range(2)]
        p_av = [ps(f"p_av{i}", [128, 512], F32) for i in range(4)]
        p_tr = [ps(f"p_tr{i}", [128, 128], BF16) for i in range(2)]
        s_cl = sem("s_cl")
        s_mk = sem("s_mk")
        s_ini = sem("s_ini")
        s_kq = [sem(f"s_kq{i}") for i in range(2)]
        s_vl = [sem(f"s_vl{i}") for i in range(2)]
        s_x = [sem(f"s_x{i}") for i in range(2)]
        s_sim = sem("s_sim")
        s_px = sem("s_px")
        s_msk = sem("s_msk")
        s_av = sem("s_av")
        s_ao = sem("s_ao")
        s_petr = sem("s_petr")
        s_trev = sem("s_trev")
        s_aost = [sem(f"s_aost{i}") for i in range(2)]
        s_wl = sem("s_wl")
        s_vhl = [sem(f"s_vhl{i}") for i in range(2)]
        s_mm = sem("s_mm")
        s_ev = sem("s_ev")
        s_vq = sem("s_vq")
        s_vs = sem("s_vs")
        s_lnx = sem("s_lnx")
        s_rs = sem("s_rs")
        s_no = sem("s_no")
        s_st = [sem(f"s_st{i}") for i in range(2)]
        block = es.enter_context(nc.Block())

        def head_sched():
            out = []
            for g, (ext, crossing) in enumerate(groups):
                for kt in range(ext):
                    out.append((g, kt, kt in crossing, crossing.index(kt) if kt in crossing else -1))
            return out
        SCH = head_sched()
        NSIM = len(SCH)
        NMSK_H = sum(1 for (_g, _kt, c, _i) in SCH if c)

        @block.sync
        def _(sync):
            sync.dma_start(out=id_s[:], in_=idn).then_inc(s_cl, 16)
            sync.wait_ge(s_cl, 16)
            sync.dma_start(out=bias_s[:], in_=biastab).then_inc(s_cl, 16)
            sync.wait_ge(s_cl, 32)
            for h in range(16):
                if h >= 2 and "loadsonly" not in variant:
                    sync.wait_ge(s_av, (h - 1) * 4 * NG)
                sync.wait_ge(s_kq[h % 2], 32 * (h // 2))
                if "notrdma" in variant:
                    sync.dma_start(out=ktb[h % 2][:, 0:2048],
                                   in_=kfull[0:128, :]).then_inc(s_kq[h % 2], 16)
                else:
                    sync.dma_start(out=ktb[h % 2][:],
                                   in_=kfull[:, h * 128:(h + 1) * 128], transpose=True).then_inc(s_kq[h % 2], 16)
                sync.wait_ge(s_kq[h % 2], 32 * (h // 2) + 16)
                if "notrdma" in variant:
                    sync.dma_start(out=qtb[h % 2][:],
                                   in_=q[0:128, 0:TT]).then_inc(s_kq[h % 2], 16)
                else:
                    sync.dma_start(out=qtb[h % 2][:],
                                   in_=q[:, h * 128:(h + 1) * 128], transpose=True).then_inc(s_kq[h % 2], 16)
                if h >= 1 and "loadsonly" not in variant:
                    hh = h - 1
                    for g in range(NG):
                        ig = hh * NG + g
                        sync.wait_ge(s_trev, hh * 4 * NG + (g + 1) * 4)
                        sync.wait_ge(s_aost[ig % 2], 16 * (ig // 2))
                        sync.dma_start(out=aot[hh, g * 4:(g + 1) * 4].rearrange("tc p c -> p tc c"),
                                       in_=aotb[ig % 2][:].rearrange("p (tc c) -> p tc c", c=128)
                                       ).then_inc(s_aost[ig % 2], 16)
            hh = 15
            for g in range(NG if "loadsonly" not in variant else 0):
                ig = hh * NG + g
                sync.wait_ge(s_trev, hh * 4 * NG + (g + 1) * 4)
                sync.wait_ge(s_aost[ig % 2], 16 * (ig // 2))
                sync.dma_start(out=aot[hh, g * 4:(g + 1) * 4].rearrange("tc p c -> p tc c"),
                               in_=aotb[ig % 2][:].rearrange("p (tc c) -> p tc c", c=128)
                               ).then_inc(s_aost[ig % 2], 16)
            sync.wait_ge(s_aost[0], 16 * ((16 * NG + 1) // 2))
            sync.wait_ge(s_aost[1], 16 * (16 * NG // 2))

        @block.gpsimd
        def _(g_):
            for gg in range(NG):
                g_.wait_ge(s_mk, 16 * gg)
                g_.dma_start(out=mkb[:, gg * NCROSS * 512:(gg + 1) * NCROSS * 512].rearrange("p (a c) -> p a c", c=512),
                             in_=masks[gg].rearrange("s p c -> p s c")).then_inc(s_mk, 16)
            g_.wait_ge(s_ini, 1)
            for h in range(16 if "novload" not in variant else 0):
                if h >= 2 and "loadsonly" not in variant:
                    g_.wait_ge(s_av, (h - 1) * 4 * NG)
                for part in range(4):
                    kc0 = part * (NKC // 4)
                    g_.wait_ge(s_vl[h % 2], 16 * (4 * (h // 2) + part))
                    g_.dma_start(
                        out=vsb[h % 2][:, kc0 * VSL:(kc0 + NKC // 4) * VSL]
                            .rearrange("p (kc c) -> p kc c", c=VSL)[:, :, 0:128],
                        in_=vfull[kc0 * 128:(kc0 + NKC // 4) * 128, h * 128:(h + 1) * 128]
                            .rearrange("(kc p) c -> p kc c", p=128)).then_inc(s_vl[h % 2], 16)
        @block.tensor
        def _(te):
            te.wait_ge(s_cl, 32)
            if "loadsonly" in variant:
                te.matmul(p_s[0][:, 0:128], id_s[:], id_s[:], start=True, stop=True)
                return
            nsim = 0; nav = 0; nmsk = 0; npx = 0; nao = 0; ntr = 0
            for h in range(16):
                te.wait_ge(s_kq[h % 2], 32 * (h // 2) + 32)
                te.wait_ge(s_vl[h % 2], 16 * (4 * (h // 2) + 4))
                for g, (ext, crossing) in enumerate(groups):
                    for kt in range(ext):
                        if nsim >= 2:
                            te.wait_ge(s_px, nsim - 1)
                        te.matmul(p_s[nsim % 2][:], ktb[h % 2][:, kt * 128:(kt + 1) * 128],
                                  qtb[h % 2][:, g * 512:(g + 1) * 512],
                                  start=True, stop=True).then_inc(s_sim, 1)
                        nsim += 1
                        if kt in crossing:
                            nmsk += 1
                    te.wait_ge(s_px, nsim)
                    te.wait_ge(s_msk, nmsk)
                    te.wait_ge(s_ao, (h * NG + g) * 8)
                    for jj in range(4):
                        for kt in range(ext):
                            mi = te.matmul(p_av[jj][:, 0:132],
                                           pslg[:, kt * 512 + jj * 128: kt * 512 + jj * 128 + 128],
                                           vsb[h % 2][:, kt * VSL: kt * VSL + 132],
                                           start=(kt == 0), stop=(kt == ext - 1))
                        mi.then_inc(s_av, 1)
                        nav += 1
                    for jj in range(4):
                        te.wait_ge(s_ao, (h * NG + g) * 8 + 2 * (jj + 1))
                        if ntr >= 2:
                            te.wait_ge(s_trev, ntr - 1)
                        te.transpose(p_tr[ntr % 2][:], ao[jj % 2][:], id_s[:]).then_inc(s_petr, 1)
                        ntr += 1
        @block.scalar
        def _(sc):
            if "loadsonly" in variant:
                sc.wait_ge(s_cl, 32)
                sc.copy(bias_s[:, 0:1], bias_s[:, 0:1])
                return
            nsim = 0; ntrev = 0; nav = 0
            for h in range(16):
                for g, (ext, crossing) in enumerate(groups):
                    sc.wait_ge(s_av, 4 * (h * NG + g))
                    for kt in range(ext):
                        sc.wait_ge(s_sim, nsim + 1)
                        sc.activation(pslg[:, kt * 512:(kt + 1) * 512], p_s[nsim % 2][:], AF.Exp,
                                      bias=bias_s[:, kt * NG + g: kt * NG + g + 1],
                                      scale=float(HD ** -0.5)).then_inc(s_px, 1)
                        nsim += 1
                    ig = h * NG + g
                    if ig >= 2:
                        sc.wait_ge(s_aost[ig % 2], 16 * (ig // 2))
                    for jj in range(4):
                        sc.wait_ge(s_petr, ntrev + 1)
                        sc.copy(aotb[ig % 2][:, jj * 128:(jj + 1) * 128],
                                p_tr[ntrev % 2][:]).then_inc(s_trev, 1)
                        ntrev += 1
        @block.vector
        def _(v):
            v.wait_ge(s_cl, 32)
            v.wait_ge(s_mk, 16 * NG)
            v.memset(vsb[0][:], 1.0)
            v.memset(vsb[1][:], 1.0).then_inc(s_ini, 1)
            if "loadsonly" in variant:
                return
            nsim = 0
            for h in range(16):
                for g, (ext, crossing) in enumerate(groups):
                    for kt in range(ext):
                        if kt in crossing:
                            v.wait_ge(s_px, nsim + 1)
                            si = crossing.index(kt)
                            v.tensor_tensor(pslg[:, kt * 512:(kt + 1) * 512],
                                            pslg[:, kt * 512:(kt + 1) * 512],
                                            mkb[:, (g * NCROSS + si) * 512:(g * NCROSS + si + 1) * 512],
                                            ALU.mult).then_inc(s_msk, 1)
                        nsim += 1
                    v.wait_ge(s_av, 4 * (h * NG + g) + 4)
                    for jj in range(4):
                        base = (h * NG + g) * 8 + 2 * jj
                        if base >= 2:
                            v.wait_ge(s_ao, base)
                        v.reciprocal(linv[:], p_av[jj][:, 128:129]).then_inc(s_ao, 1)
                        v.wait_ge(s_ao, base + 1)
                        v.wait_ge(s_petr, max(0, (h * NG + g) * 4 + jj - 1))
                        v.tensor_scalar(ao[jj % 2][:], p_av[jj][:, 0:128], linv[:], None,
                                        ALU.mult).then_inc(s_ao, 1)
    return nc


def _host_consts():
    j = np.arange(128)
    ang = 2 * np.pi * np.outer(j, j) / 128
    C = np.cos(ang).astype(np.float32)
    Sm = np.sin(ang).astype(np.float32)
    return {
        "csb": np.ascontiguousarray(np.concatenate([C, -Sm], axis=1)).astype(NPBF),
        "ub": np.triu(np.ones((128, 128), np.float32)).astype(NPBF),
        "c2b": (C / 128.0).astype(NPBF),
        "s2nb": (-Sm / 128.0).astype(NPBF),
        "onescol": np.ones((128, 1), np.float32).astype(NPBF),
        "onesr": ONESROW_BF,
        "idn": IDN_BF,
        "iota8": np.tile(np.arange(8, dtype=np.float32), (128, 1)),
        "iotabig": np.tile(np.arange(8, dtype=np.float32) + 1000.0, (128, 1)),
        "uf32": np.triu(np.ones((128, 128), np.float32)),
        "onescf": np.ones((128, 1), np.float32),
        "onesrf": np.ones((1, 128), np.float32),
    }

def _core_flags(c):
    p = c % 2
    flags = np.array([[1.0 if p == 1 else 0.0, 1.0 if p == 0 else 0.0]], np.float32)
    pcol = np.zeros((NCORE, 1), np.float32)
    pcol[c ^ 1, 0] = 1.0
    return flags, pcol

def _rot_tables(c):
    ROT = 32
    pos = global_positions(c).astype(np.float32)
    inv = 1.0 / 10000.0 ** (np.arange(0, ROT, 2, dtype=np.float32) / ROT)
    fr = pos[:, None] * inv[None, :]
    posf = np.concatenate([fr, fr], -1)
    return (np.ascontiguousarray(np.tile(np.cos(posf), (1, 16)).astype(np.float32)),
            np.ascontiguousarray(np.tile(np.sin(posf), (1, 16)).astype(np.float32)))

def _l1b_tables(c):
    NTt = T // 128
    SF = 2 * T
    NKC = SF // 128
    NG = NTt // 4
    groups = l1b_sched(T)
    NCROSS = len(groups[0][1])
    H = NTt // 2
    p = c % 2
    gq = [(j if j < H else j + NTt) if p == 0 else j + H for j in range(NTt)]
    qpos = np.concatenate([np.arange(g * 128, (g + 1) * 128) for g in gq])
    EXPB = -20.0
    biastab = np.zeros((128, NKC * NG), np.float32)
    for g in range(NG):
        etrue = max(gq[4 * g + jj] for jj in range(4)) + 1
        for kt in range(NKC):
            biastab[:, kt * NG + g] = EXPB + (-30000.0 if kt >= etrue else 0.0)
    masks = np.ones((NG, NCROSS, 128, 512), np.float32)
    for g in range(NG):
        ext, crossing = groups[g]
        for si, kt in enumerate(crossing):
            kpos = kt * 128 + np.arange(128)
            qp = qpos[g * 512:(g + 1) * 512]
            masks[g, si] = (kpos[:, None] <= qp[None, :]).astype(np.float32)
    return masks.astype(NPBF), biastab

_NC_CACHE = {}

def _get_nc(name, builder):
    if name not in _NC_CACHE:
        _NC_CACHE[name] = builder()
    return _NC_CACHE[name]

def kernel(L0_attn_x, L0_attn_W_q, L0_attn_W_kv, L0_attn_W_o,
           L0_ffn_gate_weight, L0_ffn_w1, L0_ffn_w2,
           L1_attn_w_qkv, L1_attn_w_out, L1_attn_g,
           L1_ffn_factors, L1_ffn_bias):
    cst = _host_consts()
    f32 = lambda a: np.ascontiguousarray(np.asarray(a, np.float32))
    x0s = shard_rows(f32(L0_attn_x))

    wq_t = np.ascontiguousarray(f32(L0_attn_W_q).T.reshape(16, 128, 16, 128).transpose(2, 0, 1, 3))
    wkv_t = np.ascontiguousarray(f32(L0_attn_W_kv).T.reshape(16, 128, 16, 128).transpose(2, 0, 1, 3))
    wo_t = np.ascontiguousarray(f32(L0_attn_W_o).T.reshape(16, 128, D))
    gt_t = np.ascontiguousarray(f32(L0_ffn_gate_weight).T.reshape(16, 128, 8))
    w1_t = np.ascontiguousarray(f32(L0_ffn_w1).reshape(8, 16, 128, 16, 128).transpose(0, 3, 1, 2, 4))
    w2_t = np.ascontiguousarray(f32(L0_ffn_w2).reshape(8, 16, 128, 4, 512).transpose(0, 3, 1, 2, 4))
    wqkv_t = np.ascontiguousarray(f32(L1_attn_w_qkv).T.reshape(16, 128, 12, 512).transpose(2, 0, 1, 3))
    wout_t = np.ascontiguousarray(f32(L1_attn_w_out).T.reshape(16, 128, D))
    fac = f32(L1_ffn_factors)
    bias1 = f32(L1_ffn_bias).reshape(1, D)

    nc = _get_nc("l0a", build_l0a)
    res = _run(nc, [{"x": x0s[c], "wq": wq_t, "wkv": wkv_t, "idn": cst["idn"]}
                    for c in range(NCORE)])
    qchs = [r["qch"] for r in res.results]
    kvchs = [r["kvch"] for r in res.results]

    nc = _get_nc("l0b", build_l0b)
    in_maps = []
    for c in range(NCORE):
        flags, pcol = _core_flags(c)
        in_maps.append({"qch": qchs[c], "kvch": kvchs[c], "csb": cst["csb"], "ub": cst["ub"],
                        "c2b": cst["c2b"], "s2nb": cst["s2nb"], "onescol": cst["onescol"],
                        "onesr": cst["onesr"], "idn": cst["idn"], "flags": flags, "pcolf": pcol})
    res = _run(nc, in_maps)
    vhts = [r["vht"] for r in res.results]

    nc = _get_nc("l0c", build_l0c)
    res = _run(nc, [{"x": x0s[c], "vht": vhts[c], "wo": wo_t} for c in range(NCORE)])
    x1s = [r["y"] for r in res.results]

    nc = _get_nc("moe", build_moe)
    in_maps = []
    for c in range(NCORE):
        flags, pcol = _core_flags(c)
        in_maps.append({"x": x1s[c], "gt": gt_t, "w1t": w1_t, "w2t": w2_t,
                        "iota8": cst["iota8"], "iotabig": cst["iotabig"], "uf32": cst["uf32"],
                        "onescf": cst["onescf"], "onesrf": cst["onesrf"], "idn": cst["idn"],
                        "flags": flags, "pcolf": pcol})
    res = _run(nc, in_maps)
    x2s = [r["y"] for r in res.results]

    nc = _get_nc("l1a", build_l1a)
    in_maps = []
    for c in range(NCORE):
        cosr, sinr = _rot_tables(c)
        in_maps.append({"x": x2s[c], "wqkv": wqkv_t, "cosr": cosr, "sinr": sinr, "idn": cst["idn"]})
    res = _run(nc, in_maps)
    qkvts = [np.asarray(r["qkvt"]) for r in res.results]

    NTt = T // 128
    def split_qkv(a):
        a = a.reshape(12, NTt, 128, 512)
        full = a.transpose(1, 2, 0, 3).reshape(T, 12 * 512)
        return (np.ascontiguousarray(full[:, 0:2048]),
                np.ascontiguousarray(full[:, 2048:4096]),
                np.ascontiguousarray(full[:, 4096:6144]))
    qs, ks, vs = zip(*[split_qkv(a) for a in qkvts])
    kfulls, vfulls = [], []
    for b in range(B):
        cA, cB = 2 * b, 2 * b + 1
        kf = np.concatenate([ks[cA][0:1024], ks[cB], ks[cA][1024:2048]], axis=0)
        vf = np.concatenate([vs[cA][0:1024], vs[cB], vs[cA][1024:2048]], axis=0)
        kfulls.append(np.ascontiguousarray(kf))
        vfulls.append(np.ascontiguousarray(vf))

    nc = _get_nc("l1batt", build_l1batt)
    in_maps = []
    for c in range(NCORE):
        masks, biastab = _l1b_tables(c)
        in_maps.append({"q": qs[c], "kfull": kfulls[c // 2], "vfull": vfulls[c // 2],
                        "masks": masks, "biastab": biastab, "idn": cst["idn"]})
    res = _run(nc, in_maps)
    aots = [r["aot"] for r in res.results]
    nc = _get_nc("l0c", build_l0c)
    res = _run(nc, [{"x": x2s[c], "vht": aots[c], "wo": wout_t} for c in range(NCORE)])
    x3s = [r["y"] for r in res.results]

    nc = _get_nc("l1c", build_l1c)
    res = _run(nc, [{"x": x3s[c], "fac": fac, "bias": bias1, "idn": cst["idn"],
                     "onesr": cst["onesr"]} for c in range(NCORE)])
    ys = [np.asarray(r["y"], np.float32) for r in res.results]
    return unshard_rows(ys, dtype=np.float32)



# version 03 (final)
# speedup vs baseline: 1.1219x; 1.1219x over previous
import numpy as np
import ml_dtypes
import concourse.bass as bass
import concourse.mybir as mybir
from concourse.bass_utils import run_bass_kernel_spmd
from contextlib import ExitStack

F32 = mybir.dt.float32
BF16 = mybir.dt.bfloat16
U32 = mybir.dt.uint32
AF = mybir.ActivationFunctionType
ALU = mybir.AluOpType
AX = mybir.AxisListType
NPBF = ml_dtypes.bfloat16

B, S, D = 4, 4096, 2048
T = 2048
NT = T // 128
NH, HD = 16, 128
EPS = 1e-5
NCORE = 8
RSQRT8 = float(1.0 / np.sqrt(8.0))

def core_token_ranges(c):
    return [(0, 1024), (3072, 4096)] if c % 2 == 0 else [(1024, 3072)]

def shard_rows(x_bsd):
    out = []
    for c in range(NCORE):
        b = c // 2
        segs = [x_bsd[b, a:e] for (a, e) in core_token_ranges(c)]
        out.append(np.ascontiguousarray(np.concatenate(segs, axis=0)))
    return out

def unshard_rows(per_core, dtype=None):
    tail = per_core[0].shape[1:]
    out = np.empty((B, S) + tuple(tail), dtype=dtype or per_core[0].dtype)
    for c in range(NCORE):
        b = c // 2
        ofs = 0
        for (a, e) in core_token_ranges(c):
            out[b, a:e] = per_core[c][ofs:ofs + (e - a)]
            ofs += e - a
    return out


class Ctr:
    def __init__(self):
        self.v = 0
    def inc(self, n=1):
        self.v += n
        return self.v

def global_positions(c):
    return np.concatenate([np.arange(a, e) for (a, e) in core_token_ranges(c)])

IDN_BF = np.eye(128, dtype=np.float32).astype(NPBF)
ONESROW_BF = np.ones((1, 128), dtype=np.float32).astype(NPBF)

def build_l1c(TT=T):
    NT = TT // 128
    nc = bass.Bass("TRN2", target_bir_lowering=False)
    x = nc.dram_tensor("x", [TT, D], F32, kind="ExternalInput").ap()
    fac = nc.dram_tensor("fac", [8, D, D], F32, kind="ExternalInput").ap()
    bias = nc.dram_tensor("bias", [1, D], F32, kind="ExternalInput").ap()
    idn = nc.dram_tensor("idn", [128, 128], BF16, kind="ExternalInput").ap()
    onesr = nc.dram_tensor("onesr", [1, 128], BF16, kind="ExternalInput").ap()
    y = nc.dram_tensor("y", [TT, D], F32, kind="ExternalOutput").ap()

    OS = 512
    NOS = D // OS

    es = ExitStack()
    sb = lambda n, s, d: es.enter_context(nc.sbuf_tensor(n, s, d))
    ps = lambda n, s, d: es.enter_context(nc.psum_tensor(n, s, d))
    sem = lambda n: es.enter_context(nc.semaphore(n))
    with es:
        xb = sb("xb", [128, NT * D], BF16)
        idb = sb("idb", [128, 128], BF16)
        onesb = sb("onesb", [1, 128], BF16)
        biasb = sb("biasb", [1, D], BF16)
        facc = sb("facc", [128, 16 * OS], F32)
        fbw = sb("fbw", [128, 16 * D], BF16)
        xtr = [sb(f"xtr{i}", [128, 16 * 128], BF16) for i in range(2)]
        ssl = [sb(f"ssl{i}", [128, D], BF16) for i in range(2)]
        sqs = sb("sqs", [128, D], BF16)
        acc8 = sb("acc8", [128, 8], F32)
        stat = sb("stat", [128, 4], F32)
        epsc = sb("epsc", [128, 1], F32)
        yo = [sb(f"yo{i}", [128, D], F32) for i in range(2)]
        p_tr = [ps(f"p_tr{i}", [128, 128], BF16) for i in range(2)]
        p_mm = [ps(f"p_mm{i}", [128, OS], F32) for i in range(2)]

        s_cl = sem("s_cl")
        s_cl2 = sem("s_cl2")
        s_bb = sem("s_bb")
        s_x = [sem(f"s_x{i}") for i in range(2)]
        s_fq = [sem(f"s_fq{i}") for i in range(4)]
        s_fb = sem("s_fb")
        s_petr = sem("s_petr")
        s_trev = sem("s_trev")
        s_mm = sem("s_mm")
        s_ev = sem("s_ev")
        s_vq = sem("s_vq")
        s_lnx = sem("s_lnx")
        s_vs = sem("s_vs")
        s_rs = sem("s_rs")
        s_no = sem("s_no")
        s_st = [sem(f"s_st{i}") for i in range(2)]
        block = es.enter_context(nc.Block())

        def emit_fchain4(g):
            dst = facc[:].rearrange("p (dc o) -> p dc o", o=OS)
            for o in range(NOS):
                if o >= 1:
                    g.wait_ge(s_fb, o)
                src = fac[:, :, o * OS:(o + 1) * OS].rearrange("r (dc p) o -> r p dc o", p=128)
                for r in range(8):
                    n = 8 * o + r
                    for q in range(4):
                        if n > 0:
                            g.wait_ge(s_fq[q], 16 * n)
                        g.dma_start(out=dst[:, 4 * q:4 * (q + 1), :],
                                    in_=src[r][:, 4 * q:4 * (q + 1), :],
                                    accum_op=(ALU.bypass if r == 0 else ALU.add)
                                    ).then_inc(s_fq[q], 16)

        @block.sync
        def _(sync):
            sync.dma_start(out=idb[:], in_=idn).then_inc(s_cl, 16)
            sync.dma_start(out=onesb[:], in_=onesr).then_inc(s_cl2, 16)

            for t in range(NT):
                sync.wait_ge(s_no, t + 1)
                sync.dma_start(out=y[t * 128:(t + 1) * 128, :],
                               in_=yo[t % 2][:]).then_inc(s_st[t % 2], 16)
            sync.wait_ge(s_st[0], 16 * ((NT + 1) // 2))
            sync.wait_ge(s_st[1], 16 * (NT // 2))

        @block.gpsimd
        def _(g):
            g.dma_start(out=biasb[:], in_=bias).then_inc(s_bb, 16)
            for t in range(NT):
                if t >= 2:
                    g.wait_ge(s_x[t % 2], 16 * (t // 2))
                g.dma_start(out=xb[:, t * D:(t + 1) * D],
                            in_=x[t * 128:(t + 1) * 128, :]).then_inc(s_x[t % 2], 16)
            emit_fchain4(g)

        @block.vector
        def _(v):
            v.wait_ge(s_cl2, 16)
            v.memset(epsc[:], EPS)
            for o in range(NOS):
                for q in range(4):
                    v.wait_ge(s_fq[q], 16 * 8 * (o + 1))
                v.tensor_scalar(fbw[:, o * 16 * OS:(o + 1) * 16 * OS], facc[:],
                                RSQRT8, None, ALU.mult).then_inc(s_fb, 1)
            for t in range(NT):
                if t >= 1:
                    v.wait_ge(s_rs, t)
                    if t - 1 >= 2:
                        v.wait_ge(s_st[(t - 1) % 2], 16 * ((t - 1) // 2))
                    v.tensor_scalar(yo[(t - 1) % 2][:], ssl[(t - 1) % 2][:],
                                    stat[:, 0:1], stat[:, 3:4],
                                    ALU.subtract, ALU.mult).then_inc(s_no, 1)
                if t >= 2:
                    v.wait_ge(s_no, t - 1)
                if t >= 1:
                    v.wait_ge(s_vs, 5 * t)
                for o in range(NOS):
                    gidx = t * NOS + o
                    v.wait_ge(s_mm, gidx + 1)
                    v.scalar_tensor_tensor(
                        ssl[t % 2][:, o * OS:(o + 1) * OS], p_mm[gidx % 2][:], 1.0,
                        xb[:, t * D + o * OS: t * D + (o + 1) * OS],
                        ALU.mult, ALU.add, accum_out=acc8[:, o:o + 1]).then_inc(s_ev, 1)
                v.wait_ge(s_ev, 4 * (t + 1))
                if t >= 1:
                    v.wait_ge(s_vq, 4 * t)
                for o in range(NOS):
                    v.scalar_tensor_tensor(
                        sqs[:, o * OS:(o + 1) * OS], ssl[t % 2][:, o * OS:(o + 1) * OS], 1.0,
                        ssl[t % 2][:, o * OS:(o + 1) * OS],
                        ALU.mult, ALU.mult, accum_out=acc8[:, 4 + o:5 + o]).then_inc(s_vq, 1)
                v.tensor_reduce(stat[:, 0:1], acc8[:, 0:4], AX.X, ALU.add).then_inc(s_vs, 1)
                v.wait_ge(s_vq, 4 * (t + 1))
                v.tensor_reduce(stat[:, 2:3], acc8[:, 4:8], AX.X, ALU.add).then_inc(s_vs, 1)
                v.wait_ge(s_vs, 5 * t + 1)
                v.tensor_scalar(stat[:, 0:1], stat[:, 0:1], 1.0 / D, None, ALU.mult).then_inc(s_vs, 1)
                v.wait_ge(s_vs, 5 * t + 3)
                v.tensor_tensor(stat[:, 1:2], stat[:, 0:1], stat[:, 0:1], ALU.mult).then_inc(s_vs, 1)
                v.wait_ge(s_vs, 5 * t + 4)
                v.scalar_tensor_tensor(stat[:, 2:3], stat[:, 2:3], 1.0 / D, stat[:, 1:2],
                                       ALU.mult, ALU.subtract).then_inc(s_vs, 1)
            t = NT - 1
            v.wait_ge(s_rs, NT)
            v.wait_ge(s_st[t % 2], 16 * (t // 2))
            v.tensor_scalar(yo[t % 2][:], ssl[t % 2][:], stat[:, 0:1], stat[:, 3:4],
                            ALU.subtract, ALU.mult).then_inc(s_no, 1)

        @block.scalar
        def _(sc):
            k = 0
            for t in range(NT):
                if t >= 2:
                    sc.wait_ge(s_mm, (t - 1) * NOS)
                for dc in range(16):
                    sc.wait_ge(s_petr, k + 1)
                    sc.copy(xtr[t % 2][:, dc * 128:(dc + 1) * 128], p_tr[k % 2][:]).then_inc(s_trev, 1)
                    k += 1
                if t >= 1:
                    sc.wait_ge(s_vs, 5 * t)
                    sc.activation(stat[:, 2:3], stat[:, 2:3], AF.Ln, bias=epsc[:]).then_inc(s_lnx, 1)
                    sc.wait_ge(s_lnx, t)
                    sc.activation(stat[:, 3:4], stat[:, 2:3], AF.Exp, scale=-0.5).then_inc(s_rs, 1)
            sc.wait_ge(s_vs, 5 * NT)
            sc.activation(stat[:, 2:3], stat[:, 2:3], AF.Ln, bias=epsc[:]).then_inc(s_lnx, 1)
            sc.wait_ge(s_lnx, NT)
            sc.activation(stat[:, 3:4], stat[:, 2:3], AF.Exp, scale=-0.5).then_inc(s_rs, 1)

        @block.tensor
        def _(te):
            te.wait_ge(s_cl, 16)
            k = 0
            for t in range(NT):
                te.wait_ge(s_x[t % 2], 16 * (t // 2 + 1))
                for dc in range(16):
                    if k >= 2:
                        te.wait_ge(s_trev, k - 1)
                    te.transpose(p_tr[k % 2][:], xb[:, t * D + dc * 128: t * D + (dc + 1) * 128],
                                 idb[:]).then_inc(s_petr, 1)
                    k += 1
                te.wait_ge(s_trev, (t + 1) * 16)
                for o in range(NOS):
                    gidx = t * NOS + o
                    te.wait_ge(s_fb, o + 1)
                    if gidx == 0:
                        te.wait_ge(s_bb, 16)
                    if gidx >= 2:
                        te.wait_ge(s_ev, gidx - 1)
                    for dc in range(16):
                        te.matmul(p_mm[gidx % 2][:],
                                  xtr[t % 2][:, dc * 128:(dc + 1) * 128],
                                  fbw[:, o * 16 * OS + dc * OS: o * 16 * OS + (dc + 1) * OS],
                                  start=(dc == 0), stop=False)
                    te.matmul(p_mm[gidx % 2][:], onesb[:], biasb[:, o * OS:(o + 1) * OS],
                              start=False, stop=True).then_inc(s_mm, 1)
    return nc

def _run(nc, in_maps, **kw):
    return run_bass_kernel_spmd(nc, in_maps, core_ids=list(range(NCORE)), **kw)

def run_l1c(x3_full, fac, bias_, trace=False):
    nc = build_l1c()
    shards = shard_rows(x3_full)
    fac = np.ascontiguousarray(np.asarray(fac, np.float32))
    bias_ = np.ascontiguousarray(np.asarray(bias_, np.float32))
    in_maps = [{"x": s, "fac": fac, "bias": bias_.reshape(1, D), "idn": IDN_BF, "onesr": ONESROW_BF}
               for s in shards]
    res = _run(nc, in_maps, trace=trace)
    return unshard_rows([r["y"] for r in res.results]), res


def build_l0a(TT=T):
    NTt = TT // 128
    nc = bass.Bass("TRN2", target_bir_lowering=False)
    x = nc.dram_tensor("x", [TT, D], F32, kind="ExternalInput").ap()
    wq = nc.dram_tensor("wq", [16, 16, 128, 128], F32, kind="ExternalInput").ap()
    wkv = nc.dram_tensor("wkv", [16, 16, 128, 128], F32, kind="ExternalInput").ap()
    idn = nc.dram_tensor("idn", [128, 128], BF16, kind="ExternalInput").ap()
    qch = nc.dram_tensor("qch", [D, TT], BF16, kind="ExternalOutput").ap()
    kvch = nc.dram_tensor("kvch", [D, TT], BF16, kind="ExternalOutput").ap()

    es = ExitStack()
    sb = lambda n, s, d: es.enter_context(nc.sbuf_tensor(n, s, d))
    ps = lambda n, s, d: es.enter_context(nc.psum_tensor(n, s, d))
    sem = lambda n: es.enter_context(nc.semaphore(n))
    with es:
        xb = [sb(f"xb{i}", [128, D], BF16) for i in range(2)]
        xtf = sb("xtf", [128, 16 * TT], BF16)
        wblk = [sb(f"wblk{i}", [128, 16 * 128], BF16) for i in range(4)]
        ob = [sb(f"ob{i}", [128, TT], BF16) for i in range(2)]
        idb = sb("idb", [128, 128], BF16)
        p_tr = [ps(f"p_tr{i}", [128, 128], BF16) for i in range(2)]
        p_mm = [ps(f"p_mm{i}", [128, 512], F32) for i in range(2)]
        s_cl = sem("s_cl")
        s_x = [sem(f"s_x{i}") for i in range(2)]
        s_w = [sem(f"s_w{i}") for i in range(4)]
        s_petr = sem("s_petr")
        s_trev = sem("s_trev")
        s_mm = sem("s_mm")
        s_ev = sem("s_ev")
        s_st = [sem(f"s_st{i}") for i in range(2)]
        block = es.enter_context(nc.Block())

        NB = 32
        TG = TT // 512

        @block.sync
        def _(sync):
            sync.dma_start(out=idb[:], in_=idn).then_inc(s_cl, 16)
            for bkid in range(NB):
                sync.wait_ge(s_ev, bkid * TG + TG)
                dst = (qch if bkid < 16 else kvch)[(bkid % 16) * 128:(bkid % 16) * 128 + 128, :]
                sync.dma_start(out=dst, in_=ob[bkid % 2][:]).then_inc(s_st[bkid % 2], 16)
            sync.wait_ge(s_st[0], 16 * (NB // 2))
            sync.wait_ge(s_st[1], 16 * (NB // 2))

        @block.gpsimd
        def _(g):
            for t in range(NTt):
                if t >= 2:
                    g.wait_ge(s_x[t % 2], 16 * (t // 2))
                    g.wait_ge(s_petr, (t - 1) * 16)
                g.dma_start(out=xb[t % 2][:], in_=x[t * 128:(t + 1) * 128, :]).then_inc(s_x[t % 2], 16)
            for bkid in range(NB):
                w_ap = (wq if bkid < 16 else wkv)[bkid % 16]
                if bkid >= 4:
                    g.wait_ge(s_mm, (bkid - 3) * TG)
                g.dma_start(out=wblk[bkid % 4][:].rearrange("p (dc c) -> p dc c", c=128),
                            in_=w_ap.rearrange("dc p c -> p dc c")).then_inc(s_w[bkid % 4], 16)

        @block.scalar
        def _(sc):
            k = 0
            for t in range(NTt):
                for dc in range(16):
                    sc.wait_ge(s_petr, k + 1)
                    sc.copy(xtf[:, dc * TT + t * 128: dc * TT + (t + 1) * 128], p_tr[k % 2][:]).then_inc(s_trev, 1)
                    k += 1

        @block.vector
        def _(v):
            for bkid in range(NB):
                for tg in range(TG):
                    gidx = bkid * TG + tg
                    v.wait_ge(s_mm, gidx + 1)
                    if bkid >= 2 and tg == 0:
                        v.wait_ge(s_st[bkid % 2], 16 * (bkid // 2))
                    v.tensor_copy(ob[bkid % 2][:, tg * 512:(tg + 1) * 512], p_mm[gidx % 2][:]).then_inc(s_ev, 1)

        @block.tensor
        def _(te):
            te.wait_ge(s_cl, 16)
            k = 0
            for t in range(NTt):
                te.wait_ge(s_x[t % 2], 16 * (t // 2 + 1))
                for dc in range(16):
                    if k >= 2:
                        te.wait_ge(s_trev, k - 1)
                    te.transpose(p_tr[k % 2][:], xb[t % 2][:, dc * 128:(dc + 1) * 128], idb[:]).then_inc(s_petr, 1)
                    k += 1
            te.wait_ge(s_trev, NTt * 16)
            for bkid in range(NB):
                te.wait_ge(s_w[bkid % 4], 16 * (bkid // 4 + 1))
                for tg in range(TG):
                    gidx = bkid * TG + tg
                    if gidx >= 2:
                        te.wait_ge(s_ev, gidx - 1)
                    for dc in range(16):
                        mi = te.matmul(p_mm[gidx % 2][:],
                                       wblk[bkid % 4][:, dc * 128:(dc + 1) * 128],
                                       xtf[:, dc * TT + tg * 512: dc * TT + tg * 512 + 512],
                                       start=(dc == 0), stop=(dc == 15))
                    mi.then_inc(s_mm, 1)
    return nc

def run_l0a(x_full, W_q, W_kv, trace=False):
    nc = build_l0a()
    shards = shard_rows(x_full)
    wq_t = np.ascontiguousarray(
        np.asarray(W_q, np.float32).T.reshape(16, 128, 16, 128).transpose(2, 0, 1, 3))
    wkv_t = np.ascontiguousarray(
        np.asarray(W_kv, np.float32).T.reshape(16, 128, 16, 128).transpose(2, 0, 1, 3))
    in_maps = [{"x": s, "wq": wq_t, "wkv": wkv_t, "idn": IDN_BF} for s in shards]
    res = _run(nc, in_maps, trace=trace)
    return ([r["qch"] for r in res.results], [r["kvch"] for r in res.results], res)


def build_l0c(TT=T):
    NTt = TT // 128
    nc = bass.Bass("TRN2", target_bir_lowering=False)
    x = nc.dram_tensor("x", [TT, D], F32, kind="ExternalInput").ap()
    vht = nc.dram_tensor("vht", [16, NTt, 128, 128], BF16, kind="ExternalInput").ap()
    wo = nc.dram_tensor("wo", [16, 128, D], F32, kind="ExternalInput").ap()
    y = nc.dram_tensor("y", [TT, D], F32, kind="ExternalOutput").ap()

    OS = 512
    NOS = D // OS
    es = ExitStack()
    sb = lambda n, s, d: es.enter_context(nc.sbuf_tensor(n, s, d))
    ps = lambda n, s, d: es.enter_context(nc.psum_tensor(n, s, d))
    sem = lambda n: es.enter_context(nc.semaphore(n))
    with es:
        xb = sb("xb", [128, NTt * D], BF16)
        wob = sb("wob", [128, 16 * D], BF16)
        vhb = [sb(f"vhb{i}", [128, 16 * 128], BF16) for i in range(2)]
        ssl = [sb(f"ssl{i}", [128, D], BF16) for i in range(2)]
        sqs = sb("sqs", [128, D], BF16)
        acc8 = sb("acc8", [128, 8], F32)
        stat = sb("stat", [128, 4], F32)
        epsc = sb("epsc", [128, 1], F32)
        yo = [sb(f"yo{i}", [128, D], F32) for i in range(2)]
        p_mm = [ps(f"p_mm{i}", [128, OS], F32) for i in range(2)]
        s_x = [sem(f"s_x{i}") for i in range(2)]
        s_w = sem("s_w")
        s_vh = [sem(f"s_vh{i}") for i in range(2)]
        s_mm = sem("s_mm")
        s_ev = sem("s_ev")
        s_vq = sem("s_vq")
        s_vs = sem("s_vs")
        s_lnx = sem("s_lnx")
        s_rs = sem("s_rs")
        s_no = sem("s_no")
        s_st = [sem(f"s_st{i}") for i in range(2)]
        block = es.enter_context(nc.Block())

        @block.sync
        def _(sync):
            for t in range(NTt):
                sync.wait_ge(s_no, t + 1)
                sync.dma_start(out=y[t * 128:(t + 1) * 128, :],
                               in_=yo[t % 2][:]).then_inc(s_st[t % 2], 16)
            sync.wait_ge(s_st[0], 16 * ((NTt + 1) // 2))
            sync.wait_ge(s_st[1], 16 * (NTt // 2))

        @block.gpsimd
        def _(g):
            for dc in range(16):
                g.wait_ge(s_w, 16 * dc)
                g.dma_start(out=wob[:, dc * D:(dc + 1) * D], in_=wo[dc]).then_inc(s_w, 16)
            for t in range(NTt):
                if t >= 2:
                    g.wait_ge(s_x[t % 2], 16 * (t // 2))
                g.dma_start(out=xb[:, t * D:(t + 1) * D],
                            in_=x[t * 128:(t + 1) * 128, :]).then_inc(s_x[t % 2], 16)
            for t in range(NTt):
                if t >= 2:
                    g.wait_ge(s_mm, (t - 1) * NOS)
                    g.wait_ge(s_vh[t % 2], 16 * (t // 2))
                g.dma_start(out=vhb[t % 2][:].rearrange("p (dc c) -> p dc c", c=128),
                            in_=vht[:, t].rearrange("dc p c -> p dc c")).then_inc(s_vh[t % 2], 16)

        @block.vector
        def _(v):
            v.memset(epsc[:], EPS)
            for t in range(NTt):
                if t >= 1:
                    v.wait_ge(s_rs, t)
                    if t - 1 >= 2:
                        v.wait_ge(s_st[(t - 1) % 2], 16 * ((t - 1) // 2))
                    v.tensor_scalar(yo[(t - 1) % 2][:], ssl[(t - 1) % 2][:],
                                    stat[:, 0:1], stat[:, 3:4],
                                    ALU.subtract, ALU.mult).then_inc(s_no, 1)
                if t >= 2:
                    v.wait_ge(s_no, t - 1)
                if t >= 1:
                    v.wait_ge(s_vs, 5 * t)
                for o in range(NOS):
                    gidx = t * NOS + o
                    v.wait_ge(s_mm, gidx + 1)
                    v.wait_ge(s_x[t % 2], 16 * (t // 2 + 1))
                    v.scalar_tensor_tensor(
                        ssl[t % 2][:, o * OS:(o + 1) * OS], p_mm[gidx % 2][:], 1.0,
                        xb[:, t * D + o * OS: t * D + (o + 1) * OS],
                        ALU.mult, ALU.add, accum_out=acc8[:, o:o + 1]).then_inc(s_ev, 1)
                v.wait_ge(s_ev, 4 * (t + 1))
                if t >= 1:
                    v.wait_ge(s_vq, 4 * t)
                for o in range(NOS):
                    v.scalar_tensor_tensor(
                        sqs[:, o * OS:(o + 1) * OS], ssl[t % 2][:, o * OS:(o + 1) * OS], 1.0,
                        ssl[t % 2][:, o * OS:(o + 1) * OS],
                        ALU.mult, ALU.mult, accum_out=acc8[:, 4 + o:5 + o]).then_inc(s_vq, 1)
                v.tensor_reduce(stat[:, 0:1], acc8[:, 0:4], AX.X, ALU.add).then_inc(s_vs, 1)
                v.wait_ge(s_vq, 4 * (t + 1))
                v.tensor_reduce(stat[:, 2:3], acc8[:, 4:8], AX.X, ALU.add).then_inc(s_vs, 1)
                v.wait_ge(s_vs, 5 * t + 1)
                v.tensor_scalar(stat[:, 0:1], stat[:, 0:1], 1.0 / D, None, ALU.mult).then_inc(s_vs, 1)
                v.wait_ge(s_vs, 5 * t + 3)
                v.tensor_tensor(stat[:, 1:2], stat[:, 0:1], stat[:, 0:1], ALU.mult).then_inc(s_vs, 1)
                v.wait_ge(s_vs, 5 * t + 4)
                v.scalar_tensor_tensor(stat[:, 2:3], stat[:, 2:3], 1.0 / D, stat[:, 1:2],
                                       ALU.mult, ALU.subtract).then_inc(s_vs, 1)
            t = NTt - 1
            v.wait_ge(s_rs, NTt)
            v.wait_ge(s_st[t % 2], 16 * (t // 2))
            v.tensor_scalar(yo[t % 2][:], ssl[t % 2][:], stat[:, 0:1], stat[:, 3:4],
                            ALU.subtract, ALU.mult).then_inc(s_no, 1)

        @block.scalar
        def _(sc):
            for t in range(NTt):
                sc.wait_ge(s_vs, 5 * (t + 1))
                sc.activation(stat[:, 2:3], stat[:, 2:3], AF.Ln, bias=epsc[:]).then_inc(s_lnx, 1)
                sc.wait_ge(s_lnx, t + 1)
                sc.activation(stat[:, 3:4], stat[:, 2:3], AF.Exp, scale=-0.5).then_inc(s_rs, 1)

        @block.tensor
        def _(te):
            te.wait_ge(s_w, 16 * 16)
            for t in range(NTt):
                te.wait_ge(s_vh[t % 2], 16 * (t // 2 + 1))
                for o in range(NOS):
                    gidx = t * NOS + o
                    if gidx >= 2:
                        te.wait_ge(s_ev, gidx - 1)
                    for dc in range(16):
                        mi = te.matmul(p_mm[gidx % 2][:],
                                       vhb[t % 2][:, dc * 128:(dc + 1) * 128],
                                       wob[:, dc * D + o * OS: dc * D + o * OS + OS],
                                       start=(dc == 0), stop=(dc == 15))
                    mi.then_inc(s_mm, 1)
    return nc


def build_l1a(TT=T):
    NTt = TT // 128
    nc = bass.Bass("TRN2", target_bir_lowering=False)
    x = nc.dram_tensor("x", [TT, D], F32, kind="ExternalInput").ap()
    wqkv = nc.dram_tensor("wqkv", [12, 16, 128, 512], F32, kind="ExternalInput").ap()
    cosr = nc.dram_tensor("cosr", [TT, 512], F32, kind="ExternalInput").ap()
    sinr = nc.dram_tensor("sinr", [TT, 512], F32, kind="ExternalInput").ap()
    idn = nc.dram_tensor("idn", [128, 128], BF16, kind="ExternalInput").ap()
    qkvt = nc.dram_tensor("qkvt", [12, NTt, 128, 512], BF16, kind="ExternalOutput").ap()

    es = ExitStack()
    sb = lambda n, s, d: es.enter_context(nc.sbuf_tensor(n, s, d))
    ps = lambda n, s, d: es.enter_context(nc.psum_tensor(n, s, d))
    sem = lambda n: es.enter_context(nc.semaphore(n))
    with es:
        xb = [sb(f"xb{i}", [128, D], BF16) for i in range(2)]
        xtf = sb("xtf", [128, 16 * TT], BF16)
        idb = sb("idb", [128, 128], BF16)
        wfs = [sb(f"wfs{i}", [128, 16 * 512], BF16) for i in range(2)]
        cosb = sb("cosb", [128, NTt * 512], F32)
        sinb = sb("sinb", [128, NTt * 512], F32)
        slab = [sb(f"slab{i}", [128, 512], BF16) for i in range(2)]
        rb1 = sb("rb1", [128, 128], F32)
        rb2 = sb("rb2", [128, 128], F32)
        p_tr = [ps(f"p_tr{i}", [128, 128], BF16) for i in range(2)]
        p_mm = [ps(f"p_mm{i}", [128, 512], F32) for i in range(2)]
        s_cl = sem("s_cl")
        s_x = [sem(f"s_x{i}") for i in range(2)]
        s_w = [sem(f"s_w{i}") for i in range(2)]
        s_petr = sem("s_petr")
        s_trev = sem("s_trev")
        s_mm = sem("s_mm")
        s_ev = sem("s_ev")
        s_vr = sem("s_vr")
        s_ro = sem("s_ro")
        s_st = [sem(f"s_st{i}") for i in range(2)]
        block = es.enter_context(nc.Block())

        @block.sync
        def _(sync):
            sync.dma_start(out=idb[:], in_=idn).then_inc(s_cl, 16)
            sync.wait_ge(s_cl, 16)
            sync.dma_start(out=cosb[:].rearrange("p (tc c) -> p tc c", c=512),
                           in_=cosr.rearrange("(tc p) c -> p tc c", p=128)).then_inc(s_cl, 16)
            sync.wait_ge(s_cl, 32)
            sync.dma_start(out=sinb[:].rearrange("p (tc c) -> p tc c", c=512),
                           in_=sinr.rearrange("(tc p) c -> p tc c", p=128)).then_inc(s_cl, 16)
            ro_c = 0
            for g in range(12 * NTt):
                ro_c += 2 if g < 8 * NTt else 1
                sync.wait_ge(s_ro, ro_c)
                sync.dma_start(out=qkvt[g // NTt, g % NTt],
                               in_=slab[g % 2][:]).then_inc(s_st[g % 2], 16)
            sync.wait_ge(s_st[0], 16 * (12 * NTt // 2))
            sync.wait_ge(s_st[1], 16 * (12 * NTt // 2))

        @block.gpsimd
        def _(g):
            for t in range(NTt):
                if t >= 2:
                    g.wait_ge(s_x[t % 2], 16 * (t // 2))
                    g.wait_ge(s_petr, (t - 1) * 16)
                g.dma_start(out=xb[t % 2][:], in_=x[t * 128:(t + 1) * 128, :]).then_inc(s_x[t % 2], 16)
            for fs in range(12):
                if fs >= 2:
                    g.wait_ge(s_mm, (fs - 1) * NTt)
                g.dma_start(out=wfs[fs % 2][:].rearrange("p (dc c) -> p dc c", c=512),
                            in_=wqkv[fs].rearrange("dc p c -> p dc c")).then_inc(s_w[fs % 2], 16)

        @block.scalar
        def _(sc):
            k = 0
            for t in range(NTt):
                for dc in range(16):
                    sc.wait_ge(s_petr, k + 1)
                    sc.copy(xtf[:, dc * TT + t * 128: dc * TT + (t + 1) * 128], p_tr[k % 2][:]).then_inc(s_trev, 1)
                    k += 1

        @block.vector
        def _(v):
            v.wait_ge(s_cl, 48)
            vr = 0
            ro = [0]
            for fs in range(12):
                for t in range(NTt):
                    g = fs * NTt + t
                    v.wait_ge(s_mm, g + 1)
                    if g >= 2:
                        v.wait_ge(s_st[g % 2], 16 * (g // 2))
                    v.tensor_copy(slab[g % 2][:], p_mm[g % 2][:]).then_inc(s_ev, 1)
                    if fs < 8:
                        sl3 = slab[g % 2][:].rearrange("p (h c) -> p h c", c=128)
                        q1 = sl3[:, :, 0:16]
                        q2 = sl3[:, :, 16:32]
                        tb3c = cosb[:, t * 512 + (fs % 4) * 128: t * 512 + (fs % 4) * 128 + 128].rearrange("p (h c) -> p h c", c=32)
                        tb3s = sinb[:, t * 512 + (fs % 4) * 128: t * 512 + (fs % 4) * 128 + 128].rearrange("p (h c) -> p h c", c=32)
                        rA = rb1[:, 0:64].rearrange("p (h c) -> p h c", c=16)
                        rB = rb1[:, 64:128].rearrange("p (h c) -> p h c", c=16)
                        rC = rb2[:, 0:64].rearrange("p (h c) -> p h c", c=16)
                        rD = rb2[:, 64:128].rearrange("p (h c) -> p h c", c=16)
                        v.wait_ge(s_ev, g + 1)
                        if ro[0] > 0:
                            v.wait_ge(s_ro, ro[0])
                        v.tensor_tensor(rA, q1, tb3c[:, :, 0:16], ALU.mult).then_inc(s_vr, 1)
                        v.tensor_tensor(rB, q2, tb3s[:, :, 0:16], ALU.mult).then_inc(s_vr, 1)
                        v.tensor_tensor(rC, q2, tb3c[:, :, 16:32], ALU.mult).then_inc(s_vr, 1)
                        v.tensor_tensor(rD, q1, tb3s[:, :, 16:32], ALU.mult).then_inc(s_vr, 1)
                        vr += 4
                        v.wait_ge(s_vr, vr)
                        v.tensor_tensor(q1, rA, rB, ALU.subtract).then_inc(s_ro, 1)
                        v.tensor_tensor(q2, rC, rD, ALU.add).then_inc(s_ro, 1)
                        ro[0] += 2
                    else:
                        v.wait_ge(s_ev, g + 1)
                        v.nop().then_inc(s_ro, 1)
                        ro[0] += 1

        @block.tensor
        def _(te):
            te.wait_ge(s_cl, 16)
            k = 0
            for t in range(NTt):
                te.wait_ge(s_x[t % 2], 16 * (t // 2 + 1))
                for dc in range(16):
                    if k >= 2:
                        te.wait_ge(s_trev, k - 1)
                    te.transpose(p_tr[k % 2][:], xb[t % 2][:, dc * 128:(dc + 1) * 128], idb[:]).then_inc(s_petr, 1)
                    k += 1
            te.wait_ge(s_trev, NTt * 16)
            for fs in range(12):
                te.wait_ge(s_w[fs % 2], 16 * (fs // 2 + 1))
                for t in range(NTt):
                    g = fs * NTt + t
                    if g >= 2:
                        te.wait_ge(s_ev, g - 1)
                    for dc in range(16):
                        mi = te.matmul(p_mm[g % 2][:],
                                       xtf[:, dc * TT + t * 128: dc * TT + (t + 1) * 128],
                                       wfs[fs % 2][:, dc * 512:(dc + 1) * 512],
                                       start=(dc == 0), stop=(dc == 15))
                    mi.then_inc(s_mm, 1)
    return nc



def build_l0b(TT=T, n_active_cores=NCORE, variant=''):
    NTt = TT // 128
    FB = NTt * 256
    HSEG = NTt // 2
    NTG = NTt // 4
    nc = bass.Bass("TRN2", target_bir_lowering=False)
    qch = nc.dram_tensor("qch", [D, TT], BF16, kind="ExternalInput").ap()
    kvch = nc.dram_tensor("kvch", [D, TT], BF16, kind="ExternalInput").ap()
    csb = nc.dram_tensor("csb", [128, 256], BF16, kind="ExternalInput").ap()
    ub = nc.dram_tensor("ub", [128, 128], BF16, kind="ExternalInput").ap()
    c2b = nc.dram_tensor("c2b", [128, 128], BF16, kind="ExternalInput").ap()
    s2nb = nc.dram_tensor("s2nb", [128, 128], BF16, kind="ExternalInput").ap()
    onescol = nc.dram_tensor("onescol", [128, 1], BF16, kind="ExternalInput").ap()
    onesr = nc.dram_tensor("onesr", [1, 128], BF16, kind="ExternalInput").ap()
    idn = nc.dram_tensor("idn", [128, 128], BF16, kind="ExternalInput").ap()
    flags = nc.dram_tensor("flags", [1, 2], F32, kind="ExternalInput").ap()
    pcolf = nc.dram_tensor("pcolf", [n_active_cores, 1], F32, kind="ExternalInput").ap()
    vht = nc.dram_tensor("vht", [16, NTt, 128, 128], BF16, kind="ExternalOutput").ap()
    kvfn_sp = nc.dram_tensor("kvfn_sp", [16, 128, FB], BF16).ap()
    cc_in = nc.dram_tensor("cc_in", [1, 16 * 512], F32).ap()
    cc_out = nc.dram_tensor("cc_out", [n_active_cores, 16 * 512], F32).ap()

    es = ExitStack()
    sb = lambda n, s, d: es.enter_context(nc.sbuf_tensor(n, s, d))
    ps = lambda n, s, d: es.enter_context(nc.psum_tensor(n, s, d))
    sem = lambda n: es.enter_context(nc.semaphore(n))
    with es:
        csb_s = sb("csb_s", [128, 256], BF16)
        ub_s = sb("ub_s", [128, 128], BF16)
        c2_s = sb("c2_s", [128, 128], BF16)
        s2_s = sb("s2_s", [128, 128], BF16)
        oc_s = sb("oc_s", [128, 1], BF16)
        or_s = sb("or_s", [1, 128], BF16)
        id_s = sb("id_s", [128, 128], BF16)
        fl_s = sb("fl_s", [1, 2], F32)
        pc_s = sb("pc_s", [n_active_cores, 1], F32)
        hin = [sb(f"hin{i}", [128, TT], BF16) for i in range(2)]
        kvn2 = [sb(f"kvn2{i}", [128, FB], BF16) for i in range(2)]
        ff32 = [sb(f"ff32{i}", [128, FB], F32) for i in range(2)]
        fn16 = [sb(f"fn16{i}", [128, FB], BF16) for i in range(2)]
        sqscr = sb("sqscr", [128, 256], BF16)
        ssqs = sb("ssqs", [128, 16], F32)
        invs = sb("invs", [128, 16], F32)
        carryf = sb("carryf", [1, 256], F32)
        carryb = sb("carryb", [1, 256], BF16)
        contrib = sb("contrib", [1, 16 * 512], F32)
        ccs = sb("ccs", [n_active_cores, 16 * 512], F32)
        prow = sb("prow", [1, 16 * 512], F32)
        c1b = sb("c1b", [1, 256], BF16)
        c2rb = sb("c2rb", [1, 256], BF16)
        tmpr = sb("tmpr", [1, 256], F32)
        kvfc = sb("kvfc", [128, FB], BF16)
        qvb = sb("qvb", [128, FB], BF16)
        qvt = sb("qvt", [128, FB], BF16)
        rbm1 = sb("rbm1", [128, 128], F32)
        rbm2 = sb("rbm2", [128, 128], F32)
        vhs = [sb(f"vhs{i}", [128, 512], BF16) for i in range(2)]
        p_fft = [ps(f"p_fft{i}", [128, 256], F32) for i in range(2)]
        p_cum = ps("p_cum", [128, 256], F32)
        p_tot = ps("p_tot", [1, 512], F32)
        p_tr = [ps(f"p_tr{i}", [128, 128], BF16) for i in range(2)]
        p_if = [ps(f"p_if{i}", [128, 512], F32) for i in range(2)]
        s_cl = sem("s_cl")
        s_hd = [sem(f"s_hd{i}") for i in range(2)]
        s_kb = [sem(f"s_kb{i}") for i in range(2)]
        s_fft = sem("s_fft")
        s_fe = sem("s_fe")
        s_sq = sem("s_sq")
        s_nr = sem("s_nr")
        s_nz = sem("s_nz")
        s_sp = [sem(f"s_sp{i}") for i in range(2)]
        s_ta = sem("s_ta")
        s_te = sem("s_te")
        s_cum = sem("s_cum")
        s_ce = sem("s_ce")
        s_cb = sem("s_cb")
        s_qv = sem("s_qv")
        s_petr = sem("s_petr")
        s_trev = sem("s_trev")
        s_if = sem("s_if")
        s_ie = sem("s_ie")
        s_vst = [sem(f"s_vst{i}") for i in range(2)]
        s_cc = sem("s_cc")
        s_cc2 = sem("s_cc2")
        s_cc3 = sem("s_cc3")
        block = es.enter_context(nc.Block())

        NST = 16 if "passa" in variant else 32
        for tok in variant.split("_"):
            if tok.startswith("h") and tok[1:].isdigit():
                NST = int(tok[1:])
        fft_v = lambda s, tc: s * NTt + tc + 1
        nz_v = lambda s: 1 + (s + 1) * NTt
        ta_passA = lambda s, seg: s * 2 + seg + 1
        TA_X0 = 32
        ta_passB = lambda h, tc: 48 + h * NTt + tc + 1
        TE_X0 = 32
        TE_XE = 48
        te_head0 = lambda h: TE_XE + (3 + NTt) * h
        te_carryrow = lambda h, k: te_head0(h) + k + 1
        te_add = lambda h, tc: te_head0(h) + 3 + tc + 1
        cb_head0 = lambda h: (NTt + 1) * h + 1
        cb_upd = lambda h, tc: (NTt + 1) * h + 2 + tc
        cum_v = lambda h, tc: h * NTt + tc + 1
        qv_v = lambda h, tc: h * (6 * NTt) + (tc + 1) * 6
        petr_v = lambda h, k: h * (2 * NTt) + k + 1
        if_v = lambda h, tg: h * NTG + tg + 1
        NCL = 6 if "notiny" in variant else 8

        @block.sync
        def _(sync):
            cl = Ctr()
            cl_pairs = [(csb_s, csb), (ub_s, ub), (c2_s, c2b), (s2_s, s2nb),
                        (or_s, onesr), (id_s, idn)]
            if "notiny" not in variant:
                cl_pairs += [(fl_s, flags), (pc_s, pcolf)]
            for (dst, srcap) in cl_pairs:
                sync.dma_start(out=dst[:], in_=srcap).then_inc(s_cl, 16)
                sync.wait_ge(s_cl, cl.inc(16))
            for s in range(NST):
                h = s if s < 16 else s - 16
                if 2 <= s and s - 2 < 16 and "nospill" not in variant:
                    sp = s - 2
                    sync.wait_ge(s_nz, nz_v(sp))
                    if sp >= 2:
                        sync.wait_ge(s_sp[sp % 2], 16 * (sp // 2))
                    sync.dma_start(out=kvfn_sp[sp], in_=fn16[sp % 2][:]).then_inc(s_sp[sp % 2], 16)
                if s >= 18:
                    hh = s - 18
                    for tg in range(NTG):
                        ig = hh * NTG + tg
                        sync.wait_ge(s_ie, if_v(hh, tg))
                        sync.dma_start(out=vht[hh, tg * 4:(tg + 1) * 4].rearrange("tc p c -> p tc c"),
                                       in_=vhs[ig % 2][:].rearrange("p (tc c) -> p tc c", c=128)
                                       ).then_inc(s_vst[ig % 2], 16)
                src = kvch if s < 16 else qch
                if s >= 2:
                    sync.wait_ge(s_fft, (s - 1) * NTt)
                sync.dma_start(out=hin[s % 2][:], in_=src[h * 128:(h + 1) * 128, :]).then_inc(s_hd[s % 2], 16)
            for hh in (14, 15):
                for tg in range(NTG):
                    ig = hh * NTG + tg
                    sync.wait_ge(s_ie, if_v(hh, tg))
                    sync.dma_start(out=vht[hh, tg * 4:(tg + 1) * 4].rearrange("tc p c -> p tc c"),
                                   in_=vhs[ig % 2][:].rearrange("p (tc c) -> p tc c", c=128)
                                   ).then_inc(s_vst[ig % 2], 16)
            if "passa" in variant:
                nsp = (max(0, min(NST, 16) - 2) if NST < 18 else 16) if "nospill" not in variant else 0
                sync.wait_ge(s_nz, NST * NTt)
                if nsp >= 1:
                    sync.wait_ge(s_sp[0], 16 * ((nsp + 1) // 2))
                if nsp >= 2:
                    sync.wait_ge(s_sp[1], 16 * (nsp // 2))
            else:
                sync.wait_ge(s_vst[0], 16 * ((16 * NTG + 1) // 2))
                sync.wait_ge(s_vst[1], 16 * (16 * NTG // 2))

        @block.gpsimd
        def _(g):
            if "passa" in variant:
                g.nop()
                return
            g.wait_ge(s_te, TE_X0)
            g.dma_start(out=cc_in, in_=contrib[:]).then_inc(s_cc, 16)
            g.wait_ge(s_cc, 16)
            if "nocc" in variant:
                cct = Ctr()
                for r in range(n_active_cores):
                    g.wait_ge(s_cc2, cct.v)
                    g.dma_start(out=cc_out[r:r + 1, :], in_=cc_in).then_inc(s_cc2, 16)
                    cct.inc(16)
            else:
                g.collective_compute("AllGather", ALU.bypass,
                                     replica_groups=[list(range(n_active_cores))],
                                     ins=[cc_in], outs=[cc_out]).then_inc(s_cc2)
            g.wait_ge(s_cc2, 1)
            g.dma_start(out=ccs[:], in_=cc_out).then_inc(s_cc3, 16)
            for h in range(16 if "passa" not in variant else 0):
                g.wait_ge(s_sp[h % 2], 16 * (h // 2 + 1))
                if h >= 2:
                    g.wait_ge(s_cum, cum_v(h - 2, NTt - 1))
                    g.wait_ge(s_ta, 48 + (h - 1) * NTt)
                g.dma_start(out=kvn2[h % 2][:], in_=kvfn_sp[h]).then_inc(s_kb[h % 2], 16)

        @block.tensor
        def _(te):
            te.wait_ge(s_cl, 16 * NCL)
            nfft = 0
            for s in range(NST):
                h = s if s < 16 else s - 16
                te.wait_ge(s_hd[s % 2], 16 * (s // 2 + 1))
                for tc in range(NTt):
                    if nfft >= 2:
                        te.wait_ge(s_fe, nfft - 1)
                        te.wait_ge(s_sq, nfft - 1)
                    te.matmul(p_fft[nfft % 2][:], hin[s % 2][:, tc * 128:(tc + 1) * 128], csb_s[:],
                              start=True, stop=True).then_inc(s_fft, 1)
                    nfft += 1
                if s < 16:
                    if "nototal" in variant:
                        continue
                    te.wait_ge(s_nz, nz_v(s))
                    for seg in range(2):
                        gidx = s * 2 + seg
                        if gidx >= 1:
                            te.wait_ge(s_te, gidx)
                        for j in range(HSEG):
                            tc = seg * HSEG + j
                            mi = te.matmul(p_tot[:, 0:256], oc_s[:],
                                           fn16[s % 2][:, tc * 256:(tc + 1) * 256],
                                           start=(j == 0), stop=(j == HSEG - 1))
                        mi.then_inc(s_ta, 1)
                    continue
                if s == 16:
                    te.wait_ge(s_cc3, 16)
                    te.wait_ge(s_te, TE_X0)
                    for k in range(16):
                        if k >= 1:
                            te.wait_ge(s_te, TE_X0 + k)
                        te.matmul(p_tot[:], pc_s[:], ccs[:, k * 512:(k + 1) * 512],
                                  start=True, stop=True).then_inc(s_ta, 1)
                te.wait_ge(s_nz, nz_v(s))
                te.wait_ge(s_kb[h % 2], 16 * (h // 2 + 1))
                te.wait_ge(s_te, te_carryrow(h, 2))
                for tc in range(NTt):
                    if cum_v(h, tc) >= 2:
                        te.wait_ge(s_ce, cum_v(h, tc) - 1)
                    te.wait_ge(s_cb, cb_head0(h) if tc == 0 else cb_upd(h, tc - 1))
                    te.matmul(p_cum[:], ub_s[:], kvn2[h % 2][:, tc * 256:(tc + 1) * 256],
                              start=True, stop=False)
                    te.matmul(p_cum[:], or_s[:], carryb[:], start=False, stop=False)
                    te.matmul(p_cum[:], or_s[:], (c1b if tc < HSEG else c2rb)[:],
                              start=False, stop=True).then_inc(s_cum, 1)
                    te.wait_ge(s_te, te_add(h, tc) - 1)
                    te.matmul(p_tot[:, 0:256], oc_s[:], kvn2[h % 2][:, tc * 256:(tc + 1) * 256],
                              start=True, stop=True).then_inc(s_ta, 1)
                ntr = h * 2 * NTt
                for tc in range(NTt):
                    te.wait_ge(s_qv, qv_v(h, tc))
                    for half in range(2):
                        if ntr >= 2:
                            te.wait_ge(s_trev, ntr - 1)
                        te.transpose(p_tr[ntr % 2][:],
                                     qvb[:, tc * 256 + half * 128: tc * 256 + half * 128 + 128],
                                     id_s[:]).then_inc(s_petr, 1)
                        ntr += 1
                for tg in range(NTG):
                    ig = h * NTG + tg
                    te.wait_ge(s_trev, h * 2 * NTt + (tg + 1) * 8)
                    if ig >= 2:
                        te.wait_ge(s_ie, ig - 1)
                    te.matmul(p_if[ig % 2][:], c2_s[:],
                              qvt[:, tg * 1024: tg * 1024 + 512], start=True, stop=False)
                    te.matmul(p_if[ig % 2][:], s2_s[:],
                              qvt[:, tg * 1024 + 512: tg * 1024 + 1024],
                              start=False, stop=True).then_inc(s_if, 1)

        @block.scalar
        def _(sc):
            nsq = 0
            ntrev = 0
            for s in range(NST):
                h = s if s < 16 else s - 16
                for tc in range(NTt):
                    sc.wait_ge(s_fft, nsq + 1)
                    sc.wait_ge(s_fe, nsq + 1)
                    if nsq >= 1:
                        sc.wait_ge(s_sq, nsq)
                    if "nosq" in variant:
                        sc.copy(sqscr[:], p_fft[nsq % 2][:]).then_inc(s_sq, 1)
                    else:
                        sc.activation(sqscr[:], p_fft[nsq % 2][:], AF.Square,
                                      accum_out=ssqs[:, tc:tc + 1]).then_inc(s_sq, 1)
                    nsq += 1
                sc.wait_ge(s_sq, (s + 1) * NTt)
                if s >= 1:
                    sc.wait_ge(s_nz, nz_v(s - 1))
                if "nosq" in variant:
                    sc.nop().then_inc(s_nr, 1)
                    sc.wait_ge(s_nr, 2 * s + 1)
                    sc.nop().then_inc(s_nr, 1)
                else:
                    sc.activation(ssqs[:, 0:NTt], ssqs[:, 0:NTt], AF.Ln).then_inc(s_nr, 1)
                    sc.wait_ge(s_nr, 2 * s + 1)
                    sc.activation(invs[:, 0:NTt], ssqs[:, 0:NTt], AF.Exp, scale=-0.5).then_inc(s_nr, 1)
                if s >= 16:
                    sc.wait_ge(s_if, NTG * h)
                    for k2 in range(2 * NTt):
                        sc.wait_ge(s_petr, petr_v(h, k2) - 1 + 1)
                        tc = k2 // 2
                        half = k2 % 2
                        tg = tc // 4
                        dst = qvt[:, tg * 1024 + half * 512 + (tc % 4) * 128:
                                  tg * 1024 + half * 512 + (tc % 4) * 128 + 128]
                        sc.copy(dst, p_tr[ntrev % 2][:]).then_inc(s_trev, 1)
                        ntrev += 1

        @block.vector
        def _(v):
            v.wait_ge(s_cl, 16 * NCL)
            v.memset(oc_s[:], 1.0).then_inc(s_nz, 1)
            for s in range(NST):
                h = s if s < 16 else s - 16
                if s >= 2:
                    v.wait_ge(s_nz, nz_v(s - 1))
                for tc in range(NTt):
                    v.wait_ge(s_fft, fft_v(s, tc))
                    v.tensor_copy(ff32[s % 2][:, tc * 256:(tc + 1) * 256],
                                  p_fft[(s * NTt + tc) % 2][:]).then_inc(s_fe, 1)
                v.wait_ge(s_fe, (s + 1) * NTt)
                v.wait_ge(s_nr, 2 * s + 2)
                if 2 <= s < 18:
                    v.wait_ge(s_sp[s % 2], 16 * (s // 2))
                    if s < 16:
                        pass
                if (2 <= s < 16 or s in (16, 17)) and "nototal" not in variant:
                    if s - 2 < 16:
                        v.wait_ge(s_ta, 2 * (s - 1))
                if s >= 18:
                    v.wait_ge(s_qv, qv_v(s - 18, NTt - 1))
                for tc in range(NTt):
                    v.tensor_scalar(fn16[s % 2][:, tc * 256:(tc + 1) * 256],
                                    ff32[s % 2][:, tc * 256:(tc + 1) * 256],
                                    (1.0 if "nosq" in variant else invs[:, tc:tc + 1]),
                                    None, ALU.mult).then_inc(s_nz, 1)
                if s < 16 and "nototal" not in variant:
                    for seg in range(2):
                        v.wait_ge(s_ta, ta_passA(s, seg))
                        v.tensor_copy(contrib[:, s * 512 + seg * 256: s * 512 + (seg + 1) * 256],
                                      p_tot[:, 0:256]).then_inc(s_te, 1)
                if s >= 16:
                    if h == 0:
                        for k in range(16):
                            v.wait_ge(s_ta, TA_X0 + k + 1)
                            v.tensor_copy(prow[:, k * 512:(k + 1) * 512], p_tot[:]).then_inc(s_te, 1)
                    if h >= 1:
                        v.wait_ge(s_cum, cum_v(h - 1, NTt - 1))
                    v.tensor_scalar(c1b[:], prow[:, h * 512: h * 512 + 256],
                                    fl_s[:, 0:1], None, ALU.mult).then_inc(s_te, 1)
                    v.tensor_scalar(tmpr[:], prow[:, h * 512 + 256: h * 512 + 512],
                                    fl_s[:, 1:2], None, ALU.mult).then_inc(s_te, 1)
                    v.wait_ge(s_te, te_carryrow(h, 1))
                    v.tensor_tensor(c2rb[:], prow[:, h * 512: h * 512 + 256], tmpr[:],
                                    ALU.add).then_inc(s_te, 1)
                    v.memset(carryf[:], 0.0)
                    v.memset(carryb[:], 0.0).then_inc(s_cb, 1)
                    if h >= 1:
                        v.wait_ge(s_petr, petr_v(h - 1, 2 * NTt - 1))
                        v.wait_ge(s_qv, qv_v(h - 1, NTt - 1))
                    for tc in range(NTt):
                        v.wait_ge(s_cum, cum_v(h, tc))
                        v.tensor_copy(kvfc[:, tc * 256:(tc + 1) * 256], p_cum[:]).then_inc(s_ce, 1)
                        v.wait_ge(s_ta, ta_passB(h, tc))
                        v.tensor_tensor(carryf[:], carryf[:], p_tot[:, 0:256], ALU.add).then_inc(s_te, 1)
                        v.wait_ge(s_te, te_add(h, tc))
                        v.tensor_copy(carryb[:], carryf[:]).then_inc(s_cb, 1)
                        qr = fn16[s % 2][:, tc * 256: tc * 256 + 128]
                        qi = fn16[s % 2][:, tc * 256 + 128: tc * 256 + 256]
                        kr = kvfc[:, tc * 256: tc * 256 + 128]
                        ki = kvfc[:, tc * 256 + 128: tc * 256 + 256]
                        orr = qvb[:, tc * 256: tc * 256 + 128]
                        oii = qvb[:, tc * 256 + 128: tc * 256 + 256]
                        base = qv_v(h, tc) - 6
                        v.wait_ge(s_ce, cum_v(h, tc))
                        if base >= 1:
                            v.wait_ge(s_qv, base)
                        v.tensor_tensor(rbm1[:], qr, kr, ALU.mult).then_inc(s_qv, 1)
                        v.tensor_tensor(rbm2[:], qi, ki, ALU.mult).then_inc(s_qv, 1)
                        v.wait_ge(s_qv, base + 2)
                        v.tensor_tensor(orr, rbm1[:], rbm2[:], ALU.subtract).then_inc(s_qv, 1)
                        v.wait_ge(s_qv, base + 3)
                        v.tensor_tensor(rbm1[:], qr, ki, ALU.mult).then_inc(s_qv, 1)
                        v.tensor_tensor(rbm2[:], qi, kr, ALU.mult).then_inc(s_qv, 1)
                        v.wait_ge(s_qv, base + 5)
                        v.tensor_tensor(oii, rbm1[:], rbm2[:], ALU.add).then_inc(s_qv, 1)
                    for tg in range(NTG):
                        ig = h * NTG + tg
                        v.wait_ge(s_if, if_v(h, tg))
                        if ig >= 2:
                            v.wait_ge(s_vst[ig % 2], 16 * (ig // 2))
                        v.tensor_copy(vhs[ig % 2][:], p_if[ig % 2][:]).then_inc(s_ie, 1)
    return nc


def build_moe(TT=T, n_active_cores=NCORE, CAP=2560.0):
    NTt = TT // 128
    HS = NTt // 2
    TH = TT // 2
    NE = 8
    TW1 = 512 if TH % 512 == 0 else TH
    NG1 = TH // TW1
    nc = bass.Bass("TRN2", target_bir_lowering=False)
    x = nc.dram_tensor("x", [TT, D], F32, kind="ExternalInput").ap()
    gt = nc.dram_tensor("gt", [16, 128, 8], F32, kind="ExternalInput").ap()
    w1t = nc.dram_tensor("w1t", [8, 16, 16, 128, 128], F32, kind="ExternalInput").ap()
    w2t = nc.dram_tensor("w2t", [8, 4, 16, 128, 512], F32, kind="ExternalInput").ap()
    iota8 = nc.dram_tensor("iota8", [128, 8], F32, kind="ExternalInput").ap()
    iotabig = nc.dram_tensor("iotabig", [128, 8], F32, kind="ExternalInput").ap()
    uf32 = nc.dram_tensor("uf32", [128, 128], F32, kind="ExternalInput").ap()
    onescf = nc.dram_tensor("onescf", [128, 1], F32, kind="ExternalInput").ap()
    onesrf = nc.dram_tensor("onesrf", [1, 128], F32, kind="ExternalInput").ap()
    idn = nc.dram_tensor("idn", [128, 128], BF16, kind="ExternalInput").ap()
    flags = nc.dram_tensor("flags", [1, 2], F32, kind="ExternalInput").ap()
    pcolf = nc.dram_tensor("pcolf", [n_active_cores, 1], F32, kind="ExternalInput").ap()
    y = nc.dram_tensor("y", [TT, D], F32, kind="ExternalOutput").ap()
    cc_in = nc.dram_tensor("cc_in", [1, 16], F32).ap()
    cc_out = nc.dram_tensor("cc_out", [n_active_cores, 16], F32).ap()

    es = ExitStack()
    sb = lambda n, s, d: es.enter_context(nc.sbuf_tensor(n, s, d))
    ps = lambda n, s, d: es.enter_context(nc.psum_tensor(n, s, d))
    sem = lambda n: es.enter_context(nc.semaphore(n))
    with es:
        xb = sb("xb", [128, NTt * D], BF16)
        xf = sb("xf", [128, D], F32)
        xlo = sb("xlo", [128, D], BF16)
        gtf = sb("gtf", [128, 128], F32)
        ghi = sb("ghi", [128, 128], BF16)
        glo = sb("glo", [128, 128], BF16)
        io8 = sb("io8", [128, 8], F32)
        iobig = sb("iobig", [128, 8], F32)
        uf_s = sb("uf_s", [128, 128], F32)
        ocf = sb("ocf", [128, 1], F32)
        orf = sb("orf", [1, 128], F32)
        id_s = sb("id_s", [128, 128], BF16)
        fl_s = sb("fl_s", [1, 2], F32)
        pc_s = sb("pc_s", [n_active_cores, 1], F32)
        xtr = [sb(f"xtr{i}", [128, 16 * 128], BF16) for i in range(2)]
        xlotr = sb("xlotr", [128, 16 * 128], BF16)
        xth = sb("xth", [128, 16 * TH], BF16)
        lg = sb("lg", [128, 8], F32)
        pu = sb("pu", [128, 8], F32)
        mx = sb("mx", [128, 4], F32)
        equ = sb("equ", [128, 8], F32)
        zed = sb("zed", [128, 8], F32)
        idxc = sb("idxc", [128, 1], F32)
        m1b = sb("m1b", [128, NTt * 8], F32)
        pum = sb("pum", [128, NTt * 8], F32)
        rsmb = sb("rsmb", [128, NTt], F32)
        rkl = sb("rkl", [128, NTt * 8], F32)
        carry8 = sb("carry8", [1, 8], F32)
        contrib = sb("contrib", [1, 16], F32)
        ccs = sb("ccs", [n_active_cores, 16], F32)
        prow16 = sb("prow16", [1, 16], F32)
        c1c = sb("c1c", [1, 8], F32)
        c2c = sb("c2c", [1, 8], F32)
        thr = sb("thr", [1, 16], F32)
        threp = sb("threp", [128, 16], F32)
        keptm = sb("keptm", [128, 8], F32)
        wem = sb("wem", [128, NTt * 8], F32)
        hch = sb("hch", [128, 16 * TH], BF16)
        w1s = [sb(f"w1s{i}", [128, 16 * 128], BF16) for i in range(2)]
        w2s = [sb(f"w2s{i}", [128, 16 * 512], BF16) for i in range(2)]
        acc8 = sb("acc8", [128, 2], F32)
        stat = sb("stat", [128, 4], F32)
        epsc = sb("epsc", [128, 1], F32)
        yo = [sb("yo0", [128, D], F32)] * 2
        p_gt = ps("p_gt", [128, 8], F32)
        p_rkt = ps("p_rkt", [128, 16], F32)
        p_tr = [ps(f"p_tr{i}", [128, 128], BF16) for i in range(2)]
        p1 = [ps(f"p1_{i}", [128, 512], F32) for i in range(2)]
        p2 = [ps(f"p2_{i}", [128, 512], F32) for i in range(2)]
        s_cl = sem("s_cl")
        s_x = [sem(f"s_x{i}") for i in range(2)]
        s_vc = sem("s_vc")
        s_petr = sem("s_petr")
        s_trev = sem("s_trev")
        s_gmm = sem("s_gmm")
        s_rt = sem("s_rt")
        s_ax = sem("s_ax")
        s_rk = sem("s_rk")
        s_rke = sem("s_rke")
        s_cc = sem("s_cc")
        s_cc2 = sem("s_cc2")
        s_cc3 = sem("s_cc3")
        s_pw = sem("s_pw")
        s_eptv = sem("s_eptv")
        s_w1 = [sem(f"s_w1{i}") for i in range(2)]
        s_w2 = [sem(f"s_w2{i}") for i in range(2)]
        s_m1 = sem("s_m1")
        s_ge = sem("s_ge")
        s_m2 = sem("s_m2")
        s_ac = sem("s_ac")
        s_vs = sem("s_vs")
        s_lnx = sem("s_lnx")
        s_rs = sem("s_rs")
        s_no = sem("s_no")
        s_st = [sem(f"s_st{i}") for i in range(2)]
        block = es.enter_context(nc.Block())

        vc_v = lambda t: 2 + 2 * (t + 1)
        rt_v = lambda t, k: 10 * t + k
        rke_ev = lambda t: 1 + 2 * t + 1 + (1 if t >= HS else 0)
        rke_cu = lambda t: 1 + 2 * t + 2 + (1 if t >= HS else 0)
        RKE_TOT = 2 * NTt + 3
        m1_v = lambda th, e, ic, tg: (((th * NE + e) * 16 + ic) * NG1) + tg + 1
        M1_TOT = 2 * NE * 16 * NG1
        m2_v = lambda th, e, os_, tc: (((th * NE + e) * 4 + os_) * HS) + tc + 1
        M2_TOT = 2 * NE * HS * 4
        PW_TOT = 6 + 3 * NTt
        geval = lambda th, e, ic, tg: m1_v(th, e, ic, tg)

        @block.sync
        def _(sync):
            cl = Ctr()
            for (dst, srcap) in ((gtf[:].rearrange("p (dc e) -> p dc e", e=8), gt.rearrange("dc p e -> p dc e")),
                                 (io8[:], iota8), (iobig[:], iotabig), (uf_s[:], uf32),
                                 (ocf[:], onescf), (orf[:], onesrf), (id_s[:], idn),
                                 (fl_s[:], flags), (pc_s[:], pcolf)):
                sync.dma_start(out=dst, in_=srcap).then_inc(s_cl, 16)
                sync.wait_ge(s_cl, cl.inc(16))
            for t in range(NTt):
                sync.wait_ge(s_no, t + 1)
                sync.dma_start(out=y[t * 128:(t + 1) * 128, :],
                               in_=yo[t % 2][:]).then_inc(s_st[t % 2], 16)
            sync.wait_ge(s_st[0], 16 * ((NTt + 1) // 2))
            sync.wait_ge(s_st[1], 16 * (NTt // 2))

        @block.gpsimd
        def _(g):
            for t in range(NTt):
                if t >= 1:
                    g.wait_ge(s_vc, vc_v(t - 1))
                if t >= 2:
                    g.wait_ge(s_x[t % 2], 16 * (t // 2))
                g.dma_start(out=xf[:], in_=x[t * 128:(t + 1) * 128, :]).then_inc(s_x[t % 2], 16)
            g.wait_ge(s_rke, RKE_TOT)
            g.dma_start(out=cc_in, in_=contrib[:]).then_inc(s_cc, 16)
            g.wait_ge(s_cc, 16)
            g.collective_compute("AllGather", ALU.bypass,
                                 replica_groups=[list(range(n_active_cores))],
                                 ins=[cc_in], outs=[cc_out]).then_inc(s_cc2)
            g.wait_ge(s_cc2, 1)
            g.dma_start(out=ccs[:], in_=cc_out).then_inc(s_cc3, 16)
            for th in range(2):
                for e in range(NE):
                    for ic in range(16):
                        k = (th * NE + e) * 16 + ic
                        if k >= 2:
                            g.wait_ge(s_m1, (k - 2) * NG1 + NG1)
                        g.dma_start(out=w1s[k % 2][:].rearrange("p (dc c) -> p dc c", c=128),
                                    in_=w1t[e, ic].rearrange("dc p c -> p dc c")).then_inc(s_w1[k % 2], 16)
                    for os_ in range(4):
                        k2 = (th * NE + e) * 4 + os_
                        if k2 >= 2:
                            ko = k2 - 2
                            oth, rem = divmod(ko, NE * 4)
                            oe, oos = divmod(rem, 4)
                            g.wait_ge(s_m2, m2_v(oth, oe, oos, HS - 1))
                        g.dma_start(out=w2s[k2 % 2][:].rearrange("p (ic c) -> p ic c", c=512),
                                    in_=w2t[e, os_].rearrange("ic p c -> p ic c")).then_inc(s_w2[k2 % 2], 16)

        @block.tensor
        def _(te):
            te.wait_ge(s_cl, 16 * 9)
            ktr = 0
            for t in range(NTt):
                te.wait_ge(s_vc, vc_v(t))
                for which in range(2):
                    for dc in range(16):
                        if ktr >= 2:
                            te.wait_ge(s_trev, ktr - 1)
                        srcap = (xb[:, t * D + dc * 128: t * D + (dc + 1) * 128] if which == 0
                                 else xlo[:, dc * 128:(dc + 1) * 128])
                        te.transpose(p_tr[ktr % 2][:], srcap, id_s[:]).then_inc(s_petr, 1)
                        ktr += 1
                te.wait_ge(s_trev, (t + 1) * 32)
                if t >= 1:
                    te.wait_ge(s_rt, rt_v(t - 1, 1))
                n_emitted = 0
                for grp, buf in ((0, None), (1, None), (2, None)):
                    for dc in range(16):
                        lhs = (xtr[t % 2] if grp != 1 else xlotr)[:, dc * 128:(dc + 1) * 128]
                        rhs = (ghi if grp != 1 else ghi) if grp != 2 else glo
                        rhs = rhs[:, dc * 8:(dc + 1) * 8]
                        last = (grp == 2 and dc == 15)
                        mi = te.matmul(p_gt[:], lhs, rhs, start=(n_emitted == 0), stop=last)
                        n_emitted += 1
                mi.then_inc(s_gmm, 1)
                te.wait_ge(s_rt, rt_v(t, 9))
                te.wait_ge(s_rke, rke_cu(t - 1) if t >= 1 else 1)
                te.matmul(p_rkt[:, 0:8], uf_s[:], m1b[:, t * 8:(t + 1) * 8],
                          start=True, stop=False)
                te.matmul(p_rkt[:, 0:8], orf[:], carry8[:], start=False, stop=True)
                te.matmul(p_rkt[0:1, 8:16], ocf[:], m1b[:, t * 8:(t + 1) * 8],
                          start=True, stop=True).then_inc(s_rk, 1)
            te.wait_ge(s_cc3, 16)
            te.wait_ge(s_rke, RKE_TOT)
            te.matmul(p_rkt[0:1, 0:16], pc_s[:], ccs[:], start=True, stop=True).then_inc(s_rk, 1)
            te.wait_ge(s_pw, 5)
            te.matmul(p_rkt[:, 0:16], orf[:], thr[:], start=True, stop=True).then_inc(s_rk, 1)
            for th in range(2):
                if th == 1:
                    te.wait_ge(s_m1, m1_v(0, NE - 1, 15, NG1 - 1))
                for tc in range(HS):
                    t = th * HS + tc
                    for dc in range(16):
                        if ktr >= 2:
                            te.wait_ge(s_trev, ktr - 1)
                        te.transpose(p_tr[ktr % 2][:], xb[:, t * D + dc * 128: t * D + (dc + 1) * 128],
                                     id_s[:]).then_inc(s_petr, 1)
                        ktr += 1
                te.wait_ge(s_trev, 32 * NTt + (th * HS + HS) * 16)
                for e in range(NE):
                    for ic in range(16):
                        k = (th * NE + e) * 16 + ic
                        te.wait_ge(s_w1[k % 2], 16 * (k // 2 + 1))
                        for tg in range(NG1):
                            gi = m1_v(th, e, ic, tg)
                            if gi >= 3:
                                te.wait_ge(s_ge, gi - 2)
                            for dc in range(16):
                                mi = te.matmul(p1[gi % 2][:, 0:TW1],
                                               w1s[k % 2][:, dc * 128:(dc + 1) * 128],
                                               xth[:, dc * TH + tg * TW1: dc * TH + tg * TW1 + TW1],
                                               start=(dc == 0), stop=(dc == 15))
                            mi.then_inc(s_m1, 1)
                    te.wait_ge(s_ge, m1_v(th, e, 15, NG1 - 1))
                    for os_ in range(4):
                        te.wait_ge(s_w2[((th * NE + e) * 4 + os_) % 2],
                                   16 * (((th * NE + e) * 4 + os_) // 2 + 1))
                        for tc in range(HS):
                            gi2 = m2_v(th, e, os_, tc)
                            if gi2 >= 3:
                                te.wait_ge(s_ac, gi2 - 2)
                            for ic in range(16):
                                mi = te.matmul(p2[gi2 % 2][:],
                                               hch[:, ic * TH + tc * 128: ic * TH + tc * 128 + 128],
                                               w2s[((th * NE + e) * 4 + os_) % 2][:, ic * 512:(ic + 1) * 512],
                                               start=(ic == 0), stop=(ic == 15))
                            mi.then_inc(s_m2, 1)

        @block.scalar
        def _(sc):
            sc.wait_ge(s_cl, 16 * 9)
            ktr = 0
            for t in range(NTt):
                for k2 in range(32):
                    sc.wait_ge(s_petr, ktr + 1)
                    dst = (xtr[t % 2][:, (k2 % 16) * 128:(k2 % 16 + 1) * 128] if k2 < 16
                           else xlotr[:, (k2 - 16) * 128:(k2 - 15) * 128])
                    if k2 < 16 and t >= 2:
                        sc.wait_ge(s_gmm, t - 1)
                    if k2 == 16 and t >= 1:
                        sc.wait_ge(s_gmm, t)
                    sc.copy(dst, p_tr[ktr % 2][:]).then_inc(s_trev, 1)
                    ktr += 1
                sc.wait_ge(s_rt, rt_v(t, 3))
                sc.activation(pu[:], lg[:], AF.Exp, bias=mx[:, 1:2]).then_inc(s_ax, 1)
            for th in range(2):
                for tc in range(HS):
                    for dc in range(16):
                        sc.wait_ge(s_petr, ktr + 1)
                        t = th * HS + tc
                        sc.copy(xth[:, dc * TH + tc * 128: dc * TH + (tc + 1) * 128],
                                p_tr[ktr % 2][:]).then_inc(s_trev, 1)
                        ktr += 1
                for e in range(NE):
                    if e >= 1 or th >= 1:
                        pe = (th, e - 1) if e >= 1 else (th - 1, NE - 1)
                        sc.wait_ge(s_m2, m2_v(pe[0], pe[1], 3, HS - 1))
                    for ic in range(16):
                        for tg in range(NG1):
                            gi = m1_v(th, e, ic, tg)
                            sc.wait_ge(s_m1, gi)
                            sc.activation(hch[:, ic * TH + tg * TW1: ic * TH + tg * TW1 + TW1],
                                          p1[gi % 2][:, 0:TW1], AF.Gelu).then_inc(s_ge, 1)
            for t in range(NTt):
                sc.wait_ge(s_vs, 5 * (t + 1))
                sc.activation(stat[:, 2:3], stat[:, 2:3], AF.Ln, bias=epsc[:]).then_inc(s_lnx, 1)
                sc.wait_ge(s_lnx, t + 1)
                sc.activation(stat[:, 3:4], stat[:, 2:3], AF.Exp, scale=-0.5).then_inc(s_rs, 1)

        @block.vector
        def _(v):
            v.wait_ge(s_cl, 16 * 9)
            v.memset(epsc[:], EPS)
            v.memset(carry8[:], 0.0).then_inc(s_rke, 1)
            v.tensor_copy(ghi[:], gtf[:]).then_inc(s_vc, 1)
            v.wait_ge(s_vc, 1)
            v.scalar_tensor_tensor(glo[:], gtf[:], 1.0, ghi[:], ALU.mult, ALU.subtract).then_inc(s_vc, 1)
            for t in range(NTt):
                v.wait_ge(s_x[t % 2], 16 * (t // 2 + 1))
                if t >= 1:
                    v.wait_ge(s_petr, 32 * t)
                v.tensor_copy(xb[:, t * D:(t + 1) * D], xf[:]).then_inc(s_vc, 1)
                v.wait_ge(s_vc, vc_v(t) - 1)
                v.scalar_tensor_tensor(xlo[:], xf[:], 1.0, xb[:, t * D:(t + 1) * D],
                                       ALU.mult, ALU.subtract).then_inc(s_vc, 1)
                v.wait_ge(s_gmm, t + 1)
                if t >= 1:
                    v.wait_ge(s_ax, t)
                v.tensor_copy(lg[:], p_gt[:]).then_inc(s_rt, 1)
                v.wait_ge(s_rt, rt_v(t, 1))
                v.tensor_reduce(mx[:, 0:1], lg[:], AX.X, ALU.max).then_inc(s_rt, 1)
                v.wait_ge(s_rt, rt_v(t, 2))
                v.tensor_scalar(mx[:, 1:2], mx[:, 0:1], -1.0, None, ALU.mult).then_inc(s_rt, 1)
                v.wait_ge(s_ax, t + 1)
                v.tensor_reduce(mx[:, 2:3], pu[:], AX.X, ALU.add).then_inc(s_rt, 1)
                v.wait_ge(s_rt, rt_v(t, 4))
                v.reciprocal(rsmb[:, t:t + 1], mx[:, 2:3]).then_inc(s_rt, 1)
                v.tensor_scalar(equ[:], lg[:], mx[:, 0:1], None, ALU.is_equal).then_inc(s_rt, 1)
                v.wait_ge(s_rt, rt_v(t, 6))
                v.scalar_tensor_tensor(zed[:], equ[:], -1000.0, iobig[:], ALU.mult, ALU.add).then_inc(s_rt, 1)
                v.wait_ge(s_rt, rt_v(t, 7))
                v.tensor_reduce(idxc[:], zed[:], AX.X, ALU.min).then_inc(s_rt, 1)
                v.wait_ge(s_rt, rt_v(t, 8))
                v.tensor_scalar(m1b[:, t * 8:(t + 1) * 8], io8[:], idxc[:], None,
                                ALU.is_equal).then_inc(s_rt, 1)
                v.wait_ge(s_rt, rt_v(t, 9))
                v.tensor_tensor(pum[:, t * 8:(t + 1) * 8], pu[:], m1b[:, t * 8:(t + 1) * 8],
                                ALU.mult).then_inc(s_rt, 1)
                v.wait_ge(s_rk, t + 1)
                v.tensor_copy(rkl[:, t * 8:(t + 1) * 8], p_rkt[:, 0:8]).then_inc(s_rke, 1)
                v.wait_ge(s_rke, rke_ev(t))
                v.tensor_tensor(carry8[:], carry8[:], p_rkt[0:1, 8:16], ALU.add).then_inc(s_rke, 1)
                
                if t == HS - 1:
                    v.wait_ge(s_rke, rke_cu(t))
                    v.tensor_copy(contrib[:, 0:8], carry8[:]).then_inc(s_rke, 1)
                if t == NTt - 1:
                    v.wait_ge(s_rke, rke_cu(t))
                    v.tensor_copy(contrib[:, 8:16], carry8[:]).then_inc(s_rke, 1)
            v.wait_ge(s_rk, NTt + 1)
            v.tensor_copy(prow16[:], p_rkt[0:1, 0:16]).then_inc(s_pw, 1)
            v.wait_ge(s_pw, 1)
            v.tensor_scalar(c1c[:], prow16[:, 0:8], fl_s[:, 0:1], None, ALU.mult).then_inc(s_pw, 1)
            v.scalar_tensor_tensor(c2c[:], prow16[:, 8:16], fl_s[:, 1:2], prow16[:, 0:8],
                                   ALU.mult, ALU.add).then_inc(s_pw, 1)
            v.wait_ge(s_pw, 3)
            v.tensor_scalar(thr[:, 0:8], c1c[:], -1.0, CAP, ALU.mult, ALU.add).then_inc(s_pw, 1)
            v.tensor_scalar(thr[:, 8:16], c2c[:], -1.0, CAP, ALU.mult, ALU.add).then_inc(s_pw, 1)
            v.wait_ge(s_rk, NTt + 2)
            v.tensor_copy(threp[:], p_rkt[:, 0:16]).then_inc(s_pw, 1)
            for t in range(NTt):
                seg = 0 if t < HS else 1
                v.wait_ge(s_pw, 6 + 3 * t)
                v.tensor_tensor(keptm[:], rkl[:, t * 8:(t + 1) * 8],
                                threp[:, seg * 8:(seg + 1) * 8], ALU.is_le).then_inc(s_pw, 1)
                v.wait_ge(s_pw, 6 + 3 * t + 1)
                v.tensor_tensor(wem[:, t * 8:(t + 1) * 8], pum[:, t * 8:(t + 1) * 8],
                                keptm[:], ALU.mult).then_inc(s_pw, 1)
                v.wait_ge(s_pw, 6 + 3 * t + 2)
                v.tensor_scalar(wem[:, t * 8:(t + 1) * 8], wem[:, t * 8:(t + 1) * 8],
                                rsmb[:, t:t + 1], None, ALU.mult).then_inc(s_pw, 1)
            v.wait_ge(s_pw, PW_TOT)
            for th in range(2):
                for e in range(NE):
                    for os_ in range(4):
                        for tc in range(HS):
                            t = th * HS + tc
                            gi2 = m2_v(th, e, os_, tc)
                            v.wait_ge(s_m2, gi2)
                            v.scalar_tensor_tensor(
                                xb[:, t * D + os_ * 512: t * D + os_ * 512 + 512],
                                p2[gi2 % 2][:], wem[:, t * 8 + e: t * 8 + e + 1],
                                xb[:, t * D + os_ * 512: t * D + os_ * 512 + 512],
                                ALU.mult, ALU.add).then_inc(s_ac, 1)
            for t in range(NTt):
                if t >= 1:
                    v.wait_ge(s_rs, t)
                    if t - 1 >= 1:
                        v.wait_ge(s_st[(t - 2) % 2], 16 * ((t - 2) // 2 + 1))
                    v.tensor_scalar(yo[(t - 1) % 2][:], xb[:, (t - 1) * D: t * D],
                                    stat[:, 0:1], stat[:, 3:4],
                                    ALU.subtract, ALU.mult).then_inc(s_no, 1)
                th = t // HS
                v.wait_ge(s_ac, m2_v(th, NE - 1, 3, t % HS))
                if t >= 1:
                    v.wait_ge(s_vs, 5 * t)
                v.tensor_scalar(xlo[:], xb[:, t * D:(t + 1) * D], 1.0, 0.0, ALU.mult,
                                ALU.add, accum_out=acc8[:, 0:1]).then_inc(s_vs, 1)
                v.wait_ge(s_vs, 5 * t + 1)
                v.scalar_tensor_tensor(xlo[:], xb[:, t * D:(t + 1) * D], 1.0,
                                       xb[:, t * D:(t + 1) * D], ALU.mult, ALU.mult,
                                       accum_out=acc8[:, 1:2]).then_inc(s_vs, 1)
                v.wait_ge(s_vs, 5 * t + 2)
                v.tensor_scalar(stat[:, 0:1], acc8[:, 0:1], 1.0 / D, None, ALU.mult).then_inc(s_vs, 1)
                v.wait_ge(s_vs, 5 * t + 3)
                v.tensor_tensor(stat[:, 1:2], stat[:, 0:1], stat[:, 0:1], ALU.mult).then_inc(s_vs, 1)
                v.wait_ge(s_vs, 5 * t + 4)
                v.scalar_tensor_tensor(stat[:, 2:3], acc8[:, 1:2], 1.0 / D, stat[:, 1:2],
                                       ALU.mult, ALU.subtract).then_inc(s_vs, 1)
            t = NTt - 1
            v.wait_ge(s_rs, NTt)
            v.wait_ge(s_st[(t - 1) % 2], 16 * ((t - 1) // 2 + 1))
            v.tensor_scalar(yo[t % 2][:], xb[:, t * D:(t + 1) * D], stat[:, 0:1], stat[:, 3:4],
                            ALU.subtract, ALU.mult).then_inc(s_no, 1)
    return nc


def l1b_sched(TT):
    NTt = TT // 128
    H = NTt // 2
    NG = NTt // 4
    groups = []
    for g in range(NG):
        js = range(4 * g, 4 * g + 4)
        gqA = [j if j < H else j + NTt for j in js]
        gqB = [j + H for j in js]
        ext = max(gqA + gqB) + 1
        crossing = sorted(set(gqA) | set(gqB))
        groups.append((ext, crossing))
    return groups

def build_l1b(TT=T, variant=''):
    NTt = TT // 128
    SF = 2 * TT
    NKC = SF // 128
    NG = NTt // 4
    groups = l1b_sched(TT)
    NCROSS = len(groups[0][1])
    OS = 512
    NOS = D // OS
    EXPB = -20.0

    nc = bass.Bass("TRN2", target_bir_lowering=False)
    q = nc.dram_tensor("q", [TT, D], BF16, kind="ExternalInput").ap()
    kfull = nc.dram_tensor("kfull", [SF, D], BF16, kind="ExternalInput").ap()
    vfull = nc.dram_tensor("vfull", [SF, D], BF16, kind="ExternalInput").ap()
    x = nc.dram_tensor("x", [TT, D], F32, kind="ExternalInput").ap()
    wout = nc.dram_tensor("wout", [16, 128, D], F32, kind="ExternalInput").ap()
    masks = nc.dram_tensor("masks", [NG, NCROSS, 128, 512], BF16, kind="ExternalInput").ap()
    biastab = nc.dram_tensor("biastab", [128, NKC * NG], F32, kind="ExternalInput").ap()
    idn = nc.dram_tensor("idn", [128, 128], BF16, kind="ExternalInput").ap()
    y = nc.dram_tensor("y", [TT, D], F32, kind="ExternalOutput").ap()
    aot = nc.dram_tensor("aot", [16, NTt, 128, 128], BF16).ap()

    KT0 = 0
    QT0 = 2 * SF
    V0 = QT0 + 2 * TT
    VSL = 132
    MK0 = V0 + 2 * (NKC * VSL)
    BIGN = max(MK0 + NG * NCROSS * 512, 16 * D)
    WOUT0 = 0

    es = ExitStack()
    sb = lambda n, s, d: es.enter_context(nc.sbuf_tensor(n, s, d))
    ps = lambda n, s, d: es.enter_context(nc.psum_tensor(n, s, d))
    sem = lambda n: es.enter_context(nc.semaphore(n))
    with es:
        big = sb("big", [128, BIGN], BF16)
        xb = sb("xb", [128, NTt * D], BF16)
        bias_s = sb("bias_s", [128, NKC * NG], F32)
        id_s = sb("id_s", [128, 128], BF16)
        psl = [sb(f"psl{i}", [128, 512], BF16) for i in range(2)]
        linv = sb("linv", [128, 1], F32)
        ao = [sb(f"ao{i}", [128, 128], BF16) for i in range(2)]
        aotb = [sb(f"aotb{i}", [128, 512], BF16) for i in range(2)]
        vhb = [sb(f"vhb{i}", [128, 16 * 128], BF16) for i in range(2)]
        ssl = [sb(f"ssl{i}", [128, D], BF16) for i in range(2)]
        sqs = sb("sqs", [128, D], BF16)
        acc8 = sb("acc8", [128, 8], F32)
        stat = sb("stat", [128, 4], F32)
        epsc = sb("epsc", [128, 1], F32)
        yo = [sb(f"yo{i}", [128, D], F32) for i in range(2)]
        p_s = [ps(f"p_s{i}", [128, 512], F32) for i in range(2)]
        p_av = [ps(f"p_av{i}", [128, 512], F32) for i in range(4)]
        p_tr = [ps(f"p_tr{i}", [128, 128], BF16) for i in range(2)]
        s_cl = sem("s_cl")
        s_mk = sem("s_mk")
        s_ini = sem("s_ini")
        s_kq = [sem(f"s_kq{i}") for i in range(2)]
        s_vl = [sem(f"s_vl{i}") for i in range(2)]
        s_x = [sem(f"s_x{i}") for i in range(2)]
        s_sim = sem("s_sim")
        s_px = sem("s_px")
        s_msk = sem("s_msk")
        s_av = sem("s_av")
        s_ao = sem("s_ao")
        s_petr = sem("s_petr")
        s_trev = sem("s_trev")
        s_aost = [sem(f"s_aost{i}") for i in range(2)]
        s_wl = sem("s_wl")
        s_vhl = [sem(f"s_vhl{i}") for i in range(2)]
        s_mm = sem("s_mm")
        s_ev = sem("s_ev")
        s_vq = sem("s_vq")
        s_vs = sem("s_vs")
        s_lnx = sem("s_lnx")
        s_rs = sem("s_rs")
        s_no = sem("s_no")
        s_st = [sem(f"s_st{i}") for i in range(2)]
        block = es.enter_context(nc.Block())

        def head_sched():
            out = []
            for g, (ext, crossing) in enumerate(groups):
                for kt in range(ext):
                    out.append((g, kt, kt in crossing, crossing.index(kt) if kt in crossing else -1))
            return out
        SCH = head_sched()
        NSIM = len(SCH)
        NMSK_H = sum(1 for (_g, _kt, c, _i) in SCH if c)

        @block.sync
        def _(sync):
            sync.dma_start(out=id_s[:], in_=idn).then_inc(s_cl, 16)
            sync.wait_ge(s_cl, 16)
            sync.dma_start(out=bias_s[:], in_=biastab).then_inc(s_cl, 16)
            sync.wait_ge(s_cl, 32)
            for h in range(16):
                if h >= 2 and "loadsonly" not in variant:
                    sync.wait_ge(s_av, (h - 1) * 4 * NSIM)
                sync.wait_ge(s_kq[h % 2], 32 * (h // 2))
                if "notrdma" in variant:
                    sync.dma_start(out=big[:, KT0 + (h % 2) * SF: KT0 + (h % 2) * SF + 2048],
                                   in_=kfull[0:128, :]).then_inc(s_kq[h % 2], 16)
                else:
                    sync.dma_start(out=big[:, KT0 + (h % 2) * SF: KT0 + (h % 2) * SF + SF],
                                   in_=kfull[:, h * 128:(h + 1) * 128], transpose=True).then_inc(s_kq[h % 2], 16)
                sync.wait_ge(s_kq[h % 2], 32 * (h // 2) + 16)
                if "notrdma" in variant:
                    sync.dma_start(out=big[:, QT0 + (h % 2) * TT: QT0 + (h % 2) * TT + TT],
                                   in_=q[0:128, 0:TT]).then_inc(s_kq[h % 2], 16)
                else:
                    sync.dma_start(out=big[:, QT0 + (h % 2) * TT: QT0 + (h % 2) * TT + TT],
                                   in_=q[:, h * 128:(h + 1) * 128], transpose=True).then_inc(s_kq[h % 2], 16)
                if h >= 1 and "loadsonly" not in variant:
                    hh = h - 1
                    for g in range(NG):
                        ig = hh * NG + g
                        sync.wait_ge(s_trev, hh * 4 * NG + (g + 1) * 4)
                        sync.wait_ge(s_aost[ig % 2], 16 * (ig // 2))
                        sync.dma_start(out=aot[hh, g * 4:(g + 1) * 4].rearrange("tc p c -> p tc c"),
                                       in_=aotb[ig % 2][:].rearrange("p (tc c) -> p tc c", c=128)
                                       ).then_inc(s_aost[ig % 2], 16)
            hh = 15
            for g in range(NG if "loadsonly" not in variant else 0):
                ig = hh * NG + g
                sync.wait_ge(s_trev, hh * 4 * NG + (g + 1) * 4)
                sync.wait_ge(s_aost[ig % 2], 16 * (ig // 2))
                sync.dma_start(out=aot[hh, g * 4:(g + 1) * 4].rearrange("tc p c -> p tc c"),
                               in_=aotb[ig % 2][:].rearrange("p (tc c) -> p tc c", c=128)
                               ).then_inc(s_aost[ig % 2], 16)
            if "loadsonly" in variant:
                sync.wait_ge(s_kq[0], 32 * 8)
                sync.wait_ge(s_kq[1], 32 * 8)
                if "novload" not in variant:
                    sync.wait_ge(s_vl[0], 16 * 32)
                    sync.wait_ge(s_vl[1], 16 * 32)
            elif "nophasec" in variant:
                sync.wait_ge(s_aost[0], 16 * ((16 * NG + 1) // 2))
                sync.wait_ge(s_aost[1], 16 * (16 * NG // 2))
            else:
                for t in range(NTt):
                    sync.wait_ge(s_no, t + 1)
                    sync.dma_start(out=y[t * 128:(t + 1) * 128, :],
                                   in_=yo[t % 2][:]).then_inc(s_st[t % 2], 16)
                sync.wait_ge(s_st[0], 16 * ((NTt + 1) // 2))
                sync.wait_ge(s_st[1], 16 * (NTt // 2))

        @block.gpsimd
        def _(g_):
            for gg in range(NG):
                g_.wait_ge(s_mk, 16 * gg)
                g_.dma_start(out=big[:, MK0 + gg * NCROSS * 512: MK0 + (gg + 1) * NCROSS * 512]
                             .rearrange("p (a c) -> p a c", c=512),
                             in_=masks[gg].rearrange("s p c -> p s c")).then_inc(s_mk, 16)
            for t in range(NTt):
                if t >= 2:
                    g_.wait_ge(s_x[t % 2], 16 * (t // 2))
                g_.dma_start(out=xb[:, t * D:(t + 1) * D],
                             in_=x[t * 128:(t + 1) * 128, :]).then_inc(s_x[t % 2], 16)
            g_.wait_ge(s_ini, 1)
            for h in range(16 if "novload" not in variant else 0):
                if h >= 2 and "loadsonly" not in variant:
                    g_.wait_ge(s_av, (h - 1) * 4 * NSIM)
                base = V0 + (h % 2) * (NKC * VSL)
                for part in range(4):
                    kc0 = part * (NKC // 4)
                    g_.wait_ge(s_vl[h % 2], 16 * (4 * (h // 2) + part))
                    g_.dma_start(
                        out=big[:, base + kc0 * VSL: base + (kc0 + NKC // 4) * VSL]
                            .rearrange("p (kc c) -> p kc c", c=VSL)[:, :, 0:128],
                        in_=vfull[kc0 * 128:(kc0 + NKC // 4) * 128, h * 128:(h + 1) * 128]
                            .rearrange("(kc p) c -> p kc c", p=128)).then_inc(s_vl[h % 2], 16)
            if "nophasec" in variant:
                return
            g_.wait_ge(s_trev, 16 * 4 * NG)
            for dc in range(16):
                g_.wait_ge(s_wl, 16 * dc)
                g_.dma_start(out=big[:, WOUT0 + dc * D: WOUT0 + (dc + 1) * D],
                             in_=wout[dc]).then_inc(s_wl, 16)
            g_.wait_ge(s_aost[0], 16 * ((16 * NG + 1) // 2))
            g_.wait_ge(s_aost[1], 16 * (16 * NG // 2))
            for t in range(NTt):
                if t >= 2:
                    g_.wait_ge(s_mm, (t - 1) * NOS)
                    g_.wait_ge(s_vhl[t % 2], 16 * (t // 2))
                g_.dma_start(out=vhb[t % 2][:].rearrange("p (dc c) -> p dc c", c=128),
                             in_=aot[:, t].rearrange("dc p c -> p dc c")).then_inc(s_vhl[t % 2], 16)

        @block.tensor
        def _(te):
            te.wait_ge(s_cl, 32)
            if "loadsonly" in variant:
                te.matmul(p_s[0][:, 0:128], id_s[:], id_s[:], start=True, stop=True)
                return
            nsim = 0; nav = 0; nmsk = 0; npx = 0; nao = 0; ntr = 0
            for h in range(16):
                te.wait_ge(s_kq[h % 2], 32 * (h // 2) + 32)
                te.wait_ge(s_vl[h % 2], 16 * (4 * (h // 2) + 4))
                kto = KT0 + (h % 2) * SF
                qto = QT0 + (h % 2) * TT
                vbase = V0 + (h % 2) * (NKC * VSL)
                for g, (ext, crossing) in enumerate(groups):
                    for kt in range(ext):
                        iscross = kt in crossing
                        if nsim >= 2:
                            te.wait_ge(s_px, nsim - 1)
                        te.matmul(p_s[nsim % 2][:], big[:, kto + kt * 128: kto + (kt + 1) * 128],
                                  big[:, qto + g * 512: qto + (g + 1) * 512],
                                  start=True, stop=True).then_inc(s_sim, 1)
                        nsim += 1
                        if iscross:
                            nmsk += 1
                            te.wait_ge(s_msk, nmsk)
                        else:
                            te.wait_ge(s_px, nsim)
                        if kt == 0:
                            te.wait_ge(s_ao, (h * NG + g) * 8)
                        for j in range(4):
                            mi = te.matmul(p_av[j][:, 0:132],
                                           psl[(nsim - 1) % 2][:, j * 128:(j + 1) * 128],
                                           big[:, vbase + kt * VSL: vbase + kt * VSL + 132]
                                           ,
                                           start=(kt == 0), stop=(kt == ext - 1))
                            mi.then_inc(s_av, 1)
                            nav += 1
                    for j in range(4):
                        te.wait_ge(s_ao, (h * NG + g) * 8 + 2 * (j + 1))
                        if ntr >= 2:
                            te.wait_ge(s_trev, ntr - 1)
                        te.transpose(p_tr[ntr % 2][:], ao[j % 2][:], id_s[:]).then_inc(s_petr, 1)
                        ntr += 1
            if "nophasec" in variant:
                return
            te.wait_ge(s_wl, 16 * 16)
            for t in range(NTt):
                te.wait_ge(s_vhl[t % 2], 16 * (t // 2 + 1))
                for o in range(NOS):
                    gidx = t * NOS + o
                    if gidx >= 2:
                        te.wait_ge(s_ev, gidx - 1)
                    if gidx == 0:
                        te.wait_ge(s_ao, 16 * NG * 8)
                    for dc in range(16):
                        mi = te.matmul(p_av[gidx % 2][:],
                                       vhb[t % 2][:, dc * 128:(dc + 1) * 128],
                                       big[:, WOUT0 + dc * D + o * OS: WOUT0 + dc * D + o * OS + OS],
                                       start=(dc == 0), stop=(dc == 15))
                    mi.then_inc(s_mm, 1)

        @block.scalar
        def _(sc):
            if "loadsonly" in variant:
                sc.wait_ge(s_cl, 32)
                sc.copy(bias_s[:, 0:1], bias_s[:, 0:1])
                return
            nsim = 0; ntrev = 0
            for h in range(16):
                for g, (ext, crossing) in enumerate(groups):
                    for kt in range(ext):
                        sc.wait_ge(s_sim, nsim + 1)
                        if nsim >= 2:
                            sc.wait_ge(s_av, (nsim - 1) * 4)
                        sc.activation(psl[nsim % 2][:], p_s[nsim % 2][:], AF.Exp,
                                      bias=bias_s[:, kt * NG + g: kt * NG + g + 1]).then_inc(s_px, 1)
                        nsim += 1
                for g in range(NG):
                    ig = h * NG + g
                    if ig >= 2:
                        sc.wait_ge(s_aost[ig % 2], 16 * (ig // 2))
                    for j in range(4):
                        sc.wait_ge(s_petr, ntrev + 1)
                        sc.copy(aotb[ig % 2][:, j * 128:(j + 1) * 128],
                                p_tr[ntrev % 2][:]).then_inc(s_trev, 1)
                        ntrev += 1
            for t in range(NTt if "nophasec" not in variant else 0):
                sc.wait_ge(s_vs, 5 * (t + 1))
                sc.activation(stat[:, 2:3], stat[:, 2:3], AF.Ln, bias=epsc[:]).then_inc(s_lnx, 1)
                sc.wait_ge(s_lnx, t + 1)
                sc.activation(stat[:, 3:4], stat[:, 2:3], AF.Exp, scale=-0.5).then_inc(s_rs, 1)

        @block.vector
        def _(v):
            v.wait_ge(s_cl, 32)
            v.wait_ge(s_mk, 16 * NG)
            v.memset(epsc[:], EPS)
            v.memset(big[:, V0:V0 + 2 * NKC * VSL], 1.0).then_inc(s_ini, 1)
            if "loadsonly" in variant:
                return
            nsim = 0; nmsk = 0
            for h in range(16):
                for g, (ext, crossing) in enumerate(groups):
                    for kt in range(ext):
                        if kt in crossing:
                            v.wait_ge(s_px, nsim + 1)
                            si = crossing.index(kt)
                            v.tensor_tensor(psl[nsim % 2][:], psl[nsim % 2][:],
                                            big[:, MK0 + (g * NCROSS + si) * 512:
                                                MK0 + (g * NCROSS + si) * 512 + 512],
                                            ALU.mult).then_inc(s_msk, 1)
                            nmsk += 1
                        nsim += 1
                    for j in range(4):
                        v.wait_ge(s_av, (sum(groups[gg][0] for gg in range(g + 1)) + h * NSIM) * 4)
                        if (h * NG + g) * 8 + 2 * j >= 2:
                            v.wait_ge(s_ao, (h * NG + g) * 8 + 2 * j)
                        v.reciprocal(linv[:], p_av[j][:, 128:129]).then_inc(s_ao, 1)
                        v.wait_ge(s_ao, (h * NG + g) * 8 + 2 * j + 1)
                        v.wait_ge(s_petr, max(0, (h * NG + g) * 4 + j - 1))
                        v.tensor_scalar(ao[j % 2][:], p_av[j][:, 0:128], linv[:], None,
                                        ALU.mult).then_inc(s_ao, 1)
            if "nophasec" in variant:
                return
            for t in range(NTt):
                if t >= 1:
                    v.wait_ge(s_rs, t)
                    if t - 1 >= 2:
                        v.wait_ge(s_st[(t - 1) % 2], 16 * ((t - 1) // 2))
                    v.tensor_scalar(yo[(t - 1) % 2][:], ssl[(t - 1) % 2][:],
                                    stat[:, 0:1], stat[:, 3:4],
                                    ALU.subtract, ALU.mult).then_inc(s_no, 1)
                if t >= 2:
                    v.wait_ge(s_no, t - 1)
                if t >= 1:
                    v.wait_ge(s_vs, 5 * t)
                for o in range(NOS):
                    gidx = t * NOS + o
                    v.wait_ge(s_mm, gidx + 1)
                    v.wait_ge(s_x[t % 2], 16 * (t // 2 + 1))
                    v.scalar_tensor_tensor(
                        ssl[t % 2][:, o * OS:(o + 1) * OS], p_av[gidx % 2][:], 1.0,
                        xb[:, t * D + o * OS: t * D + (o + 1) * OS],
                        ALU.mult, ALU.add, accum_out=acc8[:, o:o + 1]).then_inc(s_ev, 1)
                v.wait_ge(s_ev, 4 * (t + 1))
                if t >= 1:
                    v.wait_ge(s_vq, 4 * t)
                for o in range(NOS):
                    v.scalar_tensor_tensor(
                        sqs[:, o * OS:(o + 1) * OS], ssl[t % 2][:, o * OS:(o + 1) * OS], 1.0,
                        ssl[t % 2][:, o * OS:(o + 1) * OS],
                        ALU.mult, ALU.mult, accum_out=acc8[:, 4 + o:5 + o]).then_inc(s_vq, 1)
                v.tensor_reduce(stat[:, 0:1], acc8[:, 0:4], AX.X, ALU.add).then_inc(s_vs, 1)
                v.wait_ge(s_vq, 4 * (t + 1))
                v.tensor_reduce(stat[:, 2:3], acc8[:, 4:8], AX.X, ALU.add).then_inc(s_vs, 1)
                v.wait_ge(s_vs, 5 * t + 1)
                v.tensor_scalar(stat[:, 0:1], stat[:, 0:1], 1.0 / D, None, ALU.mult).then_inc(s_vs, 1)
                v.wait_ge(s_vs, 5 * t + 3)
                v.tensor_tensor(stat[:, 1:2], stat[:, 0:1], stat[:, 0:1], ALU.mult).then_inc(s_vs, 1)
                v.wait_ge(s_vs, 5 * t + 4)
                v.scalar_tensor_tensor(stat[:, 2:3], stat[:, 2:3], 1.0 / D, stat[:, 1:2],
                                       ALU.mult, ALU.subtract).then_inc(s_vs, 1)
            t = NTt - 1
            v.wait_ge(s_rs, NTt)
            v.wait_ge(s_st[t % 2], 16 * (t // 2))
            v.tensor_scalar(yo[t % 2][:], ssl[t % 2][:], stat[:, 0:1], stat[:, 3:4],
                            ALU.subtract, ALU.mult).then_inc(s_no, 1)
    return nc

def build_l1batt(TT=T, variant=''):
    NTt = TT // 128
    SF = 2 * TT
    NKC = SF // 128
    NG = NTt // 4
    groups = l1b_sched(TT)
    NCROSS = len(groups[0][1])
    OS = 512
    NOS = D // OS
    EXPB = -20.0

    nc = bass.Bass("TRN2", target_bir_lowering=False)
    q = nc.dram_tensor("q", [TT, D], BF16, kind="ExternalInput").ap()
    kfull = nc.dram_tensor("kfull", [SF, D], BF16, kind="ExternalInput").ap()
    vfull = nc.dram_tensor("vfull", [SF, D], BF16, kind="ExternalInput").ap()
    masks = nc.dram_tensor("masks", [NG, NCROSS, 128, 512], BF16, kind="ExternalInput").ap()
    biastab = nc.dram_tensor("biastab", [128, NKC * NG], F32, kind="ExternalInput").ap()
    idn = nc.dram_tensor("idn", [128, 128], BF16, kind="ExternalInput").ap()
    aot = nc.dram_tensor("aot", [16, NTt, 128, 128], BF16, kind="ExternalOutput").ap()

    VSL = 132

    es = ExitStack()
    sb = lambda n, s, d: es.enter_context(nc.sbuf_tensor(n, s, d))
    ps = lambda n, s, d: es.enter_context(nc.psum_tensor(n, s, d))
    sem = lambda n: es.enter_context(nc.semaphore(n))
    with es:
        ktb = [sb(f"ktb{i}", [128, SF], BF16) for i in range(2)]
        qtb = [sb(f"qtb{i}", [128, TT], BF16) for i in range(2)]
        vsb = [sb(f"vsb{i}", [128, NKC * VSL], BF16) for i in range(2)]
        mkb = sb("mkb", [128, NG * NCROSS * 512], BF16)
        bias_s = sb("bias_s", [128, NKC * NG], F32)
        id_s = sb("id_s", [128, 128], BF16)
        pslg = sb("pslg", [128, 32 * 512], BF16)
        linv = sb("linv", [128, 1], F32)
        ao = [sb(f"ao{i}", [128, 128], BF16) for i in range(2)]
        aotb = [sb(f"aotb{i}", [128, 512], BF16) for i in range(2)]
        p_s = [ps(f"p_s{i}", [128, 512], F32) for i in range(2)]
        p_av = [ps(f"p_av{i}", [128, 512], F32) for i in range(4)]
        p_tr = [ps(f"p_tr{i}", [128, 128], BF16) for i in range(2)]
        s_cl = sem("s_cl")
        s_mk = sem("s_mk")
        s_ini = sem("s_ini")
        s_kq = [sem(f"s_kq{i}") for i in range(2)]
        s_vl = [sem(f"s_vl{i}") for i in range(2)]
        s_x = [sem(f"s_x{i}") for i in range(2)]
        s_sim = sem("s_sim")
        s_px = sem("s_px")
        s_msk = sem("s_msk")
        s_av = sem("s_av")
        s_ao = sem("s_ao")
        s_petr = sem("s_petr")
        s_trev = sem("s_trev")
        s_aost = [sem(f"s_aost{i}") for i in range(2)]
        s_wl = sem("s_wl")
        s_vhl = [sem(f"s_vhl{i}") for i in range(2)]
        s_mm = sem("s_mm")
        s_ev = sem("s_ev")
        s_vq = sem("s_vq")
        s_vs = sem("s_vs")
        s_lnx = sem("s_lnx")
        s_rs = sem("s_rs")
        s_no = sem("s_no")
        s_st = [sem(f"s_st{i}") for i in range(2)]
        block = es.enter_context(nc.Block())

        def head_sched():
            out = []
            for g, (ext, crossing) in enumerate(groups):
                for kt in range(ext):
                    out.append((g, kt, kt in crossing, crossing.index(kt) if kt in crossing else -1))
            return out
        SCH = head_sched()
        NSIM = len(SCH)
        NMSK_H = sum(1 for (_g, _kt, c, _i) in SCH if c)

        @block.sync
        def _(sync):
            sync.dma_start(out=id_s[:], in_=idn).then_inc(s_cl, 16)
            sync.wait_ge(s_cl, 16)
            sync.dma_start(out=bias_s[:], in_=biastab).then_inc(s_cl, 16)
            sync.wait_ge(s_cl, 32)
            for h in range(16):
                if h >= 2 and "loadsonly" not in variant:
                    sync.wait_ge(s_av, (h - 1) * 4 * NG)
                sync.wait_ge(s_kq[h % 2], 32 * (h // 2))
                if "notrdma" in variant:
                    sync.dma_start(out=ktb[h % 2][:, 0:2048],
                                   in_=kfull[0:128, :]).then_inc(s_kq[h % 2], 16)
                else:
                    sync.dma_start(out=ktb[h % 2][:],
                                   in_=kfull[:, h * 128:(h + 1) * 128], transpose=True).then_inc(s_kq[h % 2], 16)
                sync.wait_ge(s_kq[h % 2], 32 * (h // 2) + 16)
                if "notrdma" in variant:
                    sync.dma_start(out=qtb[h % 2][:],
                                   in_=q[0:128, 0:TT]).then_inc(s_kq[h % 2], 16)
                else:
                    sync.dma_start(out=qtb[h % 2][:],
                                   in_=q[:, h * 128:(h + 1) * 128], transpose=True).then_inc(s_kq[h % 2], 16)
                if h >= 1 and "loadsonly" not in variant:
                    hh = h - 1
                    for g in range(NG):
                        ig = hh * NG + g
                        sync.wait_ge(s_trev, hh * 4 * NG + (g + 1) * 4)
                        sync.wait_ge(s_aost[ig % 2], 16 * (ig // 2))
                        sync.dma_start(out=aot[hh, g * 4:(g + 1) * 4].rearrange("tc p c -> p tc c"),
                                       in_=aotb[ig % 2][:].rearrange("p (tc c) -> p tc c", c=128)
                                       ).then_inc(s_aost[ig % 2], 16)
            hh = 15
            for g in range(NG if "loadsonly" not in variant else 0):
                ig = hh * NG + g
                sync.wait_ge(s_trev, hh * 4 * NG + (g + 1) * 4)
                sync.wait_ge(s_aost[ig % 2], 16 * (ig // 2))
                sync.dma_start(out=aot[hh, g * 4:(g + 1) * 4].rearrange("tc p c -> p tc c"),
                               in_=aotb[ig % 2][:].rearrange("p (tc c) -> p tc c", c=128)
                               ).then_inc(s_aost[ig % 2], 16)
            sync.wait_ge(s_aost[0], 16 * ((16 * NG + 1) // 2))
            sync.wait_ge(s_aost[1], 16 * (16 * NG // 2))

        @block.gpsimd
        def _(g_):
            for gg in range(NG):
                g_.wait_ge(s_mk, 16 * gg)
                g_.dma_start(out=mkb[:, gg * NCROSS * 512:(gg + 1) * NCROSS * 512].rearrange("p (a c) -> p a c", c=512),
                             in_=masks[gg].rearrange("s p c -> p s c")).then_inc(s_mk, 16)
            g_.wait_ge(s_ini, 1)
            for h in range(16 if "novload" not in variant else 0):
                if h >= 2 and "loadsonly" not in variant:
                    g_.wait_ge(s_av, (h - 1) * 4 * NG)
                for part in range(4):
                    kc0 = part * (NKC // 4)
                    g_.wait_ge(s_vl[h % 2], 16 * (4 * (h // 2) + part))
                    g_.dma_start(
                        out=vsb[h % 2][:, kc0 * VSL:(kc0 + NKC // 4) * VSL]
                            .rearrange("p (kc c) -> p kc c", c=VSL)[:, :, 0:128],
                        in_=vfull[kc0 * 128:(kc0 + NKC // 4) * 128, h * 128:(h + 1) * 128]
                            .rearrange("(kc p) c -> p kc c", p=128)).then_inc(s_vl[h % 2], 16)
        @block.tensor
        def _(te):
            te.wait_ge(s_cl, 32)
            if "loadsonly" in variant:
                te.matmul(p_s[0][:, 0:128], id_s[:], id_s[:], start=True, stop=True)
                return
            nsim = 0; nav = 0; nmsk = 0; npx = 0; nao = 0; ntr = 0
            for h in range(16):
                te.wait_ge(s_kq[h % 2], 32 * (h // 2) + 32)
                te.wait_ge(s_vl[h % 2], 16 * (4 * (h // 2) + 4))
                for g, (ext, crossing) in enumerate(groups):
                    for kt in range(ext):
                        if nsim >= 2:
                            te.wait_ge(s_px, nsim - 1)
                        te.matmul(p_s[nsim % 2][:], ktb[h % 2][:, kt * 128:(kt + 1) * 128],
                                  qtb[h % 2][:, g * 512:(g + 1) * 512],
                                  start=True, stop=True).then_inc(s_sim, 1)
                        nsim += 1
                        if kt in crossing:
                            nmsk += 1
                    te.wait_ge(s_px, nsim)
                    te.wait_ge(s_msk, nmsk)
                    te.wait_ge(s_ao, (h * NG + g) * 8)
                    for jj in range(4):
                        for kt in range(ext):
                            mi = te.matmul(p_av[jj][:, 0:132],
                                           pslg[:, kt * 512 + jj * 128: kt * 512 + jj * 128 + 128],
                                           vsb[h % 2][:, kt * VSL: kt * VSL + 132],
                                           start=(kt == 0), stop=(kt == ext - 1))
                        mi.then_inc(s_av, 1)
                        nav += 1
                    for jj in range(4):
                        te.wait_ge(s_ao, (h * NG + g) * 8 + 2 * (jj + 1))
                        if ntr >= 2:
                            te.wait_ge(s_trev, ntr - 1)
                        te.transpose(p_tr[ntr % 2][:], ao[jj % 2][:], id_s[:]).then_inc(s_petr, 1)
                        ntr += 1
        @block.scalar
        def _(sc):
            if "loadsonly" in variant:
                sc.wait_ge(s_cl, 32)
                sc.copy(bias_s[:, 0:1], bias_s[:, 0:1])
                return
            nsim = 0; ntrev = 0; nav = 0
            for h in range(16):
                for g, (ext, crossing) in enumerate(groups):
                    sc.wait_ge(s_av, 4 * (h * NG + g))
                    for kt in range(ext):
                        sc.wait_ge(s_sim, nsim + 1)
                        sc.activation(pslg[:, kt * 512:(kt + 1) * 512], p_s[nsim % 2][:], AF.Exp,
                                      bias=bias_s[:, kt * NG + g: kt * NG + g + 1],
                                      scale=float(HD ** -0.5)).then_inc(s_px, 1)
                        nsim += 1
                    ig = h * NG + g
                    if ig >= 2:
                        sc.wait_ge(s_aost[ig % 2], 16 * (ig // 2))
                    for jj in range(4):
                        sc.wait_ge(s_petr, ntrev + 1)
                        sc.copy(aotb[ig % 2][:, jj * 128:(jj + 1) * 128],
                                p_tr[ntrev % 2][:]).then_inc(s_trev, 1)
                        ntrev += 1
        @block.vector
        def _(v):
            v.wait_ge(s_cl, 32)
            v.wait_ge(s_mk, 16 * NG)
            v.memset(vsb[0][:], 1.0)
            v.memset(vsb[1][:], 1.0).then_inc(s_ini, 1)
            if "loadsonly" in variant:
                return
            nsim = 0
            for h in range(16):
                for g, (ext, crossing) in enumerate(groups):
                    for kt in range(ext):
                        if kt in crossing:
                            v.wait_ge(s_px, nsim + 1)
                            si = crossing.index(kt)
                            v.tensor_tensor(pslg[:, kt * 512:(kt + 1) * 512],
                                            pslg[:, kt * 512:(kt + 1) * 512],
                                            mkb[:, (g * NCROSS + si) * 512:(g * NCROSS + si + 1) * 512],
                                            ALU.mult).then_inc(s_msk, 1)
                        nsim += 1
                    v.wait_ge(s_av, 4 * (h * NG + g) + 4)
                    for jj in range(4):
                        base = (h * NG + g) * 8 + 2 * jj
                        if base >= 2:
                            v.wait_ge(s_ao, base)
                        v.reciprocal(linv[:], p_av[jj][:, 128:129]).then_inc(s_ao, 1)
                        v.wait_ge(s_ao, base + 1)
                        v.wait_ge(s_petr, max(0, (h * NG + g) * 4 + jj - 1))
                        v.tensor_scalar(ao[jj % 2][:], p_av[jj][:, 0:128], linv[:], None,
                                        ALU.mult).then_inc(s_ao, 1)
    return nc


def _host_consts():
    j = np.arange(128)
    ang = 2 * np.pi * np.outer(j, j) / 128
    C = np.cos(ang).astype(np.float32)
    Sm = np.sin(ang).astype(np.float32)
    return {
        "csb": np.ascontiguousarray(np.concatenate([C, -Sm], axis=1)).astype(NPBF),
        "ub": np.triu(np.ones((128, 128), np.float32)).astype(NPBF),
        "c2b": (C / 128.0).astype(NPBF),
        "s2nb": (-Sm / 128.0).astype(NPBF),
        "onescol": np.ones((128, 1), np.float32).astype(NPBF),
        "onesr": ONESROW_BF,
        "idn": IDN_BF,
        "iota8": np.tile(np.arange(8, dtype=np.float32), (128, 1)),
        "iotabig": np.tile(np.arange(8, dtype=np.float32) + 1000.0, (128, 1)),
        "uf32": np.triu(np.ones((128, 128), np.float32)),
        "onescf": np.ones((128, 1), np.float32),
        "onesrf": np.ones((1, 128), np.float32),
    }

def _core_flags(c):
    p = c % 2
    flags = np.array([[1.0 if p == 1 else 0.0, 1.0 if p == 0 else 0.0]], np.float32)
    pcol = np.zeros((NCORE, 1), np.float32)
    pcol[c ^ 1, 0] = 1.0
    return flags, pcol

def _rot_tables(c):
    ROT = 32
    pos = global_positions(c).astype(np.float32)
    inv = 1.0 / 10000.0 ** (np.arange(0, ROT, 2, dtype=np.float32) / ROT)
    fr = pos[:, None] * inv[None, :]
    posf = np.concatenate([fr, fr], -1)
    return (np.ascontiguousarray(np.tile(np.cos(posf), (1, 16)).astype(np.float32)),
            np.ascontiguousarray(np.tile(np.sin(posf), (1, 16)).astype(np.float32)))

def _l1b_tables(c):
    NTt = T // 128
    SF = 2 * T
    NKC = SF // 128
    NG = NTt // 4
    groups = l1b_sched(T)
    NCROSS = len(groups[0][1])
    H = NTt // 2
    p = c % 2
    gq = [(j if j < H else j + NTt) if p == 0 else j + H for j in range(NTt)]
    qpos = np.concatenate([np.arange(g * 128, (g + 1) * 128) for g in gq])
    EXPB = -20.0
    biastab = np.zeros((128, NKC * NG), np.float32)
    for g in range(NG):
        etrue = max(gq[4 * g + jj] for jj in range(4)) + 1
        for kt in range(NKC):
            biastab[:, kt * NG + g] = EXPB + (-30000.0 if kt >= etrue else 0.0)
    masks = np.ones((NG, NCROSS, 128, 512), np.float32)
    for g in range(NG):
        ext, crossing = groups[g]
        for si, kt in enumerate(crossing):
            kpos = kt * 128 + np.arange(128)
            qp = qpos[g * 512:(g + 1) * 512]
            masks[g, si] = (kpos[:, None] <= qp[None, :]).astype(np.float32)
    return masks.astype(NPBF), biastab

_NC_CACHE = {}

def _get_nc(name, builder):
    if name not in _NC_CACHE:
        _NC_CACHE[name] = builder()
    return _NC_CACHE[name]

def kernel(L0_attn_x, L0_attn_W_q, L0_attn_W_kv, L0_attn_W_o,
           L0_ffn_gate_weight, L0_ffn_w1, L0_ffn_w2,
           L1_attn_w_qkv, L1_attn_w_out, L1_attn_g,
           L1_ffn_factors, L1_ffn_bias):
    cst = _host_consts()
    f32 = lambda a: np.ascontiguousarray(np.asarray(a, np.float32))
    x0s = shard_rows(f32(L0_attn_x))

    wq_t = np.ascontiguousarray(f32(L0_attn_W_q).T.reshape(16, 128, 16, 128).transpose(2, 0, 1, 3))
    wkv_t = np.ascontiguousarray(f32(L0_attn_W_kv).T.reshape(16, 128, 16, 128).transpose(2, 0, 1, 3))
    wo_t = np.ascontiguousarray(f32(L0_attn_W_o).T.reshape(16, 128, D))
    gt_t = np.ascontiguousarray(f32(L0_ffn_gate_weight).T.reshape(16, 128, 8))
    w1_t = np.ascontiguousarray(f32(L0_ffn_w1).reshape(8, 16, 128, 16, 128).transpose(0, 3, 1, 2, 4))
    w2_t = np.ascontiguousarray(f32(L0_ffn_w2).reshape(8, 16, 128, 4, 512).transpose(0, 3, 1, 2, 4))
    wqkv_t = np.ascontiguousarray(f32(L1_attn_w_qkv).T.reshape(16, 128, 12, 512).transpose(2, 0, 1, 3))
    wout_t = np.ascontiguousarray(f32(L1_attn_w_out).T.reshape(16, 128, D))
    fac = f32(L1_ffn_factors)
    bias1 = f32(L1_ffn_bias).reshape(1, D)

    nc = _get_nc("l0a", build_l0a)
    res = _run(nc, [{"x": x0s[c], "wq": wq_t, "wkv": wkv_t, "idn": cst["idn"]}
                    for c in range(NCORE)])
    qchs = [r["qch"] for r in res.results]
    kvchs = [r["kvch"] for r in res.results]

    nc = _get_nc("l0b", build_l0b)
    in_maps = []
    for c in range(NCORE):
        flags, pcol = _core_flags(c)
        in_maps.append({"qch": qchs[c], "kvch": kvchs[c], "csb": cst["csb"], "ub": cst["ub"],
                        "c2b": cst["c2b"], "s2nb": cst["s2nb"], "onescol": cst["onescol"],
                        "onesr": cst["onesr"], "idn": cst["idn"], "flags": flags, "pcolf": pcol})
    res = _run(nc, in_maps)
    vhts = [r["vht"] for r in res.results]

    nc = _get_nc("l0c", build_l0c)
    res = _run(nc, [{"x": x0s[c], "vht": vhts[c], "wo": wo_t} for c in range(NCORE)])
    x1s = [r["y"] for r in res.results]

    nc = _get_nc("moe", build_moe)
    in_maps = []
    for c in range(NCORE):
        flags, pcol = _core_flags(c)
        in_maps.append({"x": x1s[c], "gt": gt_t, "w1t": w1_t, "w2t": w2_t,
                        "iota8": cst["iota8"], "iotabig": cst["iotabig"], "uf32": cst["uf32"],
                        "onescf": cst["onescf"], "onesrf": cst["onesrf"], "idn": cst["idn"],
                        "flags": flags, "pcolf": pcol})
    res = _run(nc, in_maps)
    x2s = [r["y"] for r in res.results]

    nc = _get_nc("l1a", build_l1a)
    in_maps = []
    for c in range(NCORE):
        cosr, sinr = _rot_tables(c)
        in_maps.append({"x": x2s[c], "wqkv": wqkv_t, "cosr": cosr, "sinr": sinr, "idn": cst["idn"]})
    res = _run(nc, in_maps)
    qkvts = [np.asarray(r["qkvt"]) for r in res.results]

    NTt = T // 128
    def split_qkv(a):
        a = a.reshape(12, NTt, 128, 512)
        full = a.transpose(1, 2, 0, 3).reshape(T, 12 * 512)
        return (np.ascontiguousarray(full[:, 0:2048]),
                np.ascontiguousarray(full[:, 2048:4096]),
                np.ascontiguousarray(full[:, 4096:6144]))
    qs, ks, vs = zip(*[split_qkv(a) for a in qkvts])
    kfulls, vfulls = [], []
    for b in range(B):
        cA, cB = 2 * b, 2 * b + 1
        kf = np.concatenate([ks[cA][0:1024], ks[cB], ks[cA][1024:2048]], axis=0)
        vf = np.concatenate([vs[cA][0:1024], vs[cB], vs[cA][1024:2048]], axis=0)
        kfulls.append(np.ascontiguousarray(kf))
        vfulls.append(np.ascontiguousarray(vf))

    nc = _get_nc("l1batt", build_l1batt)
    in_maps = []
    for c in range(NCORE):
        masks, biastab = _l1b_tables(c)
        in_maps.append({"q": qs[c], "kfull": kfulls[c // 2], "vfull": vfulls[c // 2],
                        "masks": masks, "biastab": biastab, "idn": cst["idn"]})
    res = _run(nc, in_maps)
    aots = [r["aot"] for r in res.results]
    nc = _get_nc("l0c", build_l0c)
    res = _run(nc, [{"x": x2s[c], "vht": aots[c], "wo": wout_t} for c in range(NCORE)])
    x3s = [r["y"] for r in res.results]

    nc = _get_nc("l1c", build_l1c)
    res = _run(nc, [{"x": x3s[c], "fac": fac, "bias": bias1, "idn": cst["idn"],
                     "onesr": cst["onesr"]} for c in range(NCORE)])
    ys = [np.asarray(r["y"], np.float32) for r in res.results]
    return unshard_rows(ys, dtype=np.float32)

